# version 07
# baseline (speedup 1.0000x reference)
import contextlib
import numpy as np
import ml_dtypes
import concourse.bass as bass
import concourse.mybir as mybir
from concourse.bass_utils import run_bass_kernel_spmd

F32 = mybir.dt.float32
BF16 = mybir.dt.bfloat16
F16 = mybir.dt.float16
AF = mybir.ActivationFunctionType
ALU = mybir.AluOpType
BF = ml_dtypes.bfloat16

S, DM, HID, NST, HEADS, HP = 2048, 2048, 4096, 128, 64, 64
MS = 512
NMS = S // MS
FS = 1024
NFS = S // FS
CH = 128
NCH = MS // CH
NCC = 66
CLAMP = 30.0
EPS = 1e-5

ENGINES = ("pe", "dve", "act", "pool", "sp")
ENG_ATTR = {"pe": "tensor", "dve": "vector", "act": "scalar",
            "pool": "gpsimd", "sp": "sync"}


class Prog:

    def __init__(self, nc):
        self.nc = nc
        self.ops = []
        self.stream = {e: [] for e in ENGINES}
        self.last_writer = {}
        self.readers = {}
        self.epoch = 0

    def emit(self, eng, fn, reads=(), writes=(), tag=""):
        idx = len(self.ops)
        deps = set()
        for r in reads:
            w = self.last_writer.get(r)
            if w is not None:
                deps.add(w)
        for w in writes:
            lw = self.last_writer.get(w)
            if lw is not None:
                deps.add(lw)
            deps.update(self.readers.get(w, ()))
        deps.discard(idx)
        self.ops.append(dict(idx=idx, eng=eng, fn=fn, deps=sorted(deps),
                             epoch=self.epoch, need_inc=False, dma=None, tag=tag))
        self.stream[eng].append(idx)
        for r in reads:
            self.readers.setdefault(r, []).append(idx)
        for w in writes:
            self.last_writer[w] = idx
            self.readers[w] = []
        return idx

    def dma(self, eng, slot, fn, reads=(), writes=(), tag=""):
        idx = self.emit(eng, fn, reads=reads, writes=writes, tag=tag)
        self.ops[idx]["dma"] = slot
        return idx

    def barrier(self):
        self.epoch += 1

    def finalize(self, stack):
        nc = self.nc
        eng_sem = {e: stack.enter_context(nc.semaphore(f"es_{e}")) for e in ENGINES}
        slots = sorted({op["dma"] for op in self.ops if op["dma"] is not None})
        dsem = {s: [stack.enter_context(nc.semaphore(f"ds_{s}")), 0] for s in slots}
        for op in self.ops:
            for d in op["deps"]:
                dop = self.ops[d]
                if dop["dma"] is None and not (dop["eng"] == "pe" and op["eng"] == "pe"):
                    dop["need_inc"] = True
        cnt = {e: 0 for e in ENGINES}
        for e in ENGINES:
            for i in self.stream[e]:
                op = self.ops[i]
                if op["dma"] is not None:
                    dsem[op["dma"]][1] += 16
                    op["sem"], op["semval"], op["incby"] = dsem[op["dma"]][0], dsem[op["dma"]][1], 16
                elif op["need_inc"]:
                    cnt[e] += 1
                    op["sem"], op["semval"], op["incby"] = eng_sem[e], cnt[e], 1
                else:
                    op["sem"] = None
        for e in ENGINES:
            cov = {}
            for i in self.stream[e]:
                op = self.ops[i]
                wmax = {}
                for d in op["deps"]:
                    dop = self.ops[d]
                    if dop["dma"] is None:
                        if dop["eng"] == "pe" and e == "pe":
                            continue
                        if dop["epoch"] < op["epoch"]:
                            continue
                    sem, val = dop["sem"], dop["semval"]
                    k = id(sem)
                    if cov.get(k, 0) >= val:
                        continue
                    if k not in wmax or wmax[k][1] < val:
                        wmax[k] = (sem, val)
                for k, (sem, val) in wmax.items():
                    cov[k] = max(cov.get(k, 0), val)
                op["waits"] = list(wmax.values())
        by_ep = {}
        for e in ENGINES:
            for i in self.stream[e]:
                by_ep.setdefault((self.ops[i]["epoch"], e), []).append(i)
        for ep in range(self.epoch + 1):
            engs = [e for e in ENGINES if (ep, e) in by_ep]
            if not engs:
                continue
            with nc.Block() as blk:
                for e in engs:
                    idxs = by_ep[(ep, e)]

                    def body(h, idxs=idxs):
                        for i in idxs:
                            op = self.ops[i]
                            for sem, val in op["waits"]:
                                h.wait_ge(sem, val)
                            inst = op["fn"](h)
                            if op["sem"] is not None:
                                inst.then_inc(op["sem"], op["incby"])
                    getattr(blk, ENG_ATTR[e])(body)
        return dsem


def ap3(t, base, s1, n1, s2, n2, p0=0, pn=128):
    rl = t.shape[1]
    return bass.AP(t, p0 * rl + base, [[rl, pn], [s1, n1], [s2, n2]])


def ap2(t, base, s1, n1, p0, pn):
    rl = t.shape[1]
    return bass.AP(t, p0 * rl + base, [[rl, pn], [s1, n1]])


def build(nc, dumps=(), stage=9):
    pr = Prog(nc)
    dumps = set(dumps)
    dump_t = {}
    R = lambda *a: a

    def dram(name, shape, dtype, kind):
        return nc.dram_tensor(name, shape, dtype, kind=kind).ap()

    x_h = dram("x", [S, DM], F32, "ExternalInput")
    out_h = dram("out", [S, DM], F32, "ExternalOutput")
    wlay_h = dram("wlay", [128, NCC * 2048 + 1024], F32, "ExternalInput")
    wout_h = dram("wout", [128, 8 * 8192], F32, "ExternalInput")
    w1_h = dram("w1", [128, 32 * 2048], F32, "ExternalInput")
    w2_h = dram("w2", [128, 32 * 4096], F32, "ExternalInput")
    w3_h = dram("w3", [128, 8 * 8192], F32, "ExternalInput")
    cwt_h = dram("cwt", [128, 136], F32, "ExternalInput")
    b1_h = dram("b1t", [128, 32], F32, "ExternalInput")
    b2_h = dram("b2t", [128, 32], F32, "ExternalInput")
    b3_h = dram("b3r", [1, DM], F32, "ExternalInput")
    nw_h = dram("nwt", [128, 32], F32, "ExternalInput")
    dcol_h = dram("dcol", [128, 32], F32, "ExternalInput")
    dtb_h = dram("dtb", [64, 1], F32, "ExternalInput")
    alog_h = dram("alog", [64, 1], F32, "ExternalInput")
    idbf_h = dram("idbf", [128, 128], BF16, "ExternalInput")
    idf_h = dram("idf", [128, 128], F32, "ExternalInput")
    triu_h = dram("triu", [128, 128], BF16, "ExternalInput")
    rep_h = dram("rep64", [64, 4096], BF16, "ExternalInput")
    rsel_h = dram("rsel", [64, 4096], BF16, "ExternalInput")
    onesf_h = dram("onesf", [1, 128], F32, "ExternalInput")
    onescb_h = dram("onescb", [128, 1], BF16, "ExternalInput")
    ones64_h = dram("ones64", [64, 128], BF16, "ExternalInput")

    DUMP_SHAPES = {
        "xbcz": ([128, NCC * MS], BF16), "dt": ([64, MS], F32),
        "dAc": ([64, MS], F32), "gsl": ([128, 32 * MS], BF16),
        "vsl": ([128, NMS * 2048], F16), "x2": ([128, 8 * 2048], BF16),
        "h1T": ([128, 32 * FS], BF16), "fsl": ([128, 8 * 2048], BF16),
        "xhT": ([128, 4096], BF16), "carry": ([128, 4096], BF16),
        "yrw1": ([128, 2048], BF16), "rsb1": ([128, 128], F32),
        "cds0": ([128, 512], BF16), "cmy0": ([128, 512], BF16),
        "cdq0": ([64, 512], BF16), "st0": ([128, 512], BF16),
        "bsb0": ([128, 128], BF16), "xhd0": ([128, 512], BF16), "dtd0": ([128, 512], BF16),
        "ddtbf0": ([64, 512], BF16),
        "carry0": ([128, 4096], BF16), "esb1": ([128, 512], BF16),
    }
    for nm in dumps:
        shp, dt_ = DUMP_SHAPES[nm]
        dump_t[nm] = dram("dump_" + nm, shp, dt_, "ExternalOutput")

    def do_dump(name, src_fn, reads):
        t = dump_t.get(name)
        if t is not None:
            pr.dma("sp", "dmp_" + name, lambda h, t=t: h.dma_start(t, src_fn()),
                   reads=list(reads), tag="dump")

    st = contextlib.ExitStack()
    with st:
        def sb(n, shape, dt_, side="left", stk=st):
            return stk.enter_context(nc.sbuf_tensor("s_" + n, shape, dt_, side=side))

        PY = st.enter_context(nc.psum_tensor("PY", [128, 2048], F32))
        PA = st.enter_context(nc.psum_tensor("PA", [128, 512], F32))
        PB = st.enter_context(nc.psum_tensor("PB", [128, 512], F32))
        PC = st.enter_context(nc.psum_tensor("PC", [128, 512], F32))
        PT = st.enter_context(nc.psum_tensor("PT", [128, 1024], BF16))

        idbf = sb("idbf", [128, 128], BF16)
        idf = sb("idf", [128, 128], F32)
        triu = sb("triu", [128, 128], BF16)
        onesf = sb("onesf", [1, 128], F32)
        onescb = sb("onescb", [128, 1], BF16)
        ones64 = sb("ones64", [64, 128], BF16)
        cwt = sb("cwt", [128, 136], F32)
        b1t = sb("b1t", [128, 32], F32)
        b2t = sb("b2t", [128, 32], F32)
        nwt = sb("nwt", [128, 32], F32)
        dcol = sb("dcol", [128, 32], F32)
        dtb = sb("dtb", [64, 1], F32)
        alog = sb("alog", [64, 1], F32)
        acol = sb("acol", [64, 1], F32)
        carry = sb("carry", [128, 4096], BF16)
        tails = sb("tails", [128, 102], F32)
        lntmp = sb("lntmp", [128, 2048], BF16)
        stat = sb("stat", [128, 16], F32)
        srow = sb("srow", [1, 128], F32)
        x2sl = sb("x2sl", [128, 8 * 2048], BF16, side="right")

        def cload(dst, src, slot):
            pr.dma("sp", slot, lambda h, d=dst, s=src: h.dma_start(d[:, :], s),
                   writes=[R(slot)], tag="cload")
        cload(idbf, idbf_h, "c_idbf"); cload(idf, idf_h, "c_idf")
        cload(triu, triu_h, "c_triu")
        cload(onesf, onesf_h, "c_onesf")
        cload(onescb, onescb_h, "c_onescb"); cload(ones64, ones64_h, "c_ones64")
        cload(cwt, cwt_h, "c_cwt"); cload(b1t, b1_h, "c_b1")
        cload(b2t, b2_h, "c_b2"); cload(nwt, nw_h, "c_nw")
        cload(dcol, dcol_h, "c_dcol"); cload(dtb, dtb_h, "c_dtb")
        cload(alog, alog_h, "c_alog")
        pr.emit("act", lambda h: h.activation(acol[:, :], alog[:, :], AF.Exp),
                reads=[R("c_alog")], writes=[R("acol")])
        pr.emit("dve", lambda h: h.tensor_scalar_mul(acol[:, :], acol[:, :], -1.0),
                reads=[R("acol")], writes=[R("acol")])
        pr.emit("dve", lambda h: h.memset(tails[:, :], 0.0),
                writes=[R("tails", c) for c in range(34)])
        pr.emit("dve", lambda h: h.memset(carry[:, :], 0.0),
                writes=[R("carry", t) for t in range(32)])
        pr.barrier()

        ln_ctr = [0]

        def ln_stats(src_fn, src_keys, scrap_fn, scrap_key):
            k = (ln_ctr[0] % 2) * 5
            ln_ctr[0] += 1
            ssum, ssq, va, rsd, mm_ = k, k + 1, k + 2, k + 3, k + 4
            K = lambda j: R("stat", j)
            pr.emit("dve", lambda h: h.tensor_scalar(
                scrap_fn(), src_fn(), 1.0, 0.0, op0=ALU.mult, op1=ALU.add,
                accum_out=stat[:, ssum:ssum + 1]),
                reads=list(src_keys), writes=[scrap_key, K(ssum)], tag="lns")
            pr.emit("act", lambda h: h.activation(
                scrap_fn(), src_fn(), AF.Square, accum_out=stat[:, ssq:ssq + 1]),
                reads=list(src_keys) + [scrap_key], writes=[scrap_key, K(ssq)], tag="lnq")
            N = 2048.0
            pr.emit("dve", lambda h: h.tensor_scalar(
                stat[:, va:va + 1], stat[:, ssq:ssq + 1], 1.0 / N, EPS,
                op0=ALU.mult, op1=ALU.add), reads=[K(ssq)], writes=[K(va)])
            pr.emit("dve", lambda h: h.tensor_scalar_mul(
                stat[:, mm_:mm_ + 1], stat[:, ssum:ssum + 1], 1.0 / N),
                reads=[K(ssum)], writes=[K(mm_)])
            pr.emit("dve", lambda h: h.tensor_scalar_mul(
                stat[:, ssum:ssum + 1], stat[:, ssum:ssum + 1], -1.0 / N),
                reads=[K(ssum)], writes=[K(ssum)])
            pr.emit("dve", lambda h: h.scalar_tensor_tensor(
                stat[:, va:va + 1], stat[:, mm_:mm_ + 1], stat[:, ssum:ssum + 1],
                stat[:, va:va + 1], op0=ALU.mult, op1=ALU.add),
                reads=[K(mm_), K(ssum), K(va)], writes=[K(va)])
            pr.emit("act", lambda h: h.activation(
                stat[:, rsd:rsd + 1], stat[:, va:va + 1], AF.Ln),
                reads=[K(va)], writes=[K(rsd)])
            pr.emit("act", lambda h: h.activation(
                stat[:, rsd:rsd + 1], stat[:, rsd:rsd + 1], AF.Exp, scale=-0.5),
                reads=[K(rsd)], writes=[K(rsd)])
            pr.emit("dve", lambda h: h.tensor_scalar(
                stat[:, ssum:ssum + 1], stat[:, ssum:ssum + 1],
                stat[:, rsd:rsd + 1], None, op0=ALU.mult),
                reads=[K(ssum), K(rsd)], writes=[K(ssum)])
            return stat[:, rsd:rsd + 1], stat[:, ssum:ssum + 1], [K(rsd), K(ssum)]

        PSL = [("PY", 0), ("PY", 512), ("PY", 1024), ("PY", 1536),
               ("PA", 0), ("PB", 0), ("PC", 0)]

        def psum_of(name):
            return {"PY": PY, "PA": PA, "PB": PB, "PC": PC}[name]

        def pskeys(name, off, width=512):
            if name == "PY":
                return [R("PY", off // 128 + j) for j in range(width // 128)]
            if name == "PA":
                return [R("PAF")]
            return [R(name, off // 128 + j) for j in range(width // 128)]

        def mixer_slab(m, mstage):
            tok0 = m * MS
            ms = contextlib.ExitStack()
            with ms:
                msb = lambda n, s_, d, side="left": sb(f"{n}_{m}", s_, d, side, ms)
                dtF = msb("dtF", [64, MS], F32)
                dAcF = msb("dAcF", [64, MS], F32)
                colBF = msb("colBF", [64, MS], F32)
                ddtF = msb("ddtF", [64, MS], F32)
                zer64 = msb("zer64", [64, 128], F32)
                ddtbf = msb("ddtbf", [64, MS], BF16)
                dachi = msb("dachi", [128, MS], BF16)
                daclo = msb("daclo", [128, MS], BF16)
                dachi2 = msb("dachi2", [64, MS], BF16)
                daclo2 = msb("daclo2", [64, MS], BF16)
                ephbf = msb("ephbf", [64, MS], BF16)
                ephF = msb("ephF", [64, MS], F32)
                gsl = msb("gsl", [128, 32 * MS], BF16, side="right")
                kz = lambda cc: R("xbcz", m, cc)
                xkey, gkeys = R("xst", m), [R("gsl", m, c, H)
                                           for c in range(NCH) for H in range(2)]

                sx = ms.enter_context(contextlib.ExitStack())
                xbcz = sb(f"xbcz_{m}", [128, NCC * MS], BF16, "left", sx)

                with contextlib.ExitStack() as s1:
                    ln1T = sb(f"ln1T_{m}", [128, 16 * MS], BF16, "left", s1)
                    wst = [sb(f"wst{i}_{m}", [128, 2048], BF16, "left", s1) for i in range(2)]
                    rawx = sb(f"rawx_{m}", [128, 1032], F32, "left", s1)
                    cvt = [sb(f"cvt{i}_{m}", [128, 512], F32, "left", s1) for i in range(2)]
                    xst = sb(f"xst_{m}", [128, 2048], F32, "left", s1)
                    lnkeys = []
                    for stt in range(NMS):
                        row0 = tok0 + stt * 128
                        pr.dma("sp", "xld",
                               lambda h, r=row0: h.dma_start(xst[:, :], x_h[r:r + 128, :]),
                               writes=[xkey], tag="xld")
                        rsd, nmr, skeys = ln_stats(lambda: xst[:, :], [xkey],
                                                   lambda: lntmp[:, :], R("lntmp"))
                        pr.emit("dve", lambda h, r=rsd, b=nmr: h.tensor_scalar(
                            lntmp[:, :], xst[:, :], r, b, op0=ALU.mult, op1=ALU.add),
                            reads=[xkey] + skeys, writes=[R("lntmp")], tag="ln1n")
                        for g in range(4):
                            for j in range(4):
                                kt = g * 4 + j
                                pr.emit("pe", lambda h, kt=kt, j=j: h.transpose(
                                    PT[:, j * 128:(j + 1) * 128],
                                    lntmp[:, kt * 128:(kt + 1) * 128], idbf[:, :]),
                                    reads=[R("lntmp"), R("c_idbf")],
                                    writes=[R("PT", j)], tag="ln1t")
                            pr.emit("dve", lambda h, g=g, stt=stt: h.tensor_copy(
                                ap3(ln1T, g * 4 * MS + stt * 128, MS, 4, 1, 128),
                                PT[:, 0:512]),
                                reads=[R("PT", j) for j in range(4)],
                                writes=[R("ln1T", m, g, stt)], tag="ln1e")
                        lnkeys += [R("ln1T", m, g, stt) for g in range(4)]
                    pr.barrier()

                    for cc in range(NCC + 1):
                        slot = cc % 2
                        if cc < NCC:
                            src = wlay_h[:, cc * 2048:(cc + 1) * 2048]
                            dst = wst[slot][:, 0:2048]
                        else:
                            src = wlay_h[:, NCC * 2048:NCC * 2048 + 1024]
                            dst = wst[slot][:, 0:1024]
                        pr.dma("pool", f"wst{slot}",
                               lambda h, d=dst, s=src: h.dma_start(d, s),
                               writes=[R("wst", slot)], tag=f"wld{cc}")
                        pn, off = PSL[cc % 7]
                        pt = psum_of(pn)
                        pkeys = pskeys(pn, off)
                        wid = 64 if cc == NCC else 128
                        for kt in range(16):
                            pr.emit("pe", lambda h, kt=kt, pt=pt, off=off, w=wid, sl=slot:
                                    h.matmul(pt[0:w, off:off + 512],
                                             wst[sl][:, kt * w:(kt + 1) * w],
                                             ln1T[:, kt * MS:(kt + 1) * MS],
                                             start=(kt == 0), stop=(kt == 15)),
                                    reads=[R("wst", slot)] + lnkeys,
                                    writes=pkeys, tag=f"ip{cc}")
                        if cc < 32:
                            pr.emit("act", lambda h, pt=pt, off=off, cc=cc: h.activation(
                                xbcz[:, cc * MS:(cc + 1) * MS], pt[:, off:off + 512],
                                AF.Silu), reads=pkeys, writes=[kz(cc)], tag="zev")
                        elif cc < NCC:
                            ccv = cc - 32
                            par = ccv % 2
                            rx0 = par * 516
                            cv = cvt[par]
                            ck = R("cvt", par)
                            rk0, rk1 = R("rawx", par, 0), R("rawx", par, 1)
                            pr.emit("dve", lambda h, ccv=ccv, rx0=rx0: h.tensor_copy(
                                rawx[:, rx0:rx0 + 3], tails[:, ccv * 3:ccv * 3 + 3]),
                                reads=[R("tails", ccv)], writes=[rk0], tag="cvt0")
                            pr.emit("act", lambda h, pt=pt, off=off, rx0=rx0: h.copy(
                                rawx[:, rx0 + 3:rx0 + 515], pt[:, off:off + 512]),
                                reads=pkeys, writes=[rk1], tag="cvtr")
                            pr.emit("dve", lambda h, ccv=ccv, rx0=rx0: h.tensor_copy(
                                tails[:, ccv * 3:ccv * 3 + 3], rawx[:, rx0 + 512:rx0 + 515]),
                                reads=[rk1], writes=[R("tails", ccv)], tag="cvnt")
                            w = lambda j, ccv=ccv: cwt[:, ccv * 4 + j:ccv * 4 + j + 1]
                            pr.emit("dve", lambda h, w=w, cv=cv, rx0=rx0: h.tensor_scalar(
                                cv[:, :], rawx[:, rx0 + 1:rx0 + 513], w(1), None,
                                op0=ALU.mult),
                                reads=[rk0, rk1, R("c_cwt")], writes=[ck], tag="cva1")
                            pr.emit("dve", lambda h, w=w, cv=cv, rx0=rx0:
                                    h.scalar_tensor_tensor(
                                        cv[:, :], rawx[:, rx0:rx0 + 512], w(0), cv[:, :],
                                        op0=ALU.mult, op1=ALU.add),
                                    reads=[rk0, rk1, R("c_cwt"), ck], writes=[ck], tag="cva0")
                            pr.emit("dve", lambda h, w=w, cv=cv, rx0=rx0:
                                    h.scalar_tensor_tensor(
                                        cv[:, :], rawx[:, rx0 + 3:rx0 + 515], w(3), cv[:, :],
                                        op0=ALU.mult, op1=ALU.add),
                                    reads=[rk1, R("c_cwt"), ck], writes=[ck], tag="cvb3")
                            pr.emit("dve", lambda h, w=w, cv=cv, rx0=rx0:
                                    h.scalar_tensor_tensor(
                                        cv[:, :], rawx[:, rx0 + 2:rx0 + 514], w(2), cv[:, :],
                                        op0=ALU.mult, op1=ALU.add),
                                    reads=[rk0, rk1, R("c_cwt"), ck], writes=[ck], tag="cvb2")
                            pr.emit("act", lambda h, cc=cc, cv=cv: h.activation(
                                xbcz[:, cc * MS:(cc + 1) * MS], cv[:, :], AF.Silu),
                                reads=[ck], writes=[kz(cc)], tag="cvs")
                        else:
                            t0k, t1k = R("dt0"), R("dt1")
                            pr.emit("act", lambda h, pt=pt, off=off: h.activation(
                                cvt[0][0:64, :], pt[0:64, off:off + 512], AF.Abs,
                                bias=dtb[:, :]),
                                reads=pkeys + [R("c_dtb")], writes=[t0k], tag="dtab")
                            pr.emit("act", lambda h: h.activation(
                                cvt[0][0:64, :], cvt[0][0:64, :], AF.Exp, scale=-1.0),
                                reads=[t0k], writes=[t0k], tag="dtex")
                            pr.emit("act", lambda h: h.activation(
                                cvt[0][0:64, :], cvt[0][0:64, :], AF.Ln, bias=1.0),
                                reads=[t0k], writes=[t0k], tag="dtln")
                            pr.emit("act", lambda h, pt=pt, off=off: h.activation(
                                cvt[1][0:64, :], pt[0:64, off:off + 512], AF.Relu,
                                bias=dtb[:, :]),
                                reads=pkeys + [R("c_dtb")], writes=[t1k], tag="dtre")
                            pr.emit("dve", lambda h: h.tensor_add(
                                dtF[:, :], cvt[0][0:64, :], cvt[1][0:64, :]),
                                reads=[t0k, t1k], writes=[R("dt", m)], tag="dtad")
                            pr.emit("act", lambda h: h.activation(
                                colBF[:, :], dtF[:, :], AF.Ln),
                                reads=[R("dt", m)], writes=[R("lndt", m)], tag="dtld")
                    do_dump("xbcz", lambda: xbcz[:, :], [kz(c) for c in range(NCC)])
                    do_dump("dt", lambda: dtF[:, :], [R("dt", m)])
                    pr.barrier()
                if mstage < 2:
                    return

                pr.emit("dve", lambda h: h.tensor_scalar(
                    ddtF[:, :], dtF[:, :], acol[:, :], None, op0=ALU.mult),
                    reads=[R("dt", m), R("acol")], writes=[R("dA", m)], tag="dA")
                pr.emit("dve", lambda h: h.memset(zer64[:, :], 0.0),
                        writes=[R("z64", m)], tag="z64")
                pr.emit("dve", lambda h: h.memset(dachi[64:128, :], 0.0),
                        writes=[R("dahz", m)], tag="hz")
                pr.emit("dve", lambda h: h.memset(daclo[64:128, :], 0.0),
                        writes=[R("dalz", m)], tag="lz")
                dack = [R("dAc", m, c) for c in range(NCH)]
                for c in range(NCH):
                    pr.emit("dve", lambda h, c=c: h.tensor_tensor_scan(
                        dAcF[:, c * CH:(c + 1) * CH],
                        ddtF[:, c * CH:(c + 1) * CH],
                        zer64[:, :], 0.0, op0=ALU.add, op1=ALU.add),
                        reads=[R("dA", m), R("z64", m)], writes=[R("dAc", m, c)], tag="cums")
                pr.emit("dve", lambda h: h.tensor_sub(
                    colBF[:, :], colBF[:, :], dAcF[:, :]),
                    reads=[R("lndt", m)] + dack, writes=[R("colB", m)], tag="colB")
                pr.emit("act", lambda h: h.activation(
                    ephF[:, :], dAcF[:, :], AF.Exp),
                    reads=dack, writes=[R("eph", m)], tag="eph")
                pr.emit("dve", lambda h: h.tensor_copy(ephbf[:, :], ephF[:, :]),
                        reads=[R("eph", m)], writes=[R("ephbf", m)], tag="ephb")
                pr.emit("dve", lambda h: h.tensor_copy(dachi[0:64, :], dAcF[:, :]),
                        reads=dack, writes=[R("dachi", m)], tag="hib")
                pr.emit("dve", lambda h: h.scalar_tensor_tensor(
                    daclo[0:64, :], dachi[0:64, :], -1.0, dAcF[:, :],
                    op0=ALU.mult, op1=ALU.add),
                    reads=[R("dachi", m)] + dack, writes=[R("daclo", m)], tag="lob")
                pr.dma("pool", "dh2", lambda h: h.dma_start(
                    dachi2[0:64, :], dachi[32:96, :]),
                    reads=[R("dachi", m), R("dahz", m)], writes=[R("dachi2", m)], tag="dh2")
                pr.dma("pool", "dl2", lambda h: h.dma_start(
                    daclo2[0:64, :], daclo[32:96, :]),
                    reads=[R("daclo", m), R("dalz", m)], writes=[R("daclo2", m)], tag="dl2")
                for c in range(NCH):
                    last = c * CH + CH - 1
                    pr.emit("dve", lambda h, c=c, last=last: h.tensor_scalar(
                        ddtF[:, c * CH:(c + 1) * CH],
                        dAcF[:, c * CH:(c + 1) * CH],
                        dAcF[:, last:last + 1], -1.0,
                        op0=ALU.subtract, op1=ALU.mult),
                        reads=[R("dAc", m, c)],
                        writes=[R("ddt", m, c), R("dA", m)], tag="dd1")
                    pr.emit("act", lambda h, c=c: h.activation(
                        ddtF[:, c * CH:(c + 1) * CH],
                        ddtF[:, c * CH:(c + 1) * CH], AF.Exp),
                        reads=[R("ddt", m, c)], writes=[R("ddt", m, c)], tag="dd2")
                    pr.emit("dve", lambda h, c=c: h.tensor_mul(
                        ddtF[:, c * CH:(c + 1) * CH],
                        ddtF[:, c * CH:(c + 1) * CH],
                        dtF[:, c * CH:(c + 1) * CH]),
                        reads=[R("ddt", m, c), R("dt", m)],
                        writes=[R("ddt", m, c)], tag="dd3")
                pr.emit("dve", lambda h: h.tensor_copy(ddtbf[:, :], ddtF[:, :]),
                        reads=[R("ddt", m, c) for c in range(NCH)],
                        writes=[R("ddtbf", m)], tag="ddb")
                do_dump("dAc", lambda: dAcF[:, :], dack)
                pr.barrier()

                with contextlib.ExitStack() as s2:
                    rsel = sb(f"rsel_{m}", [64, 4096], BF16, "left", s2)
                    rep64 = sb(f"rep64_{m}", [64, 4096], BF16, "left", s2)
                    xhT = sb(f"xhT_{m}", [128, 4096], BF16, "left", s2)
                    xhd = sb(f"xhd_{m}", [128, 1024], BF16, "left", s2)
                    mtr = sb(f"mtr_{m}", [128, 1024], BF16, "left", s2)
                    gtm = sb(f"gtm_{m}", [128, 128], BF16, "left", s2)
                    bsb = sb(f"bsb_{m}", [128, 128], BF16, "left", s2)
                    cbt = sb(f"cbt_{m}", [128, 64], F32, "left", s2)
                    esb = sb(f"esb_{m}", [128, 512], BF16, "left", s2)
                    yrw = sb(f"yrw_{m}", [128, 2048], BF16, "left", s2)
                    qeb = sb(f"qeb_{m}", [128, 1024], BF16, "left", s2)
                    dxh = sb(f"dxh_{m}", [128, 2048], BF16, "left", s2)
                    cdq = sb(f"cdq_{m}", [64, 512], BF16, "left", s2)
                    cds = sb(f"cds_{m}", [128, 512], BF16, "left", s2)
                    cmy = sb(f"cmy_{m}", [128, 512], BF16, "left", s2)
                    rsb = sb(f"rsb_{m}", [128, 128], F32, "left", s2)
                    pr.dma("sp", "rsel", lambda h: h.dma_start(rsel[:, :], rsel_h),
                           writes=[R("c_rsel")], tag="rsl")
                    pr.dma("sp", "rep", lambda h: h.dma_start(rep64[:, :], rep_h),
                           writes=[R("c_rep")], tag="repl")

                    for c in range(NCH):
                        xhTk = [R("xhT", g) for g in range(8)]
                        for g in range(8):
                            for j in range(4):
                                t = g * 4 + j
                                pr.emit("pe", lambda h, t=t, j=j, c=c: h.transpose(
                                    PT[:, j * 128:(j + 1) * 128],
                                    xbcz[:, (32 + t) * MS + c * CH:(32 + t) * MS + c * CH + 128],
                                    idbf[:, :]),
                                    reads=[kz(32 + t), R("c_idbf")],
                                    writes=[R("PT", j)], tag="xhTt")
                            pr.emit("dve", lambda h, g=g: h.tensor_copy(
                                xhT[:, g * 512:(g + 1) * 512], PT[:, 0:512]),
                                reads=[R("PT", j) for j in range(4)],
                                writes=[R("xhT", g)], tag="xhTe")
                        pr.emit("pe", lambda h, c=c: h.transpose(
                            PT[:, 0:128],
                            xbcz[:, 64 * MS + c * CH:64 * MS + c * CH + 128], idbf[:, :]),
                            reads=[kz(64), R("c_idbf")], writes=[R("PT", 0)], tag="Btt")
                        pr.emit("dve", lambda h: h.tensor_copy(bsb[:, :], PT[:, 0:128]),
                                reads=[R("PT", 0)], writes=[R("bsb")], tag="Bte")
                        pr.emit("pe", lambda h, c=c: h.matmul(
                            PA[:, 0:64], colBF[:, c * CH:(c + 1) * CH],
                            idf[0:64, 0:64], start=True, stop=True),
                            reads=[R("colB", m), R("c_idf")], writes=[R("PAa")], tag="cbtt")
                        pr.emit("dve", lambda h: h.tensor_copy(cbt[:, :], PA[:, 0:64]),
                                reads=[R("PAa")], writes=[R("cbt")], tag="cbte")
                        pr.emit("pe", lambda h, c=c: h.matmul(
                            PB[:, 0:128],
                            xbcz[:, 64 * MS + c * CH:64 * MS + c * CH + 128],
                            xbcz[:, 65 * MS + c * CH:65 * MS + c * CH + 128],
                            start=True, stop=True),
                            reads=[kz(64), kz(65)], writes=[R("PB", 0)], tag="Gt")
                        pr.emit("dve", lambda h: h.tensor_mul(
                            gtm[:, :], PB[:, 0:128], triu[:, :]),
                            reads=[R("PB", 0), R("c_triu")], writes=[R("gtm")], tag="Gtm")

                        for H in range(2):
                            hp0, t0 = H * 2048, H * 16
                            for hh in range(32):
                                h_ = H * 32 + hh
                                pbs, pcs, mts = hh % 4, hh % 4, hh % 8
                                hi_t = dachi if h_ < 32 else dachi2
                                lo_t = daclo if h_ < 32 else daclo2
                                hi_k = R("dachi", m) if h_ < 32 else R("dachi2", m)
                                lo_k = R("daclo", m) if h_ < 32 else R("daclo2", m)
                                pr.emit("pe", lambda h, h_=h_, pbs=pbs, t=hi_t, c=c:
                                        h.matmul(PB[:, pbs * 128:(pbs + 1) * 128],
                                                 rsel[:, (h_ % 32) * 128:(h_ % 32 + 1) * 128],
                                                 ap2(t, c * CH, 1, CH, 0, 64),
                                                 start=True, stop=False),
                                        reads=[hi_k, R("dahz", m), R("c_rsel")],
                                        writes=[R("PB", pbs)], tag="bch")
                                pr.emit("pe", lambda h, h_=h_, pbs=pbs, t=lo_t, c=c:
                                        h.matmul(PB[:, pbs * 128:(pbs + 1) * 128],
                                                 rsel[:, (h_ % 32) * 128:(h_ % 32 + 1) * 128],
                                                 ap2(t, c * CH, 1, CH, 0, 64),
                                                 start=False, stop=True),
                                        reads=[lo_k, R("dalz", m), R("c_rsel")],
                                        writes=[R("PB", pbs)], tag="bcl")
                                pr.emit("dve", lambda h, pbs=pbs, pcs=pcs, h_=h_:
                                        h.tensor_scalar(
                                            PC[:, pcs * 128:(pcs + 1) * 128],
                                            PB[:, pbs * 128:(pbs + 1) * 128],
                                            cbt[:, h_ % 64:h_ % 64 + 1], CLAMP,
                                            op0=ALU.add, op1=ALU.min),
                                        reads=[R("PB", pbs), R("cbt")],
                                        writes=[R("PC", pcs)], tag="seg")
                                pr.emit("act", lambda h, pcs=pcs, mts=mts: h.activation(
                                    mtr[:, mts * 128:(mts + 1) * 128],
                                    PC[:, pcs * 128:(pcs + 1) * 128], AF.Exp),
                                    reads=[R("PC", pcs)], writes=[R("mtr", mts)], tag="exp")
                                pr.emit("dve", lambda h, mts=mts: h.tensor_mul(
                                    mtr[:, mts * 128:(mts + 1) * 128],
                                    mtr[:, mts * 128:(mts + 1) * 128], gtm[:, :]),
                                    reads=[R("mtr", mts), R("gtm")],
                                    writes=[R("mtr", mts)], tag="mtm")
                                pr.emit("pe", lambda h, h_=h_, hh=hh, mts=mts: h.matmul(
                                    PY[(h_ % 2) * 64:(h_ % 2) * 64 + 64,
                                       (hh // 2) * 128:(hh // 2) * 128 + 128],
                                    xhT[:, h_ * 64:(h_ + 1) * 64],
                                    mtr[:, mts * 128:(mts + 1) * 128],
                                    start=True, stop=True,
                                    tile_position=(0, (h_ % 2) * 64)),
                                    reads=[R("mtr", mts)] + xhTk,
                                    writes=[R("PY", hh // 2)], tag="ydg")
                            pr.emit("act", lambda h: h.copy(yrw[:, :], PY[:, 0:2048]),
                                    reads=[R("PY", t) for t in range(16)],
                                    writes=[R("yrw")], tag="yev")
                            if c == 1 and H == 0 and m == 0:
                                do_dump("yrw1", lambda: yrw[:, :], [R("yrw")])
                            for t in range(16):
                                pr.emit("pe", lambda h, t=t, t0=t0, c=c: h.matmul(
                                    PY[:, t * 128:(t + 1) * 128],
                                    carry[:, (t0 + t) * 128:(t0 + t + 1) * 128],
                                    xbcz[:, 65 * MS + c * CH:65 * MS + c * CH + 128],
                                    start=True, stop=True),
                                    reads=[R("carry", t0 + t), kz(65)],
                                    writes=[R("PY", t)], tag="Q")
                            for q in range(4):
                                for j in range(4):
                                    t = q * 4 + j
                                    pr.emit("pe", lambda h, t=t, j=j, t0=t0, c=c: h.matmul(
                                        PC[:, j * 128:(j + 1) * 128],
                                        rep64[:, (t0 + t) * 128:(t0 + t + 1) * 128],
                                        ephbf[:, c * CH:(c + 1) * CH],
                                        start=True, stop=True),
                                        reads=[R("c_rep"), R("ephbf", m)],
                                        writes=[R("PC", j)], tag="E")
                                pr.emit("act", lambda h: h.copy(esb[:, :], PC[:, 0:512]),
                                        reads=[R("PC", j) for j in range(4)],
                                        writes=[R("esb")], tag="Eev")
                                qs = q % 2
                                pr.emit("dve", lambda h, q=q, qs=qs: h.tensor_mul(
                                    qeb[:, qs * 512:(qs + 1) * 512],
                                    PY[:, q * 512:(q + 1) * 512], esb[:, :]),
                                    reads=[R("PY", q * 4 + j) for j in range(4)] + [R("esb")],
                                    writes=[R("qeb", qs)], tag="qe")
                                pr.emit("dve", lambda h, q=q, qs=qs: h.scalar_tensor_tensor(
                                    yrw[:, q * 512:(q + 1) * 512],
                                    qeb[:, qs * 512:(qs + 1) * 512], 1.0,
                                    yrw[:, q * 512:(q + 1) * 512],
                                    op0=ALU.mult, op1=ALU.add),
                                    reads=[R("qeb", qs), R("yrw")], writes=[R("yrw")],
                                    tag="add1")
                            for t in range(16):
                                pr.emit("dve", lambda h, t=t, t0=t0, c=c: h.tensor_scalar(
                                    dxh[:, t * 128:(t + 1) * 128],
                                    xbcz[:, (32 + t0 + t) * MS + c * CH:
                                         (32 + t0 + t) * MS + c * CH + 128],
                                    dcol[:, t0 + t:t0 + t + 1], None, op0=ALU.mult),
                                    reads=[kz(32 + t0 + t), R("c_dcol")],
                                    writes=[R("dxh", t)], tag="dxh")
                            pr.emit("dve", lambda h: h.scalar_tensor_tensor(
                                yrw[:, :], dxh[:, :], 1.0, yrw[:, :],
                                op0=ALU.mult, op1=ALU.add),
                                reads=[R("dxh", t) for t in range(16)] + [R("yrw")],
                                writes=[R("yrw")], tag="add2")
                            zk = [kz(t0 + t) for t in range(16)]
                            pr.emit("dve", lambda h, t0=t0, c=c: h.tensor_mul(
                                ap3(gsl, t0 * MS + c * CH, MS, 16, 1, 128), yrw[:, :],
                                ap3(xbcz, t0 * MS + c * CH, MS, 16, 1, 128)),
                                reads=[R("yrw")] + zk, writes=[R("gsl", m, c, H)],
                                tag="gate")
                            pr.emit("act", lambda h, t0=t0, c=c: h.activation(
                                dxh[:, :],
                                ap3(gsl, t0 * MS + c * CH, MS, 16, 1, 128), AF.Square),
                                reads=[R("gsl", m, c, H)],
                                writes=[R("gsq")] + [R("dxh", t) for t in range(16)],
                                tag="gsq")
                            for t in range(16):
                                pr.emit("pe", lambda h, t=t, H=H: h.matmul(
                                    PA[0:1, 384:512], onescb[:, :],
                                    dxh[:, t * 128:(t + 1) * 128],
                                    start=(H == 0 and t == 0),
                                    stop=(H == 1 and t == 15)),
                                    reads=[R("gsq"), R("dxh", t), R("c_onescb")],
                                    writes=[R("PAs")], tag="ssq")
                            for q in range(4):
                                pr.emit("pe", lambda h, q=q, hp0=hp0, c=c: h.matmul(
                                    PC[:, 0:512],
                                    ap2(ddtbf, c * CH, 1, CH, 0, 64),
                                    rep64[:, hp0 + q * 512:hp0 + (q + 1) * 512],
                                    start=True, stop=True),
                                    reads=[R("ddtbf", m), R("c_rep")],
                                    writes=[R("PC", j) for j in range(4)], tag="dtd")
                                pr.emit("dve", lambda h, q=q, hp0=hp0: h.tensor_mul(
                                    xhd[:, (q % 2) * 512:(q % 2 + 1) * 512],
                                    xhT[:, hp0 + q * 512:hp0 + (q + 1) * 512],
                                    PC[:, 0:512]),
                                    reads=[R("PC", j) for j in range(4)] + xhTk,
                                    writes=[R("xhd", q % 2)], tag="xhd")
                                if c == 0 and H == 0 and q == 0 and m == 0:
                                    pr.emit("dve", lambda h: h.tensor_copy(
                                        cds[:, :], PC[:, 0:512]),
                                        reads=[R("PC", j) for j in range(4)],
                                        writes=[R("cds")], tag="dtddmp")
                                    do_dump("dtd0", lambda: cds[:, :], [R("cds")])
                                    do_dump("bsb0", lambda: bsb[:, :], [R("bsb")])
                                    do_dump("xhd0", lambda: xhd[:, 0:512], [R("xhd", 0)])
                                    do_dump("ddtbf0", lambda: ddtbf[:, :], [R("ddtbf", m)])
                                for j in range(8):
                                    hh = q * 8 + j
                                    pr.emit("pe", lambda h, hh=hh, q=q: h.matmul(
                                        PY[:, hh * 64:(hh + 1) * 64],
                                        bsb[:, :],
                                        xhd[:, (q % 2) * 512 + (hh % 8) * 64:
                                            (q % 2) * 512 + (hh % 8 + 1) * 64],
                                        start=True, stop=True),
                                        reads=[R("bsb"), R("xhd", q % 2)],
                                        writes=[R("PY", hh // 2)], tag="st")
                            for q in range(4):
                                pr.emit("dve", lambda h, q=q, hp0=hp0, c=c: h.tensor_scalar(
                                    cdq[:, :],
                                    rep64[0:64, hp0 + q * 512:hp0 + (q + 1) * 512],
                                    ephF[:, c * CH + 127:c * CH + 128], None,
                                    op0=ALU.mult),
                                    reads=[R("c_rep"), R("eph", m)],
                                    writes=[R("cdq")], tag="cdq")
                                pr.emit("pe", lambda h: h.matmul(
                                    PC[:, 0:512], ones64[:, :], cdq[:, :],
                                    start=True, stop=True),
                                    reads=[R("cdq"), R("c_ones64")],
                                    writes=[R("PC", j) for j in range(4)], tag="cdm")
                                pr.emit("act", lambda h: h.copy(cds[:, :], PC[:, 0:512]),
                                        reads=[R("PC", j) for j in range(4)],
                                        writes=[R("cds")], tag="cde")
                                pr.emit("dve", lambda h, q=q, hp0=hp0: h.tensor_mul(
                                    cmy[:, :],
                                    carry[:, hp0 + q * 512:hp0 + (q + 1) * 512],
                                    cds[:, :]),
                                    reads=[R("cds")] + [R("carry", t0 + q * 4 + j)
                                                        for j in range(4)],
                                    writes=[R("cmy")], tag="cmy")
                                if c == 0 and H == 0 and q == 0 and m == 0:
                                    do_dump("cdq0", lambda: cdq[:, :], [R("cdq")])
                                    do_dump("cds0", lambda: cds[:, :], [R("cds")])
                                    do_dump("cmy0", lambda: cmy[:, :], [R("cmy")])
                                    pr.emit("dve", lambda h: h.tensor_copy(
                                        esb[:, :], PY[:, 0:512]),
                                        reads=[R("PY", j) for j in range(4)],
                                        writes=[R("esb")], tag="stdmp")
                                    do_dump("st0", lambda: esb[:, :], [R("esb")])
                                pr.emit("dve", lambda h, q=q, hp0=hp0: h.tensor_add(
                                    carry[:, hp0 + q * 512:hp0 + (q + 1) * 512],
                                    cmy[:, :], PY[:, q * 512:(q + 1) * 512]),
                                    reads=[R("cmy")] + [R("PY", q * 4 + j)
                                                        for j in range(4)],
                                    writes=[R("carry", t0 + q * 4 + j) for j in range(4)],
                                    tag="cup")
                        pr.emit("dve", lambda h: h.tensor_scalar(
                            srow[0:1, :], PA[0:1, 384:512], 1.0 / HID, EPS,
                            op0=ALU.mult, op1=ALU.add),
                            reads=[R("PAs")], writes=[R("srow")], tag="rs1")
                        pr.emit("act", lambda h: h.activation(
                            srow[0:1, :], srow[0:1, :], AF.Ln),
                            reads=[R("srow")], writes=[R("srow")], tag="rs2")
                        pr.emit("act", lambda h: h.activation(
                            srow[0:1, :], srow[0:1, :], AF.Exp, scale=-0.5),
                            reads=[R("srow")], writes=[R("srow")], tag="rs3")
                        pr.emit("pe", lambda h: h.matmul(
                            PA[:, 256:384], onesf[:, :], srow[0:1, :],
                            start=True, stop=True),
                            reads=[R("srow"), R("c_onesf")], writes=[R("PAr")], tag="rs4")
                        pr.emit("dve", lambda h: h.tensor_copy(rsb[:, :], PA[:, 256:384]),
                                reads=[R("PAr")], writes=[R("rsb")], tag="rs5")
                        for H in range(2):
                            t0 = H * 16
                            pr.emit("dve", lambda h, t0=t0, c=c: h.tensor_mul(
                                ap3(gsl, t0 * MS + c * CH, MS, 16, 1, 128),
                                ap3(gsl, t0 * MS + c * CH, MS, 16, 1, 128),
                                bass.AP(rsb, 0, [[128, 128], [0, 16], [1, 128]])),
                                reads=[R("gsl", m, c, H), R("rsb")],
                                writes=[R("gsl", m, c, H)], tag="gfin")
                        if c == 0 and m == 0:
                            do_dump("xhT", lambda: xhT[:, :], xhTk)
                            do_dump("carry0", lambda: carry[:, :],
                                    [R("carry", t) for t in range(32)])
                        if c == 1 and m == 0:
                            do_dump("rsb1", lambda: rsb[:, :], [R("rsb")])
                        if c == NCH - 1 and m == NMS - 1:
                            do_dump("carry", lambda: carry[:, :],
                                    [R("carry", t) for t in range(32)])
                    do_dump("gsl", lambda: gsl[:, :], gkeys)
                    pr.barrier()
                sx.close()
                if mstage < 3:
                    return

                with contextlib.ExitStack() as s3:
                    wmv = sb(f"wmv_{m}", [128, 2 * 8192], BF16, "left", s3)
                    vsl = sb(f"vsl_{m}", [128, NMS * 2048], F16, "left", s3)
                    xst = sb(f"xs2_{m}", [128, 2048], F32, "left", s3)
                    for fo in range(8):
                        slot = fo % 2
                        wks = [R("wmv", slot, kt) for kt in range(32)]
                        pr.dma("pool", f"wmv{slot}",
                               lambda h, fo=fo, slot=slot: h.dma_start(
                                   wmv[:, slot * 8192:(slot + 1) * 8192],
                                   wout_h[:, fo * 8192:(fo + 1) * 8192]),
                               writes=wks, tag="wol")
                        for kt in range(32):
                            pr.emit("dve", lambda h, slot=slot, kt=kt: h.tensor_scalar(
                                wmv[:, slot * 8192 + kt * 256:slot * 8192 + (kt + 1) * 256],
                                wmv[:, slot * 8192 + kt * 256:slot * 8192 + (kt + 1) * 256],
                                nwt[:, kt:kt + 1], None, op0=ALU.mult),
                                reads=[R("wmv", slot, kt), R("c_nw")],
                                writes=[R("wmv", slot, kt)], tag="wosc")
                        for stt in range(NMS):
                            for kt in range(32):
                                pr.emit("pe", lambda h, kt=kt, stt=stt, slot=slot:
                                        h.matmul(PY[:, stt * 512:stt * 512 + 256],
                                                 gsl[:, kt * MS + stt * 128:
                                                     kt * MS + stt * 128 + 128],
                                                 wmv[:, slot * 8192 + kt * 256:
                                                     slot * 8192 + (kt + 1) * 256],
                                                 start=(kt == 0), stop=(kt == 31)),
                                        reads=[R("wmv", slot, kt)] + gkeys,
                                        writes=[R("PY", stt * 4), R("PY", stt * 4 + 1)],
                                        tag="opm")
                            pr.emit("dve", lambda h, stt=stt, fo=fo: h.tensor_copy(
                                vsl[:, stt * 2048 + fo * 256:stt * 2048 + (fo + 1) * 256],
                                PY[:, stt * 512:stt * 512 + 256]),
                                reads=[R("PY", stt * 4), R("PY", stt * 4 + 1)],
                                writes=[R("vsl", m, stt, fo)], tag="opev")
                    pr.barrier()
                    for stt in range(NMS):
                        row0 = tok0 + stt * 128
                        vk = [R("vsl", m, stt, fo) for fo in range(8)]
                        pr.dma("sp", "xld",
                               lambda h, r=row0: h.dma_start(xst[:, :], x_h[r:r + 128, :]),
                               writes=[xkey], tag="xld2")
                        rsd, nmr, skeys = ln_stats(
                            lambda stt=stt: vsl[:, stt * 2048:(stt + 1) * 2048], vk,
                            lambda: lntmp[:, :], R("lntmp"))
                        pr.emit("dve", lambda h, r=rsd, b=nmr, stt=stt: h.tensor_scalar(
                            lntmp[:, :], vsl[:, stt * 2048:(stt + 1) * 2048], r, b,
                            op0=ALU.mult, op1=ALU.add),
                            reads=vk + skeys, writes=[R("lntmp")], tag="ln2n")
                        gi = (m * NMS + stt) % 8
                        pr.emit("dve", lambda h, gi=gi: h.tensor_add(
                            x2sl[:, gi * 2048:(gi + 1) * 2048], lntmp[:, :], xst[:, :]),
                            reads=[R("lntmp"), xkey], writes=[R("x2", gi)], tag="x2a")
                    do_dump("vsl", lambda: vsl[:, :],
                            [R("vsl", m, stt, fo) for stt in range(NMS) for fo in range(8)])
                    pr.barrier()

        def ffn_slab(f):
            tok0 = f * FS
            with contextlib.ExitStack() as fsk:
                fh = fsk.enter_context(contextlib.ExitStack())
                h1T = sb(f"h1T_{f}", [128, 32 * FS], BF16, "left", fh)
                h1k = [R("h1T", f, cc) for cc in range(32)]
                with contextlib.ExitStack() as f1:
                    ln3T = sb(f"ln3T_{f}", [128, 16 * FS], BF16, "left", f1)
                    wst = [sb(f"fw{i}_{f}", [128, 2048], BF16, "left", f1)
                           for i in range(2)]
                    lk = []
                    for stt in range(8):
                        x2k = [R("x2", stt)]
                        rsd, nmr, skeys = ln_stats(
                            lambda stt=stt: x2sl[:, stt * 2048:(stt + 1) * 2048], x2k,
                            lambda: lntmp[:, :], R("lntmp"))
                        pr.emit("dve", lambda h, r=rsd, b=nmr, stt=stt: h.tensor_scalar(
                            lntmp[:, :], x2sl[:, stt * 2048:(stt + 1) * 2048], r, b,
                            op0=ALU.mult, op1=ALU.add),
                            reads=x2k + skeys, writes=[R("lntmp")], tag="ln3n")
                        for g in range(4):
                            for j in range(4):
                                kt = g * 4 + j
                                pr.emit("pe", lambda h, kt=kt, j=j: h.transpose(
                                    PT[:, j * 128:(j + 1) * 128],
                                    lntmp[:, kt * 128:(kt + 1) * 128], idbf[:, :]),
                                    reads=[R("lntmp"), R("c_idbf")],
                                    writes=[R("PT", j)], tag="ln3t")
                            pr.emit("dve", lambda h, g=g, stt=stt: h.tensor_copy(
                                ap3(ln3T, g * 4 * FS + stt * 128, FS, 4, 1, 128),
                                PT[:, 0:512]),
                                reads=[R("PT", j) for j in range(4)],
                                writes=[R("ln3T", f, g, stt)], tag="ln3e")
                        lk += [R("ln3T", f, g, stt) for g in range(4)]
                    pr.barrier()
                    for cc in range(32):
                        slot = cc % 2
                        pr.dma("pool", f"fw{slot}",
                               lambda h, cc=cc, slot=slot: h.dma_start(
                                   wst[slot][:, :], w1_h[:, cc * 2048:(cc + 1) * 2048]),
                               writes=[R("fw", slot)], tag="w1l")
                        for ff in range(2):
                            pn, off = PSL[(cc * 2 + ff) % 7]
                            pt = psum_of(pn)
                            pk = pskeys(pn, off)
                            for kt in range(16):
                                pr.emit("pe", lambda h, pt=pt, off=off, kt=kt, ff=ff, sl=slot:
                                        h.matmul(pt[:, off:off + 512],
                                                 wst[sl][:, kt * 128:(kt + 1) * 128],
                                                 ln3T[:, kt * FS + ff * 512:
                                                      kt * FS + ff * 512 + 512],
                                                 start=(kt == 0), stop=(kt == 15)),
                                        reads=[R("fw", slot)] + lk, writes=pk, tag="h1m")
                            pr.emit("act", lambda h, pt=pt, off=off, cc=cc, ff=ff:
                                    h.activation(
                                        h1T[:, cc * FS + ff * 512:cc * FS + ff * 512 + 512],
                                        pt[:, off:off + 512], AF.Relu,
                                        bias=b1t[:, cc:cc + 1]),
                                    reads=pk + [R("c_b1")], writes=[R("h1T", f, cc)],
                                    tag="h1e")
                    do_dump("h1T", lambda: h1T[:, :], h1k)
                    pr.barrier()
                h2T = sb(f"h2T_{f}", [128, 32 * FS], BF16, side="right", stk=fsk)
                h2k = [R("h2T", f, cc) for cc in range(32)]
                with contextlib.ExitStack() as f2:
                    wst2 = [sb(f"f2w{i}_{f}", [128, 4096], BF16, "left", f2)
                            for i in range(2)]
                    for cc in range(32):
                        slot = cc % 2
                        pr.dma("pool", f"f2w{slot}",
                               lambda h, cc=cc, slot=slot: h.dma_start(
                                   wst2[slot][:, :], w2_h[:, cc * 4096:(cc + 1) * 4096]),
                               writes=[R("f2w", slot)], tag="w2l")
                        for ff in range(2):
                            pn, off = PSL[(cc * 2 + ff) % 7]
                            pt = psum_of(pn)
                            pk = pskeys(pn, off)
                            for kt in range(32):
                                pr.emit("pe", lambda h, pt=pt, off=off, kt=kt, ff=ff, sl=slot:
                                        h.matmul(pt[:, off:off + 512],
                                                 wst2[sl][:, kt * 128:(kt + 1) * 128],
                                                 h1T[:, kt * FS + ff * 512:
                                                      kt * FS + ff * 512 + 512],
                                                 start=(kt == 0), stop=(kt == 31)),
                                        reads=[R("f2w", slot)] + h1k, writes=pk, tag="h2m")
                            pr.emit("act", lambda h, pt=pt, off=off, cc=cc, ff=ff:
                                    h.activation(
                                        h2T[:, cc * FS + ff * 512:cc * FS + ff * 512 + 512],
                                        pt[:, off:off + 512], AF.Relu,
                                        bias=b2t[:, cc:cc + 1]),
                                    reads=pk + [R("c_b2")], writes=[R("h2T", f, cc)],
                                    tag="h2e")
                    pr.barrier()
                fh.close()
                with contextlib.ExitStack() as f3:
                    wmv = sb(f"w3m_{f}", [128, 2 * 8192], BF16, "left", f3)
                    fsl = sb(f"fsl_{f}", [128, 8 * 2048], BF16, "left", f3)
                    b3m = sb(f"b3m_{f}", [128, 2048], BF16, "left", f3)
                    b3row = sb(f"b3row_{f}", [1, DM], F32, "left", f3)
                    ost = sb(f"ost_{f}", [128, 2048], F32, "left", f3)
                    pr.dma("sp", "c_b3", lambda h: h.dma_start(b3row[:, :], b3_h),
                           writes=[R("c_b3")], tag="b3l")
                    for q in range(4):
                        pr.emit("pe", lambda h, q=q: h.matmul(
                            PA[:, 0:512], onesf[:, :],
                            b3row[0:1, q * 512:(q + 1) * 512], start=True, stop=True),
                            reads=[R("c_b3"), R("c_onesf")], writes=[R("PAF")], tag="b3b")
                        pr.emit("dve", lambda h, q=q: h.tensor_copy(
                            b3m[:, q * 512:(q + 1) * 512], PA[:, 0:512]),
                            reads=[R("PAF")], writes=[R("b3m", q)], tag="b3e")
                    for fo in range(8):
                        slot = fo % 2
                        pr.dma("pool", f"w3m{slot}",
                               lambda h, fo=fo, slot=slot: h.dma_start(
                                   wmv[:, slot * 8192:(slot + 1) * 8192],
                                   w3_h[:, fo * 8192:(fo + 1) * 8192]),
                               writes=[R("w3m", slot)], tag="w3l")
                        for stt in range(8):
                            pn, off = [("PY", 0), ("PY", 512), ("PY", 1024),
                                       ("PY", 1536), ("PB", 0), ("PC", 0)][stt % 6]
                            pt = psum_of(pn)
                            pk = pskeys(pn, off, 256)
                            for kt in range(32):
                                pr.emit("pe", lambda h, pt=pt, off=off, kt=kt, stt=stt,
                                        sl=slot: h.matmul(
                                            pt[:, off:off + 256],
                                            h2T[:, kt * FS + stt * 128:
                                                 kt * FS + stt * 128 + 128],
                                            wmv[:, sl * 8192 + kt * 256:
                                                sl * 8192 + (kt + 1) * 256],
                                            start=(kt == 0), stop=(kt == 31)),
                                        reads=[R("w3m", slot)] + h2k, writes=pk, tag="w3m")
                            pr.emit("dve", lambda h, pt=pt, off=off, stt=stt, fo=fo:
                                    h.tensor_add(
                                        fsl[:, stt * 2048 + fo * 256:
                                            stt * 2048 + (fo + 1) * 256],
                                        pt[:, off:off + 256],
                                        b3m[:, fo * 256:(fo + 1) * 256]),
                                    reads=pk + [R("b3m", fo // 2)],
                                    writes=[R("fsl", f, stt, fo)], tag="w3e")
                    do_dump("fsl", lambda: fsl[:, :],
                            [R("fsl", f, stt, fo) for stt in range(8) for fo in range(8)])
                    pr.barrier()
                    for stt in range(8):
                        fk = [R("fsl", f, stt, fo) for fo in range(8)]
                        rsd, nmr, skeys = ln_stats(
                            lambda stt=stt: fsl[:, stt * 2048:(stt + 1) * 2048], fk,
                            lambda: lntmp[:, :], R("lntmp"))
                        pr.emit("dve", lambda h, r=rsd, b=nmr, stt=stt: h.tensor_scalar(
                            ost[:, :], fsl[:, stt * 2048:(stt + 1) * 2048], r, b,
                            op0=ALU.mult, op1=ALU.add),
                            reads=fk + skeys, writes=[R("ost")], tag="ln4n")
                        pr.emit("dve", lambda h, stt=stt: h.scalar_tensor_tensor(
                            ost[:, :], x2sl[:, stt * 2048:(stt + 1) * 2048], 1.0,
                            ost[:, :], op0=ALU.mult, op1=ALU.add),
                            reads=[R("x2", stt), R("ost")], writes=[R("ost")], tag="resf")
                        row0 = tok0 + stt * 128
                        pr.dma("sp", "outst",
                               lambda h, r=row0: h.dma_start(out_h[r:r + 128, :],
                                                             ost[:, :]),
                               reads=[R("ost")], tag="outd")
                    for e_ in ENGINES:
                        pr.emit(e_, lambda h: h.nop(), writes=[R("ost")], tag="odrain")
                    pr.barrier()

        if stage <= 3:
            mixer_slab(0, stage)
        else:
            for f in range(NFS):
                for sub in range(2):
                    mixer_slab(f * 2 + sub, 9)
                ffn_slab(f)
                do_dump("x2", lambda: x2sl[:, :], [R("x2", j) for j in range(8)])
        dsem = pr.finalize(st)
        if dsem:
            with nc.Block() as blk:
                def _fin(h):
                    for s_, c_ in dsem.values():
                        h.wait_ge(s_, c_)
                blk.sync(_fin)
    return nc, dump_t


def _rsel():
    r = np.zeros((64, 32 * 128), np.float32)
    for j in range(32):
        r[j, j * 128:(j + 1) * 128] = 1.0
    return r.astype(BF)


def _host_prep(inputs):
    ip = np.ascontiguousarray
    W = inputs["L0_attn_in_proj_w"]
    main = W[:8448].reshape(66, 128, 16, 128).transpose(3, 0, 2, 1).reshape(128, -1)
    dtw = W[8448:].reshape(64, 16, 128).transpose(2, 1, 0).reshape(128, -1)
    per = {
        "wlay": ip(np.concatenate([main, dtw], axis=1)),
        "wout": ip(inputs["L0_attn_out_proj_w"].reshape(8, 256, 32, 128)
                   .transpose(3, 0, 2, 1).reshape(128, -1)),
        "w1": ip(inputs["L0_ffn_W1"].reshape(32, 128, 16, 128)
                 .transpose(3, 0, 2, 1).reshape(128, -1)),
        "w2": ip(inputs["L0_ffn_W2"].reshape(32, 128, 32, 128)
                 .transpose(3, 0, 2, 1).reshape(128, -1)),
        "w3": ip(inputs["L0_ffn_W3"].reshape(8, 256, 32, 128)
                 .transpose(3, 0, 2, 1).reshape(128, -1)),
        "cwt": ip(inputs["L0_attn_conv_w"].reshape(34, 128, 4)
                  .transpose(1, 0, 2).reshape(128, -1)),
        "b1t": ip(inputs["L0_ffn_b1"].reshape(32, 128).T),
        "b2t": ip(inputs["L0_ffn_b2"].reshape(32, 128).T),
        "b3r": ip(inputs["L0_ffn_b3"].reshape(1, -1)),
        "nwt": ip(inputs["L0_attn_norm_w"].reshape(32, 128).T),
        "dtb": ip(inputs["L0_attn_dt_bias"].reshape(64, 1)),
        "alog": ip(inputs["L0_attn_A_log"].reshape(64, 1)),
        "idbf": np.eye(128, dtype=BF),
        "idf": np.eye(128, dtype=np.float32),
        "triu": np.triu(np.ones((128, 128), np.float32)).astype(BF),
        "rep64": np.kron(np.eye(64, dtype=np.float32),
                         np.ones((1, 64), np.float32)).astype(BF),
        "rsel": _rsel(),
        "onesf": np.ones((1, 128), np.float32),
        "onescb": np.ones((128, 1), BF),
        "ones64": np.ones((64, 128), BF),
    }
    D = inputs["L0_attn_D"]
    dc = np.empty((128, 32), np.float32)
    dc[:64, :] = D[0::2][None, :]
    dc[64:, :] = D[1::2][None, :]
    per["dcol"] = dc
    return per


def kernel(**inputs):
    inputs = {k: np.asarray(v) for k, v in inputs.items()}
    per = _host_prep(inputs)
    nc, _ = build(bass.Bass("TRN2", target_bir_lowering=False))
    x = inputs["L0_attn_x"]
    in_maps = []
    for b in range(8):
        d = dict(per)
        d["x"] = np.ascontiguousarray(x[b])
        in_maps.append(d)
    res = run_bass_kernel_spmd(nc, in_maps, core_ids=list(range(8)))
    return np.stack([r["out"] for r in res.results]).astype(np.float32)


# version 09
# speedup vs baseline: 1.0782x; 1.0782x over previous
import contextlib
import numpy as np
import ml_dtypes
import concourse.bass as bass
import concourse.mybir as mybir
from concourse.bass_utils import run_bass_kernel_spmd

F32 = mybir.dt.float32
BF16 = mybir.dt.bfloat16
F16 = mybir.dt.float16
AF = mybir.ActivationFunctionType
ALU = mybir.AluOpType
BF = ml_dtypes.bfloat16

S, DM, HID, NST, HEADS, HP = 2048, 2048, 4096, 128, 64, 64
MS = 512
NMS = S // MS
FS = 1024
NFS = S // FS
CH = 128
NCH = MS // CH
NCC = 66
CLAMP = 30.0
EPS = 1e-5

ENGINES = ("pe", "dve", "act", "pool", "sp")
ENG_ATTR = {"pe": "tensor", "dve": "vector", "act": "scalar",
            "pool": "gpsimd", "sp": "sync"}


class Prog:

    PSBANK = {}

    def __init__(self, nc):
        self.nc = nc
        self.ops = []
        self.stream = {e: [] for e in ENGINES}
        self.last_writer = {}
        self.readers = {}
        self.bankw = {}
        self.epoch = 0

    def emit(self, eng, fn, reads=(), writes=(), tag=""):
        idx = len(self.ops)
        deps = set()
        for r in reads:
            w = self.last_writer.get(r)
            if w is not None:
                deps.add(w)
        for w in writes:
            lw = self.last_writer.get(w)
            if lw is not None:
                deps.add(lw)
            deps.update(self.readers.get(w, ()))
        for w in list(writes) + list(reads):
            bank = Prog.PSBANK.get(w[:2] if len(w) >= 2 else w)
            if bank is not None:
                bw = self.bankw.setdefault(bank, {})
                for e2, i2 in bw.items():
                    if e2 != eng:
                        deps.add(i2)
                bw[eng] = idx
        deps.discard(idx)
        self.ops.append(dict(idx=idx, eng=eng, fn=fn, deps=sorted(deps),
                             epoch=self.epoch, need_inc=False, dma=None, tag=tag))
        self.stream[eng].append(idx)
        for r in reads:
            self.readers.setdefault(r, []).append(idx)
        for w in writes:
            self.last_writer[w] = idx
            self.readers[w] = []
        return idx

    def dma(self, eng, slot, fn, reads=(), writes=(), tag=""):
        idx = self.emit(eng, fn, reads=reads, writes=writes, tag=tag)
        self.ops[idx]["dma"] = slot
        return idx

    def barrier(self):
        self.epoch += 1

    def finalize(self, stack):
        nc = self.nc
        eng_sem = {e: stack.enter_context(nc.semaphore(f"es_{e}")) for e in ENGINES}
        slots = sorted({op["dma"] for op in self.ops if op["dma"] is not None})
        dsem = {s: [stack.enter_context(nc.semaphore(f"ds_{s}")), 0] for s in slots}
        for op in self.ops:
            for d in op["deps"]:
                dop = self.ops[d]
                if dop["dma"] is None and not (dop["eng"] == "pe" and op["eng"] == "pe"):
                    dop["need_inc"] = True
        cnt = {e: 0 for e in ENGINES}
        for e in ENGINES:
            for i in self.stream[e]:
                op = self.ops[i]
                if op["dma"] is not None:
                    dsem[op["dma"]][1] += 16
                    op["sem"], op["semval"], op["incby"] = dsem[op["dma"]][0], dsem[op["dma"]][1], 16
                elif op["need_inc"]:
                    cnt[e] += 1
                    op["sem"], op["semval"], op["incby"] = eng_sem[e], cnt[e], 1
                else:
                    op["sem"] = None
        for e in ENGINES:
            cov = {}
            for i in self.stream[e]:
                op = self.ops[i]
                wmax = {}
                for d in op["deps"]:
                    dop = self.ops[d]
                    if dop["dma"] is None:
                        if dop["eng"] == "pe" and e == "pe":
                            continue
                        if dop["epoch"] < op["epoch"]:
                            continue
                    sem, val = dop["sem"], dop["semval"]
                    k = id(sem)
                    if cov.get(k, 0) >= val:
                        continue
                    if k not in wmax or wmax[k][1] < val:
                        wmax[k] = (sem, val)
                for k, (sem, val) in wmax.items():
                    cov[k] = max(cov.get(k, 0), val)
                op["waits"] = list(wmax.values())
        by_ep = {}
        for e in ENGINES:
            for i in self.stream[e]:
                by_ep.setdefault((self.ops[i]["epoch"], e), []).append(i)
        for ep in range(self.epoch + 1):
            engs = [e for e in ENGINES if (ep, e) in by_ep]
            if not engs:
                continue
            with nc.Block() as blk:
                for e in engs:
                    idxs = by_ep[(ep, e)]

                    def body(h, idxs=idxs):
                        for i in idxs:
                            op = self.ops[i]
                            for sem, val in op["waits"]:
                                h.wait_ge(sem, val)
                            inst = op["fn"](h)
                            if op["sem"] is not None:
                                inst.then_inc(op["sem"], op["incby"])
                    getattr(blk, ENG_ATTR[e])(body)
        return dsem


def ap3(t, base, s1, n1, s2, n2, p0=0, pn=128):
    rl = t.shape[1]
    return bass.AP(t, p0 * rl + base, [[rl, pn], [s1, n1], [s2, n2]])


def ap2(t, base, s1, n1, p0, pn):
    rl = t.shape[1]
    return bass.AP(t, p0 * rl + base, [[rl, pn], [s1, n1]])


def build(nc, dumps=(), stage=9):
    pr = Prog(nc)
    dumps = set(dumps)
    dump_t = {}
    R = lambda *a: a

    def dram(name, shape, dtype, kind):
        return nc.dram_tensor(name, shape, dtype, kind=kind).ap()

    x_h = dram("x", [S, DM], F32, "ExternalInput")
    out_h = dram("out", [S, DM], F32, "ExternalOutput")
    wlay_h = dram("wlay", [128, NCC * 2048 + 1024], F32, "ExternalInput")
    wout_h = dram("wout", [128, 8 * 8192], F32, "ExternalInput")
    w1_h = dram("w1", [128, 32 * 2048], F32, "ExternalInput")
    w2_h = dram("w2", [128, 32 * 4096], F32, "ExternalInput")
    w3_h = dram("w3", [128, 8 * 8192], F32, "ExternalInput")
    cwt_h = dram("cwt", [128, 136], F32, "ExternalInput")
    b1_h = dram("b1t", [128, 32], F32, "ExternalInput")
    b2_h = dram("b2t", [128, 32], F32, "ExternalInput")
    b3_h = dram("b3r", [1, DM], F32, "ExternalInput")
    nw_h = dram("nwt", [128, 32], F32, "ExternalInput")
    dcol_h = dram("dcol", [128, 32], F32, "ExternalInput")
    dtb_h = dram("dtb", [64, 1], F32, "ExternalInput")
    alog_h = dram("alog", [64, 1], F32, "ExternalInput")
    idbf_h = dram("idbf", [128, 128], BF16, "ExternalInput")
    idf_h = dram("idf", [128, 128], F32, "ExternalInput")
    triu_h = dram("triu", [128, 128], BF16, "ExternalInput")
    rep_h = dram("rep64", [64, 4096], BF16, "ExternalInput")
    rsel_h = dram("rsel", [64, 4096], BF16, "ExternalInput")
    onesf_h = dram("onesf", [1, 128], F32, "ExternalInput")
    onescb_h = dram("onescb", [128, 1], BF16, "ExternalInput")
    ones64_h = dram("ones64", [64, 128], BF16, "ExternalInput")

    DUMP_SHAPES = {
        "xbcz": ([128, NCC * MS], BF16), "dt": ([64, MS], F32),
        "dAc": ([64, MS], F32), "gsl": ([128, 32 * MS], BF16),
        "vsl": ([128, NMS * 2048], F16), "x2": ([128, 8 * 2048], BF16),
        "h1T": ([128, 32 * FS], BF16), "fsl": ([128, 8 * 2048], BF16),
        "xhT": ([128, 4096], BF16), "carry": ([128, 4096], BF16),
        "yrw1": ([128, 2048], BF16), "rsb1": ([128, 128], F32),
        "cds0": ([128, 512], BF16), "cmy0": ([128, 512], BF16),
        "cdq0": ([64, 512], BF16), "st0": ([128, 512], BF16),
        "bsb0": ([128, 128], BF16), "xhd0": ([128, 512], BF16), "dtd0": ([128, 512], BF16),
        "ddtbf0": ([64, 512], BF16),
        "carry0": ([128, 4096], BF16), "esb1": ([128, 512], BF16),
    }
    for nm in dumps:
        shp, dt_ = DUMP_SHAPES[nm]
        dump_t[nm] = dram("dump_" + nm, shp, dt_, "ExternalOutput")

    def do_dump(name, src_fn, reads):
        t = dump_t.get(name)
        if t is not None:
            pr.dma("sp", "dmp_" + name, lambda h, t=t: h.dma_start(t, src_fn()),
                   reads=list(reads), tag="dump")

    st = contextlib.ExitStack()
    with st:
        def sb(n, shape, dt_, side="left", stk=st):
            return stk.enter_context(nc.sbuf_tensor("s_" + n, shape, dt_, side=side))

        PY = st.enter_context(nc.psum_tensor("PY", [128, 2048], F32))
        PA = st.enter_context(nc.psum_tensor("PA", [128, 512], F32))
        PB = st.enter_context(nc.psum_tensor("PB", [128, 512], F32))
        PC = st.enter_context(nc.psum_tensor("PC", [128, 512], F32))
        PT = st.enter_context(nc.psum_tensor("PT", [128, 1024], BF16))
        for _t in range(16):
            Prog.PSBANK[("PY", _t)] = _t // 4
        Prog.PSBANK[("PAa",)] = 4
        Prog.PSBANK[("PAs",)] = 4
        Prog.PSBANK[("PAr",)] = 4
        Prog.PSBANK[("PAF",)] = 4
        for _j in range(4):
            Prog.PSBANK[("PB", _j)] = 5
            Prog.PSBANK[("PC", _j)] = 6
        for _j in range(8):
            Prog.PSBANK[("PT", _j)] = 7

        idbf = sb("idbf", [128, 128], BF16)
        idf = sb("idf", [128, 128], F32)
        triu = sb("triu", [128, 128], BF16)
        onesf = sb("onesf", [1, 128], F32)
        onescb = sb("onescb", [128, 1], BF16)
        ones64 = sb("ones64", [64, 128], BF16)
        cwt = sb("cwt", [128, 136], F32)
        b1t = sb("b1t", [128, 32], F32)
        b2t = sb("b2t", [128, 32], F32)
        nwt = sb("nwt", [128, 32], F32)
        dcol = sb("dcol", [128, 32], F32)
        dtb = sb("dtb", [64, 1], F32)
        alog = sb("alog", [64, 1], F32)
        acol = sb("acol", [64, 1], F32)
        carry = sb("carry", [128, 4096], BF16)
        tails = sb("tails", [128, 102], F32)
        lntmp = sb("lntmp", [128, 2048], BF16)
        stat = sb("stat", [128, 16], F32)
        srow = sb("srow", [1, 128], F32)
        x2sl = sb("x2sl", [128, 8 * 2048], BF16, side="right")

        def cload(dst, src, slot):
            pr.dma("sp", slot, lambda h, d=dst, s=src: h.dma_start(d[:, :], s),
                   writes=[R(slot)], tag="cload")
        cload(idbf, idbf_h, "c_idbf"); cload(idf, idf_h, "c_idf")
        cload(triu, triu_h, "c_triu")
        cload(onesf, onesf_h, "c_onesf")
        cload(onescb, onescb_h, "c_onescb"); cload(ones64, ones64_h, "c_ones64")
        cload(cwt, cwt_h, "c_cwt"); cload(b1t, b1_h, "c_b1")
        cload(b2t, b2_h, "c_b2"); cload(nwt, nw_h, "c_nw")
        cload(dcol, dcol_h, "c_dcol"); cload(dtb, dtb_h, "c_dtb")
        cload(alog, alog_h, "c_alog")
        pr.emit("act", lambda h: h.activation(acol[:, :], alog[:, :], AF.Exp),
                reads=[R("c_alog")], writes=[R("acol")])
        pr.emit("dve", lambda h: h.tensor_scalar_mul(acol[:, :], acol[:, :], -1.0),
                reads=[R("acol")], writes=[R("acol")])
        pr.emit("dve", lambda h: h.memset(tails[:, :], 0.0),
                writes=[R("tails", c) for c in range(34)])
        pr.emit("dve", lambda h: h.memset(carry[:, :], 0.0),
                writes=[R("carry", t) for t in range(32)])
        pr.barrier()

        ln_ctr = [0]

        def ln_stats(src_fn, src_keys, scrap_fn, scrap_key):
            k = (ln_ctr[0] % 2) * 5
            ln_ctr[0] += 1
            ssum, ssq, va, rsd, mm_ = k, k + 1, k + 2, k + 3, k + 4
            K = lambda j: R("stat", j)
            pr.emit("dve", lambda h: h.tensor_scalar(
                scrap_fn(), src_fn(), 1.0, 0.0, op0=ALU.mult, op1=ALU.add,
                accum_out=stat[:, ssum:ssum + 1]),
                reads=list(src_keys), writes=[scrap_key, K(ssum)], tag="lns")
            pr.emit("act", lambda h: h.activation(
                scrap_fn(), src_fn(), AF.Square, accum_out=stat[:, ssq:ssq + 1]),
                reads=list(src_keys) + [scrap_key], writes=[scrap_key, K(ssq)], tag="lnq")
            N = 2048.0
            pr.emit("dve", lambda h: h.tensor_scalar(
                stat[:, va:va + 1], stat[:, ssq:ssq + 1], 1.0 / N, EPS,
                op0=ALU.mult, op1=ALU.add), reads=[K(ssq)], writes=[K(va)])
            pr.emit("dve", lambda h: h.tensor_scalar_mul(
                stat[:, mm_:mm_ + 1], stat[:, ssum:ssum + 1], 1.0 / N),
                reads=[K(ssum)], writes=[K(mm_)])
            pr.emit("dve", lambda h: h.tensor_scalar_mul(
                stat[:, ssum:ssum + 1], stat[:, ssum:ssum + 1], -1.0 / N),
                reads=[K(ssum)], writes=[K(ssum)])
            pr.emit("dve", lambda h: h.scalar_tensor_tensor(
                stat[:, va:va + 1], stat[:, mm_:mm_ + 1], stat[:, ssum:ssum + 1],
                stat[:, va:va + 1], op0=ALU.mult, op1=ALU.add),
                reads=[K(mm_), K(ssum), K(va)], writes=[K(va)])
            pr.emit("act", lambda h: h.activation(
                stat[:, rsd:rsd + 1], stat[:, va:va + 1], AF.Ln),
                reads=[K(va)], writes=[K(rsd)])
            pr.emit("act", lambda h: h.activation(
                stat[:, rsd:rsd + 1], stat[:, rsd:rsd + 1], AF.Exp, scale=-0.5),
                reads=[K(rsd)], writes=[K(rsd)])
            pr.emit("dve", lambda h: h.tensor_scalar(
                stat[:, ssum:ssum + 1], stat[:, ssum:ssum + 1],
                stat[:, rsd:rsd + 1], None, op0=ALU.mult),
                reads=[K(ssum), K(rsd)], writes=[K(ssum)])
            return stat[:, rsd:rsd + 1], stat[:, ssum:ssum + 1], [K(rsd), K(ssum)]

        PSL = [("PY", 0), ("PY", 512), ("PY", 1024), ("PY", 1536),
               ("PA", 0), ("PB", 0), ("PC", 0)]

        def psum_of(name):
            return {"PY": PY, "PA": PA, "PB": PB, "PC": PC}[name]

        def pskeys(name, off, width=512):
            if name == "PY":
                return [R("PY", off // 128 + j) for j in range(width // 128)]
            if name == "PA":
                return [R("PAF")]
            return [R(name, off // 128 + j) for j in range(width // 128)]

        def mixer_slab(m, mstage):
            tok0 = m * MS
            ms = contextlib.ExitStack()
            with ms:
                msb = lambda n, s_, d, side="left": sb(f"{n}_{m}", s_, d, side, ms)
                dtF = msb("dtF", [64, MS], F32)
                dAcF = msb("dAcF", [64, MS], F32)
                colBF = msb("colBF", [64, MS], F32)
                ddtF = msb("ddtF", [64, MS], F32)
                zer64 = msb("zer64", [64, 128], F32)
                ddtbf = msb("ddtbf", [64, MS], BF16)
                dachi = msb("dachi", [128, MS], BF16)
                daclo = msb("daclo", [128, MS], BF16)
                dachi2 = msb("dachi2", [64, MS], BF16)
                daclo2 = msb("daclo2", [64, MS], BF16)
                ephbf = msb("ephbf", [64, MS], BF16)
                ephF = msb("ephF", [64, MS], F32)
                gsl = msb("gsl", [128, 32 * MS], BF16, side="right")
                kz = lambda cc: R("xbcz", m, cc)
                xkey, gkeys = R("xst", m), [R("gsl", m, c, H)
                                           for c in range(NCH) for H in range(2)]

                sx = ms.enter_context(contextlib.ExitStack())
                xbcz = sb(f"xbcz_{m}", [128, NCC * MS], BF16, "left", sx)

                with contextlib.ExitStack() as s1:
                    ln1T = sb(f"ln1T_{m}", [128, 16 * MS], BF16, "left", s1)
                    wst = [sb(f"wst{i}_{m}", [128, 2048], BF16, "left", s1) for i in range(2)]
                    rawx = sb(f"rawx_{m}", [128, 1032], F32, "left", s1)
                    cvt = [sb(f"cvt{i}_{m}", [128, 512], F32, "left", s1) for i in range(2)]
                    xst = sb(f"xst_{m}", [128, 2048], F32, "left", s1)
                    lnkeys = []
                    for stt in range(NMS):
                        row0 = tok0 + stt * 128
                        pr.dma("sp", "xld",
                               lambda h, r=row0: h.dma_start(xst[:, :], x_h[r:r + 128, :]),
                               writes=[xkey], tag="xld")
                        rsd, nmr, skeys = ln_stats(lambda: xst[:, :], [xkey],
                                                   lambda: lntmp[:, :], R("lntmp"))
                        pr.emit("dve", lambda h, r=rsd, b=nmr: h.tensor_scalar(
                            lntmp[:, :], xst[:, :], r, b, op0=ALU.mult, op1=ALU.add),
                            reads=[xkey] + skeys, writes=[R("lntmp")], tag="ln1n")
                        for g in range(4):
                            for j in range(4):
                                kt = g * 4 + j
                                pr.emit("pe", lambda h, kt=kt, j=j: h.transpose(
                                    PT[:, j * 128:(j + 1) * 128],
                                    lntmp[:, kt * 128:(kt + 1) * 128], idbf[:, :]),
                                    reads=[R("lntmp"), R("c_idbf")],
                                    writes=[R("PT", j)], tag="ln1t")
                            pr.emit("dve", lambda h, g=g, stt=stt: h.tensor_copy(
                                ap3(ln1T, g * 4 * MS + stt * 128, MS, 4, 1, 128),
                                PT[:, 0:512]),
                                reads=[R("PT", j) for j in range(4)],
                                writes=[R("ln1T", m, g, stt)], tag="ln1e")
                        lnkeys += [R("ln1T", m, g, stt) for g in range(4)]
                    pr.barrier()

                    for cc in range(NCC + 1):
                        slot = cc % 2
                        if cc < NCC:
                            src = wlay_h[:, cc * 2048:(cc + 1) * 2048]
                            dst = wst[slot][:, 0:2048]
                        else:
                            src = wlay_h[:, NCC * 2048:NCC * 2048 + 1024]
                            dst = wst[slot][:, 0:1024]
                        pr.dma("pool", f"wst{slot}",
                               lambda h, d=dst, s=src: h.dma_start(d, s),
                               writes=[R("wst", slot)], tag=f"wld{cc}")
                        pn, off = PSL[cc % 7]
                        pt = psum_of(pn)
                        pkeys = pskeys(pn, off)
                        wid = 64 if cc == NCC else 128
                        for kt in range(16):
                            pr.emit("pe", lambda h, kt=kt, pt=pt, off=off, w=wid, sl=slot:
                                    h.matmul(pt[0:w, off:off + 512],
                                             wst[sl][:, kt * w:(kt + 1) * w],
                                             ln1T[:, kt * MS:(kt + 1) * MS],
                                             start=(kt == 0), stop=(kt == 15)),
                                    reads=[R("wst", slot)] + lnkeys,
                                    writes=pkeys, tag=f"ip{cc}")
                        if cc < 32:
                            pr.emit("act", lambda h, pt=pt, off=off, cc=cc: h.activation(
                                xbcz[:, cc * MS:(cc + 1) * MS], pt[:, off:off + 512],
                                AF.Silu), reads=pkeys, writes=[kz(cc)], tag="zev")
                        elif cc < NCC:
                            ccv = cc - 32
                            par = ccv % 2
                            rx0 = par * 516
                            cv = cvt[par]
                            ck = R("cvt", par)
                            rk0, rk1 = R("rawx", par, 0), R("rawx", par, 1)
                            pr.emit("dve", lambda h, ccv=ccv, rx0=rx0: h.tensor_copy(
                                rawx[:, rx0:rx0 + 3], tails[:, ccv * 3:ccv * 3 + 3]),
                                reads=[R("tails", ccv)], writes=[rk0], tag="cvt0")
                            pr.emit("act", lambda h, pt=pt, off=off, rx0=rx0: h.copy(
                                rawx[:, rx0 + 3:rx0 + 515], pt[:, off:off + 512]),
                                reads=pkeys, writes=[rk1], tag="cvtr")
                            pr.emit("dve", lambda h, ccv=ccv, rx0=rx0: h.tensor_copy(
                                tails[:, ccv * 3:ccv * 3 + 3], rawx[:, rx0 + 512:rx0 + 515]),
                                reads=[rk1], writes=[R("tails", ccv)], tag="cvnt")
                            w = lambda j, ccv=ccv: cwt[:, ccv * 4 + j:ccv * 4 + j + 1]
                            pr.emit("dve", lambda h, w=w, cv=cv, rx0=rx0: h.tensor_scalar(
                                cv[:, :], rawx[:, rx0 + 1:rx0 + 513], w(1), None,
                                op0=ALU.mult),
                                reads=[rk0, rk1, R("c_cwt")], writes=[ck], tag="cva1")
                            pr.emit("dve", lambda h, w=w, cv=cv, rx0=rx0:
                                    h.scalar_tensor_tensor(
                                        cv[:, :], rawx[:, rx0:rx0 + 512], w(0), cv[:, :],
                                        op0=ALU.mult, op1=ALU.add),
                                    reads=[rk0, rk1, R("c_cwt"), ck], writes=[ck], tag="cva0")
                            pr.emit("dve", lambda h, w=w, cv=cv, rx0=rx0:
                                    h.scalar_tensor_tensor(
                                        cv[:, :], rawx[:, rx0 + 3:rx0 + 515], w(3), cv[:, :],
                                        op0=ALU.mult, op1=ALU.add),
                                    reads=[rk1, R("c_cwt"), ck], writes=[ck], tag="cvb3")
                            pr.emit("dve", lambda h, w=w, cv=cv, rx0=rx0:
                                    h.scalar_tensor_tensor(
                                        cv[:, :], rawx[:, rx0 + 2:rx0 + 514], w(2), cv[:, :],
                                        op0=ALU.mult, op1=ALU.add),
                                    reads=[rk0, rk1, R("c_cwt"), ck], writes=[ck], tag="cvb2")
                            pr.emit("act", lambda h, cc=cc, cv=cv: h.activation(
                                xbcz[:, cc * MS:(cc + 1) * MS], cv[:, :], AF.Silu),
                                reads=[ck], writes=[kz(cc)], tag="cvs")
                        else:
                            t0k, t1k = R("dt0"), R("dt1")
                            pr.emit("act", lambda h, pt=pt, off=off: h.activation(
                                cvt[0][0:64, :], pt[0:64, off:off + 512], AF.Abs,
                                bias=dtb[:, :]),
                                reads=pkeys + [R("c_dtb")], writes=[t0k], tag="dtab")
                            pr.emit("act", lambda h: h.activation(
                                cvt[0][0:64, :], cvt[0][0:64, :], AF.Exp, scale=-1.0),
                                reads=[t0k], writes=[t0k], tag="dtex")
                            pr.emit("act", lambda h: h.activation(
                                cvt[0][0:64, :], cvt[0][0:64, :], AF.Ln, bias=1.0),
                                reads=[t0k], writes=[t0k], tag="dtln")
                            pr.emit("act", lambda h, pt=pt, off=off: h.activation(
                                cvt[1][0:64, :], pt[0:64, off:off + 512], AF.Relu,
                                bias=dtb[:, :]),
                                reads=pkeys + [R("c_dtb")], writes=[t1k], tag="dtre")
                            pr.emit("dve", lambda h: h.tensor_add(
                                dtF[:, :], cvt[0][0:64, :], cvt[1][0:64, :]),
                                reads=[t0k, t1k], writes=[R("dt", m)], tag="dtad")
                            pr.emit("act", lambda h: h.activation(
                                colBF[:, :], dtF[:, :], AF.Ln),
                                reads=[R("dt", m)], writes=[R("lndt", m)], tag="dtld")
                    do_dump("xbcz", lambda: xbcz[:, :], [kz(c) for c in range(NCC)])
                    do_dump("dt", lambda: dtF[:, :], [R("dt", m)])
                    pr.barrier()
                if mstage < 2:
                    return

                pr.emit("dve", lambda h: h.tensor_scalar(
                    ddtF[:, :], dtF[:, :], acol[:, :], None, op0=ALU.mult),
                    reads=[R("dt", m), R("acol")], writes=[R("dA", m)], tag="dA")
                pr.emit("dve", lambda h: h.memset(zer64[:, :], 0.0),
                        writes=[R("z64", m)], tag="z64")
                pr.emit("dve", lambda h: h.memset(dachi[64:128, :], 0.0),
                        writes=[R("dahz", m)], tag="hz")
                pr.emit("dve", lambda h: h.memset(daclo[64:128, :], 0.0),
                        writes=[R("dalz", m)], tag="lz")
                dack = [R("dAc", m, c) for c in range(NCH)]
                for c in range(NCH):
                    pr.emit("dve", lambda h, c=c: h.tensor_tensor_scan(
                        dAcF[:, c * CH:(c + 1) * CH],
                        ddtF[:, c * CH:(c + 1) * CH],
                        zer64[:, :], 0.0, op0=ALU.add, op1=ALU.add),
                        reads=[R("dA", m), R("z64", m)], writes=[R("dAc", m, c)], tag="cums")
                pr.emit("dve", lambda h: h.tensor_sub(
                    colBF[:, :], colBF[:, :], dAcF[:, :]),
                    reads=[R("lndt", m)] + dack, writes=[R("colB", m)], tag="colB")
                pr.emit("act", lambda h: h.activation(
                    ephF[:, :], dAcF[:, :], AF.Exp),
                    reads=dack, writes=[R("eph", m)], tag="eph")
                pr.emit("dve", lambda h: h.tensor_copy(ephbf[:, :], ephF[:, :]),
                        reads=[R("eph", m)], writes=[R("ephbf", m)], tag="ephb")
                pr.emit("dve", lambda h: h.tensor_copy(dachi[0:64, :], dAcF[:, :]),
                        reads=dack, writes=[R("dachi", m)], tag="hib")
                pr.emit("dve", lambda h: h.scalar_tensor_tensor(
                    daclo[0:64, :], dachi[0:64, :], -1.0, dAcF[:, :],
                    op0=ALU.mult, op1=ALU.add),
                    reads=[R("dachi", m)] + dack, writes=[R("daclo", m)], tag="lob")
                pr.dma("pool", "dh2", lambda h: h.dma_start(
                    dachi2[0:64, :], dachi[32:96, :]),
                    reads=[R("dachi", m), R("dahz", m)], writes=[R("dachi2", m)], tag="dh2")
                pr.dma("pool", "dl2", lambda h: h.dma_start(
                    daclo2[0:64, :], daclo[32:96, :]),
                    reads=[R("daclo", m), R("dalz", m)], writes=[R("daclo2", m)], tag="dl2")
                for c in range(NCH):
                    last = c * CH + CH - 1
                    pr.emit("dve", lambda h, c=c, last=last: h.tensor_scalar(
                        ddtF[:, c * CH:(c + 1) * CH],
                        dAcF[:, c * CH:(c + 1) * CH],
                        dAcF[:, last:last + 1], -1.0,
                        op0=ALU.subtract, op1=ALU.mult),
                        reads=[R("dAc", m, c)],
                        writes=[R("ddt", m, c), R("dA", m)], tag="dd1")
                    pr.emit("act", lambda h, c=c: h.activation(
                        ddtF[:, c * CH:(c + 1) * CH],
                        ddtF[:, c * CH:(c + 1) * CH], AF.Exp),
                        reads=[R("ddt", m, c)], writes=[R("ddt", m, c)], tag="dd2")
                    pr.emit("dve", lambda h, c=c: h.tensor_mul(
                        ddtF[:, c * CH:(c + 1) * CH],
                        ddtF[:, c * CH:(c + 1) * CH],
                        dtF[:, c * CH:(c + 1) * CH]),
                        reads=[R("ddt", m, c), R("dt", m)],
                        writes=[R("ddt", m, c)], tag="dd3")
                pr.emit("dve", lambda h: h.tensor_copy(ddtbf[:, :], ddtF[:, :]),
                        reads=[R("ddt", m, c) for c in range(NCH)],
                        writes=[R("ddtbf", m)], tag="ddb")
                do_dump("dAc", lambda: dAcF[:, :], dack)
                pr.barrier()

                with contextlib.ExitStack() as s2:
                    rsel = sb(f"rsel_{m}", [64, 4096], BF16, "left", s2)
                    rep64 = sb(f"rep64_{m}", [64, 4096], BF16, "left", s2)
                    xhT = sb(f"xhT_{m}", [128, 4096], BF16, "left", s2)
                    xhd = sb(f"xhd_{m}", [128, 1024], BF16, "left", s2)
                    mtr = sb(f"mtr_{m}", [128, 1024], BF16, "left", s2)
                    gtm = sb(f"gtm_{m}", [128, 128], BF16, "left", s2)
                    bsb = sb(f"bsb_{m}", [128, 128], BF16, "left", s2)
                    cbt = sb(f"cbt_{m}", [128, 64], F32, "left", s2)
                    esb = sb(f"esb_{m}", [128, 512], BF16, "left", s2)
                    yrw = sb(f"yrw_{m}", [128, 2048], BF16, "left", s2)
                    qeb = sb(f"qeb_{m}", [128, 1024], BF16, "left", s2)
                    dxh = sb(f"dxh_{m}", [128, 2048], BF16, "left", s2)
                    cdq = sb(f"cdq_{m}", [64, 512], BF16, "left", s2)
                    cds = sb(f"cds_{m}", [128, 512], BF16, "left", s2)
                    cmy = sb(f"cmy_{m}", [128, 512], BF16, "left", s2)
                    rsb = sb(f"rsb_{m}", [128, 128], F32, "left", s2)
                    pr.dma("sp", "rsel", lambda h: h.dma_start(rsel[:, :], rsel_h),
                           writes=[R("c_rsel")], tag="rsl")
                    pr.dma("sp", "rep", lambda h: h.dma_start(rep64[:, :], rep_h),
                           writes=[R("c_rep")], tag="repl")

                    for c in range(NCH):
                        xhTk = [R("xhT", g) for g in range(8)]
                        for g in range(8):
                            for j in range(4):
                                t = g * 4 + j
                                pr.emit("pe", lambda h, t=t, j=j, c=c: h.transpose(
                                    PT[:, j * 128:(j + 1) * 128],
                                    xbcz[:, (32 + t) * MS + c * CH:(32 + t) * MS + c * CH + 128],
                                    idbf[:, :]),
                                    reads=[kz(32 + t), R("c_idbf")],
                                    writes=[R("PT", j)], tag="xhTt")
                            pr.emit("dve", lambda h, g=g: h.tensor_copy(
                                xhT[:, g * 512:(g + 1) * 512], PT[:, 0:512]),
                                reads=[R("PT", j) for j in range(4)],
                                writes=[R("xhT", g)], tag="xhTe")
                        pr.emit("pe", lambda h, c=c: h.transpose(
                            PT[:, 0:128],
                            xbcz[:, 64 * MS + c * CH:64 * MS + c * CH + 128], idbf[:, :]),
                            reads=[kz(64), R("c_idbf")], writes=[R("PT", 0)], tag="Btt")
                        pr.emit("dve", lambda h: h.tensor_copy(bsb[:, :], PT[:, 0:128]),
                                reads=[R("PT", 0)], writes=[R("bsb")], tag="Bte")
                        pr.emit("pe", lambda h, c=c: h.matmul(
                            PA[:, 0:64], colBF[:, c * CH:(c + 1) * CH],
                            idf[0:64, 0:64], start=True, stop=True),
                            reads=[R("colB", m), R("c_idf")], writes=[R("PAa")], tag="cbtt")
                        pr.emit("dve", lambda h: h.tensor_copy(cbt[:, :], PA[:, 0:64]),
                                reads=[R("PAa")], writes=[R("cbt")], tag="cbte")
                        pr.emit("pe", lambda h, c=c: h.matmul(
                            PB[:, 0:128],
                            xbcz[:, 64 * MS + c * CH:64 * MS + c * CH + 128],
                            xbcz[:, 65 * MS + c * CH:65 * MS + c * CH + 128],
                            start=True, stop=True),
                            reads=[kz(64), kz(65)], writes=[R("PB", 0)], tag="Gt")
                        pr.emit("dve", lambda h: h.tensor_mul(
                            gtm[:, :], PB[:, 0:128], triu[:, :]),
                            reads=[R("PB", 0), R("c_triu")], writes=[R("gtm")], tag="Gtm")

                        for H in range(2):
                            hp0, t0 = H * 2048, H * 16
                            def em_bc(hh):
                                h_ = H * 32 + hh
                                pbs = hh % 4
                                hi_t = dachi if h_ < 32 else dachi2
                                lo_t = daclo if h_ < 32 else daclo2
                                hi_k = R("dachi", m) if h_ < 32 else R("dachi2", m)
                                lo_k = R("daclo", m) if h_ < 32 else R("daclo2", m)
                                pr.emit("pe", lambda h, h_=h_, pbs=pbs, t=hi_t, c=c:
                                        h.matmul(PB[:, pbs * 128:(pbs + 1) * 128],
                                                 rsel[:, (h_ % 32) * 128:(h_ % 32 + 1) * 128],
                                                 ap2(t, c * CH, 1, CH, 0, 64),
                                                 start=True, stop=False),
                                        reads=[hi_k, R("dahz", m), R("c_rsel")],
                                        writes=[R("PB", pbs)], tag="bch")
                                pr.emit("pe", lambda h, h_=h_, pbs=pbs, t=lo_t, c=c:
                                        h.matmul(PB[:, pbs * 128:(pbs + 1) * 128],
                                                 rsel[:, (h_ % 32) * 128:(h_ % 32 + 1) * 128],
                                                 ap2(t, c * CH, 1, CH, 0, 64),
                                                 start=False, stop=True),
                                        reads=[lo_k, R("dalz", m), R("c_rsel")],
                                        writes=[R("PB", pbs)], tag="bcl")

                            def em_seg(hh):
                                h_ = H * 32 + hh
                                pbs = pcs = hh % 4
                                pr.emit("dve", lambda h, pbs=pbs, pcs=pcs, h_=h_:
                                        h.tensor_scalar(
                                            PC[:, pcs * 128:(pcs + 1) * 128],
                                            PB[:, pbs * 128:(pbs + 1) * 128],
                                            cbt[:, h_ % 64:h_ % 64 + 1], CLAMP,
                                            op0=ALU.add, op1=ALU.min),
                                        reads=[R("PB", pbs), R("cbt")],
                                        writes=[R("PC", pcs)], tag="seg")

                            def em_exp(hh):
                                pcs, mts = hh % 4, hh % 8
                                pr.emit("act", lambda h, pcs=pcs, mts=mts: h.activation(
                                    mtr[:, mts * 128:(mts + 1) * 128],
                                    PC[:, pcs * 128:(pcs + 1) * 128], AF.Exp),
                                    reads=[R("PC", pcs)], writes=[R("mtr", mts)], tag="exp")

                            def em_mtm(hh):
                                mts = hh % 8
                                pr.emit("dve", lambda h, mts=mts: h.tensor_mul(
                                    mtr[:, mts * 128:(mts + 1) * 128],
                                    mtr[:, mts * 128:(mts + 1) * 128], gtm[:, :]),
                                    reads=[R("mtr", mts), R("gtm")],
                                    writes=[R("mtr", mts)], tag="mtm")

                            def em_ydg(hh):
                                h_ = H * 32 + hh
                                mts = hh % 8
                                pr.emit("pe", lambda h, h_=h_, hh=hh, mts=mts: h.matmul(
                                    PY[(h_ % 2) * 64:(h_ % 2) * 64 + 64,
                                       (hh // 2) * 128:(hh // 2) * 128 + 128],
                                    xhT[:, h_ * 64:(h_ + 1) * 64],
                                    mtr[:, mts * 128:(mts + 1) * 128],
                                    start=True, stop=True,
                                    tile_position=(0, (h_ % 2) * 64)),
                                    reads=[R("mtr", mts)] + xhTk,
                                    writes=[R("PY", hh // 2)], tag="ydg")

                            for hh in range(35):
                                if hh < 32:
                                    em_bc(hh)
                                if 1 <= hh < 33:
                                    em_seg(hh - 1)
                                if 2 <= hh < 34:
                                    em_exp(hh - 2)
                                if 3 <= hh:
                                    em_mtm(hh - 3)
                                    em_ydg(hh - 3)
                            pr.emit("act", lambda h: h.copy(yrw[:, :], PY[:, 0:2048]),
                                    reads=[R("PY", t) for t in range(16)],
                                    writes=[R("yrw")], tag="yev")
                            if c == 1 and H == 0 and m == 0:
                                do_dump("yrw1", lambda: yrw[:, :], [R("yrw")])
                            for t in range(16):
                                pr.emit("pe", lambda h, t=t, t0=t0, c=c: h.matmul(
                                    PY[:, t * 128:(t + 1) * 128],
                                    carry[:, (t0 + t) * 128:(t0 + t + 1) * 128],
                                    xbcz[:, 65 * MS + c * CH:65 * MS + c * CH + 128],
                                    start=True, stop=True),
                                    reads=[R("carry", t0 + t), kz(65)],
                                    writes=[R("PY", t)], tag="Q")
                            for q in range(4):
                                for j in range(4):
                                    t = q * 4 + j
                                    pr.emit("pe", lambda h, t=t, j=j, t0=t0, c=c: h.matmul(
                                        PC[:, j * 128:(j + 1) * 128],
                                        rep64[:, (t0 + t) * 128:(t0 + t + 1) * 128],
                                        ephbf[:, c * CH:(c + 1) * CH],
                                        start=True, stop=True),
                                        reads=[R("c_rep"), R("ephbf", m)],
                                        writes=[R("PC", j)], tag="E")
                                pr.emit("act", lambda h: h.copy(esb[:, :], PC[:, 0:512]),
                                        reads=[R("PC", j) for j in range(4)],
                                        writes=[R("esb")], tag="Eev")
                                qs = q % 2
                                pr.emit("dve", lambda h, q=q, qs=qs: h.tensor_mul(
                                    qeb[:, qs * 512:(qs + 1) * 512],
                                    PY[:, q * 512:(q + 1) * 512], esb[:, :]),
                                    reads=[R("PY", q * 4 + j) for j in range(4)] + [R("esb")],
                                    writes=[R("qeb", qs)], tag="qe")
                                pr.emit("dve", lambda h, q=q, qs=qs: h.scalar_tensor_tensor(
                                    yrw[:, q * 512:(q + 1) * 512],
                                    qeb[:, qs * 512:(qs + 1) * 512], 1.0,
                                    yrw[:, q * 512:(q + 1) * 512],
                                    op0=ALU.mult, op1=ALU.add),
                                    reads=[R("qeb", qs), R("yrw")], writes=[R("yrw")],
                                    tag="add1")
                            for t in range(16):
                                pr.emit("dve", lambda h, t=t, t0=t0, c=c: h.tensor_scalar(
                                    dxh[:, t * 128:(t + 1) * 128],
                                    xbcz[:, (32 + t0 + t) * MS + c * CH:
                                         (32 + t0 + t) * MS + c * CH + 128],
                                    dcol[:, t0 + t:t0 + t + 1], None, op0=ALU.mult),
                                    reads=[kz(32 + t0 + t), R("c_dcol")],
                                    writes=[R("dxh", t)], tag="dxh")
                            pr.emit("dve", lambda h: h.scalar_tensor_tensor(
                                yrw[:, :], dxh[:, :], 1.0, yrw[:, :],
                                op0=ALU.mult, op1=ALU.add),
                                reads=[R("dxh", t) for t in range(16)] + [R("yrw")],
                                writes=[R("yrw")], tag="add2")
                            zk = [kz(t0 + t) for t in range(16)]
                            pr.emit("dve", lambda h, t0=t0, c=c: h.tensor_mul(
                                ap3(gsl, t0 * MS + c * CH, MS, 16, 1, 128), yrw[:, :],
                                ap3(xbcz, t0 * MS + c * CH, MS, 16, 1, 128)),
                                reads=[R("yrw")] + zk, writes=[R("gsl", m, c, H)],
                                tag="gate")
                            pr.emit("act", lambda h, t0=t0, c=c: h.activation(
                                dxh[:, :],
                                ap3(gsl, t0 * MS + c * CH, MS, 16, 1, 128), AF.Square),
                                reads=[R("gsl", m, c, H)],
                                writes=[R("gsq")] + [R("dxh", t) for t in range(16)],
                                tag="gsq")
                            for t in range(16):
                                pr.emit("pe", lambda h, t=t, H=H: h.matmul(
                                    PA[0:1, 384:512], onescb[:, :],
                                    dxh[:, t * 128:(t + 1) * 128],
                                    start=(H == 0 and t == 0),
                                    stop=(H == 1 and t == 15)),
                                    reads=[R("gsq"), R("dxh", t), R("c_onescb")],
                                    writes=[R("PAs")], tag="ssq")
                            for q in range(4):
                                pr.emit("pe", lambda h, q=q, hp0=hp0, c=c: h.matmul(
                                    PC[:, 0:512],
                                    ap2(ddtbf, c * CH, 1, CH, 0, 64),
                                    rep64[:, hp0 + q * 512:hp0 + (q + 1) * 512],
                                    start=True, stop=True),
                                    reads=[R("ddtbf", m), R("c_rep")],
                                    writes=[R("PC", j) for j in range(4)], tag="dtd")
                                pr.emit("dve", lambda h, q=q, hp0=hp0: h.tensor_mul(
                                    xhd[:, (q % 2) * 512:(q % 2 + 1) * 512],
                                    xhT[:, hp0 + q * 512:hp0 + (q + 1) * 512],
                                    PC[:, 0:512]),
                                    reads=[R("PC", j) for j in range(4)] + xhTk,
                                    writes=[R("xhd", q % 2)], tag="xhd")
                                if c == 0 and H == 0 and q == 0 and m == 0:
                                    pr.emit("dve", lambda h: h.tensor_copy(
                                        cds[:, :], PC[:, 0:512]),
                                        reads=[R("PC", j) for j in range(4)],
                                        writes=[R("cds")], tag="dtddmp")
                                    do_dump("dtd0", lambda: cds[:, :], [R("cds")])
                                    do_dump("bsb0", lambda: bsb[:, :], [R("bsb")])
                                    do_dump("xhd0", lambda: xhd[:, 0:512], [R("xhd", 0)])
                                    do_dump("ddtbf0", lambda: ddtbf[:, :], [R("ddtbf", m)])
                                for j in range(8):
                                    hh = q * 8 + j
                                    pr.emit("pe", lambda h, hh=hh, q=q: h.matmul(
                                        PY[:, hh * 64:(hh + 1) * 64],
                                        bsb[:, :],
                                        xhd[:, (q % 2) * 512 + (hh % 8) * 64:
                                            (q % 2) * 512 + (hh % 8 + 1) * 64],
                                        start=True, stop=True),
                                        reads=[R("bsb"), R("xhd", q % 2)],
                                        writes=[R("PY", hh // 2)], tag="st")
                            for q in range(4):
                                pr.emit("dve", lambda h, q=q, hp0=hp0, c=c: h.tensor_scalar(
                                    cdq[:, :],
                                    rep64[0:64, hp0 + q * 512:hp0 + (q + 1) * 512],
                                    ephF[:, c * CH + 127:c * CH + 128], None,
                                    op0=ALU.mult),
                                    reads=[R("c_rep"), R("eph", m)],
                                    writes=[R("cdq")], tag="cdq")
                                pr.emit("pe", lambda h: h.matmul(
                                    PC[:, 0:512], ones64[:, :], cdq[:, :],
                                    start=True, stop=True),
                                    reads=[R("cdq"), R("c_ones64")],
                                    writes=[R("PC", j) for j in range(4)], tag="cdm")
                                pr.emit("act", lambda h: h.copy(cds[:, :], PC[:, 0:512]),
                                        reads=[R("PC", j) for j in range(4)],
                                        writes=[R("cds")], tag="cde")
                                pr.emit("dve", lambda h, q=q, hp0=hp0: h.tensor_mul(
                                    cmy[:, :],
                                    carry[:, hp0 + q * 512:hp0 + (q + 1) * 512],
                                    cds[:, :]),
                                    reads=[R("cds")] + [R("carry", t0 + q * 4 + j)
                                                        for j in range(4)],
                                    writes=[R("cmy")], tag="cmy")
                                if c == 0 and H == 0 and q == 0 and m == 0:
                                    do_dump("cdq0", lambda: cdq[:, :], [R("cdq")])
                                    do_dump("cds0", lambda: cds[:, :], [R("cds")])
                                    do_dump("cmy0", lambda: cmy[:, :], [R("cmy")])
                                    pr.emit("dve", lambda h: h.tensor_copy(
                                        esb[:, :], PY[:, 0:512]),
                                        reads=[R("PY", j) for j in range(4)],
                                        writes=[R("esb")], tag="stdmp")
                                    do_dump("st0", lambda: esb[:, :], [R("esb")])
                                pr.emit("dve", lambda h, q=q, hp0=hp0: h.tensor_add(
                                    carry[:, hp0 + q * 512:hp0 + (q + 1) * 512],
                                    cmy[:, :], PY[:, q * 512:(q + 1) * 512]),
                                    reads=[R("cmy")] + [R("PY", q * 4 + j)
                                                        for j in range(4)],
                                    writes=[R("carry", t0 + q * 4 + j) for j in range(4)],
                                    tag="cup")
                        pr.emit("dve", lambda h: h.tensor_scalar(
                            srow[0:1, :], PA[0:1, 384:512], 1.0 / HID, EPS,
                            op0=ALU.mult, op1=ALU.add),
                            reads=[R("PAs")], writes=[R("srow")], tag="rs1")
                        pr.emit("act", lambda h: h.activation(
                            srow[0:1, :], srow[0:1, :], AF.Ln),
                            reads=[R("srow")], writes=[R("srow")], tag="rs2")
                        pr.emit("act", lambda h: h.activation(
                            srow[0:1, :], srow[0:1, :], AF.Exp, scale=-0.5),
                            reads=[R("srow")], writes=[R("srow")], tag="rs3")
                        pr.emit("pe", lambda h: h.matmul(
                            PA[:, 256:384], onesf[:, :], srow[0:1, :],
                            start=True, stop=True),
                            reads=[R("srow"), R("c_onesf")], writes=[R("PAr")], tag="rs4")
                        pr.emit("dve", lambda h: h.tensor_copy(rsb[:, :], PA[:, 256:384]),
                                reads=[R("PAr")], writes=[R("rsb")], tag="rs5")
                        for H in range(2):
                            t0 = H * 16
                            pr.emit("dve", lambda h, t0=t0, c=c: h.tensor_mul(
                                ap3(gsl, t0 * MS + c * CH, MS, 16, 1, 128),
                                ap3(gsl, t0 * MS + c * CH, MS, 16, 1, 128),
                                bass.AP(rsb, 0, [[128, 128], [0, 16], [1, 128]])),
                                reads=[R("gsl", m, c, H), R("rsb")],
                                writes=[R("gsl", m, c, H)], tag="gfin")
                        if c == 0 and m == 0:
                            do_dump("xhT", lambda: xhT[:, :], xhTk)
                            do_dump("carry0", lambda: carry[:, :],
                                    [R("carry", t) for t in range(32)])
                        if c == 1 and m == 0:
                            do_dump("rsb1", lambda: rsb[:, :], [R("rsb")])
                        if c == NCH - 1 and m == NMS - 1:
                            do_dump("carry", lambda: carry[:, :],
                                    [R("carry", t) for t in range(32)])
                    do_dump("gsl", lambda: gsl[:, :], gkeys)
                    pr.barrier()
                sx.close()
                if mstage < 3:
                    return

                with contextlib.ExitStack() as s3:
                    wmv = sb(f"wmv_{m}", [128, 2 * 8192], BF16, "left", s3)
                    vsl = sb(f"vsl_{m}", [128, NMS * 2048], F16, "left", s3)
                    xst = sb(f"xs2_{m}", [128, 2048], F32, "left", s3)
                    for fo in range(8):
                        slot = fo % 2
                        wks = [R("wmv", slot, kt) for kt in range(32)]
                        pr.dma("pool", f"wmv{slot}",
                               lambda h, fo=fo, slot=slot: h.dma_start(
                                   wmv[:, slot * 8192:(slot + 1) * 8192],
                                   wout_h[:, fo * 8192:(fo + 1) * 8192]),
                               writes=wks, tag="wol")
                        for kt in range(32):
                            pr.emit("dve", lambda h, slot=slot, kt=kt: h.tensor_scalar(
                                wmv[:, slot * 8192 + kt * 256:slot * 8192 + (kt + 1) * 256],
                                wmv[:, slot * 8192 + kt * 256:slot * 8192 + (kt + 1) * 256],
                                nwt[:, kt:kt + 1], None, op0=ALU.mult),
                                reads=[R("wmv", slot, kt), R("c_nw")],
                                writes=[R("wmv", slot, kt)], tag="wosc")
                        for stt in range(NMS):
                            for kt in range(32):
                                pr.emit("pe", lambda h, kt=kt, stt=stt, slot=slot:
                                        h.matmul(PY[:, stt * 512:stt * 512 + 256],
                                                 gsl[:, kt * MS + stt * 128:
                                                     kt * MS + stt * 128 + 128],
                                                 wmv[:, slot * 8192 + kt * 256:
                                                     slot * 8192 + (kt + 1) * 256],
                                                 start=(kt == 0), stop=(kt == 31)),
                                        reads=[R("wmv", slot, kt)] + gkeys,
                                        writes=[R("PY", stt * 4), R("PY", stt * 4 + 1)],
                                        tag="opm")
                            pr.emit("dve", lambda h, stt=stt, fo=fo: h.tensor_copy(
                                vsl[:, stt * 2048 + fo * 256:stt * 2048 + (fo + 1) * 256],
                                PY[:, stt * 512:stt * 512 + 256]),
                                reads=[R("PY", stt * 4), R("PY", stt * 4 + 1)],
                                writes=[R("vsl", m, stt, fo)], tag="opev")
                    pr.barrier()
                    for stt in range(NMS):
                        row0 = tok0 + stt * 128
                        vk = [R("vsl", m, stt, fo) for fo in range(8)]
                        pr.dma("sp", "xld",
                               lambda h, r=row0: h.dma_start(xst[:, :], x_h[r:r + 128, :]),
                               writes=[xkey], tag="xld2")
                        rsd, nmr, skeys = ln_stats(
                            lambda stt=stt: vsl[:, stt * 2048:(stt + 1) * 2048], vk,
                            lambda: lntmp[:, :], R("lntmp"))
                        pr.emit("dve", lambda h, r=rsd, b=nmr, stt=stt: h.tensor_scalar(
                            lntmp[:, :], vsl[:, stt * 2048:(stt + 1) * 2048], r, b,
                            op0=ALU.mult, op1=ALU.add),
                            reads=vk + skeys, writes=[R("lntmp")], tag="ln2n")
                        gi = (m * NMS + stt) % 8
                        pr.emit("dve", lambda h, gi=gi: h.tensor_add(
                            x2sl[:, gi * 2048:(gi + 1) * 2048], lntmp[:, :], xst[:, :]),
                            reads=[R("lntmp"), xkey], writes=[R("x2", gi)], tag="x2a")
                    do_dump("vsl", lambda: vsl[:, :],
                            [R("vsl", m, stt, fo) for stt in range(NMS) for fo in range(8)])
                    pr.barrier()

        def ffn_slab(f):
            tok0 = f * FS
            with contextlib.ExitStack() as fsk:
                fh = fsk.enter_context(contextlib.ExitStack())
                h1T = sb(f"h1T_{f}", [128, 32 * FS], BF16, "left", fh)
                h1k = [R("h1T", f, cc) for cc in range(32)]
                with contextlib.ExitStack() as f1:
                    ln3T = sb(f"ln3T_{f}", [128, 16 * FS], BF16, "left", f1)
                    wst = [sb(f"fw{i}_{f}", [128, 2048], BF16, "left", f1)
                           for i in range(2)]
                    lk = []
                    for stt in range(8):
                        x2k = [R("x2", stt)]
                        rsd, nmr, skeys = ln_stats(
                            lambda stt=stt: x2sl[:, stt * 2048:(stt + 1) * 2048], x2k,
                            lambda: lntmp[:, :], R("lntmp"))
                        pr.emit("dve", lambda h, r=rsd, b=nmr, stt=stt: h.tensor_scalar(
                            lntmp[:, :], x2sl[:, stt * 2048:(stt + 1) * 2048], r, b,
                            op0=ALU.mult, op1=ALU.add),
                            reads=x2k + skeys, writes=[R("lntmp")], tag="ln3n")
                        for g in range(4):
                            for j in range(4):
                                kt = g * 4 + j
                                pr.emit("pe", lambda h, kt=kt, j=j: h.transpose(
                                    PT[:, j * 128:(j + 1) * 128],
                                    lntmp[:, kt * 128:(kt + 1) * 128], idbf[:, :]),
                                    reads=[R("lntmp"), R("c_idbf")],
                                    writes=[R("PT", j)], tag="ln3t")
                            pr.emit("dve", lambda h, g=g, stt=stt: h.tensor_copy(
                                ap3(ln3T, g * 4 * FS + stt * 128, FS, 4, 1, 128),
                                PT[:, 0:512]),
                                reads=[R("PT", j) for j in range(4)],
                                writes=[R("ln3T", f, g, stt)], tag="ln3e")
                        lk += [R("ln3T", f, g, stt) for g in range(4)]
                    pr.barrier()
                    for cc in range(32):
                        slot = cc % 2
                        pr.dma("pool", f"fw{slot}",
                               lambda h, cc=cc, slot=slot: h.dma_start(
                                   wst[slot][:, :], w1_h[:, cc * 2048:(cc + 1) * 2048]),
                               writes=[R("fw", slot)], tag="w1l")
                        for ff in range(2):
                            pn, off = PSL[(cc * 2 + ff) % 7]
                            pt = psum_of(pn)
                            pk = pskeys(pn, off)
                            for kt in range(16):
                                pr.emit("pe", lambda h, pt=pt, off=off, kt=kt, ff=ff, sl=slot:
                                        h.matmul(pt[:, off:off + 512],
                                                 wst[sl][:, kt * 128:(kt + 1) * 128],
                                                 ln3T[:, kt * FS + ff * 512:
                                                      kt * FS + ff * 512 + 512],
                                                 start=(kt == 0), stop=(kt == 15)),
                                        reads=[R("fw", slot)] + lk, writes=pk, tag="h1m")
                            pr.emit("act", lambda h, pt=pt, off=off, cc=cc, ff=ff:
                                    h.activation(
                                        h1T[:, cc * FS + ff * 512:cc * FS + ff * 512 + 512],
                                        pt[:, off:off + 512], AF.Relu,
                                        bias=b1t[:, cc:cc + 1]),
                                    reads=pk + [R("c_b1")], writes=[R("h1T", f, cc)],
                                    tag="h1e")
                    do_dump("h1T", lambda: h1T[:, :], h1k)
                    pr.barrier()
                h2T = sb(f"h2T_{f}", [128, 32 * FS], BF16, side="right", stk=fsk)
                h2k = [R("h2T", f, cc) for cc in range(32)]
                with contextlib.ExitStack() as f2:
                    wst2 = [sb(f"f2w{i}_{f}", [128, 4096], BF16, "left", f2)
                            for i in range(2)]
                    for cc in range(32):
                        slot = cc % 2
                        pr.dma("pool", f"f2w{slot}",
                               lambda h, cc=cc, slot=slot: h.dma_start(
                                   wst2[slot][:, :], w2_h[:, cc * 4096:(cc + 1) * 4096]),
                               writes=[R("f2w", slot)], tag="w2l")
                        for ff in range(2):
                            pn, off = PSL[(cc * 2 + ff) % 7]
                            pt = psum_of(pn)
                            pk = pskeys(pn, off)
                            for kt in range(32):
                                pr.emit("pe", lambda h, pt=pt, off=off, kt=kt, ff=ff, sl=slot:
                                        h.matmul(pt[:, off:off + 512],
                                                 wst2[sl][:, kt * 128:(kt + 1) * 128],
                                                 h1T[:, kt * FS + ff * 512:
                                                      kt * FS + ff * 512 + 512],
                                                 start=(kt == 0), stop=(kt == 31)),
                                        reads=[R("f2w", slot)] + h1k, writes=pk, tag="h2m")
                            pr.emit("act", lambda h, pt=pt, off=off, cc=cc, ff=ff:
                                    h.activation(
                                        h2T[:, cc * FS + ff * 512:cc * FS + ff * 512 + 512],
                                        pt[:, off:off + 512], AF.Relu,
                                        bias=b2t[:, cc:cc + 1]),
                                    reads=pk + [R("c_b2")], writes=[R("h2T", f, cc)],
                                    tag="h2e")
                    pr.barrier()
                fh.close()
                with contextlib.ExitStack() as f3:
                    wmv = sb(f"w3m_{f}", [128, 2 * 8192], BF16, "left", f3)
                    fsl = sb(f"fsl_{f}", [128, 8 * 2048], BF16, "left", f3)
                    b3m = sb(f"b3m_{f}", [128, 2048], BF16, "left", f3)
                    b3row = sb(f"b3row_{f}", [1, DM], F32, "left", f3)
                    ost = sb(f"ost_{f}", [128, 2048], F32, "left", f3)
                    pr.dma("sp", "c_b3", lambda h: h.dma_start(b3row[:, :], b3_h),
                           writes=[R("c_b3")], tag="b3l")
                    for q in range(4):
                        pr.emit("pe", lambda h, q=q: h.matmul(
                            PA[:, 0:512], onesf[:, :],
                            b3row[0:1, q * 512:(q + 1) * 512], start=True, stop=True),
                            reads=[R("c_b3"), R("c_onesf")], writes=[R("PAF")], tag="b3b")
                        pr.emit("dve", lambda h, q=q: h.tensor_copy(
                            b3m[:, q * 512:(q + 1) * 512], PA[:, 0:512]),
                            reads=[R("PAF")], writes=[R("b3m", q)], tag="b3e")
                    for fo in range(8):
                        slot = fo % 2
                        pr.dma("pool", f"w3m{slot}",
                               lambda h, fo=fo, slot=slot: h.dma_start(
                                   wmv[:, slot * 8192:(slot + 1) * 8192],
                                   w3_h[:, fo * 8192:(fo + 1) * 8192]),
                               writes=[R("w3m", slot)], tag="w3l")
                        for stt in range(8):
                            pn, off = [("PY", 0), ("PY", 512), ("PY", 1024),
                                       ("PY", 1536), ("PB", 0), ("PC", 0)][stt % 6]
                            pt = psum_of(pn)
                            pk = pskeys(pn, off, 256)
                            for kt in range(32):
                                pr.emit("pe", lambda h, pt=pt, off=off, kt=kt, stt=stt,
                                        sl=slot: h.matmul(
                                            pt[:, off:off + 256],
                                            h2T[:, kt * FS + stt * 128:
                                                 kt * FS + stt * 128 + 128],
                                            wmv[:, sl * 8192 + kt * 256:
                                                sl * 8192 + (kt + 1) * 256],
                                            start=(kt == 0), stop=(kt == 31)),
                                        reads=[R("w3m", slot)] + h2k, writes=pk, tag="w3m")
                            pr.emit("dve", lambda h, pt=pt, off=off, stt=stt, fo=fo:
                                    h.tensor_add(
                                        fsl[:, stt * 2048 + fo * 256:
                                            stt * 2048 + (fo + 1) * 256],
                                        pt[:, off:off + 256],
                                        b3m[:, fo * 256:(fo + 1) * 256]),
                                    reads=pk + [R("b3m", fo // 2)],
                                    writes=[R("fsl", f, stt, fo)], tag="w3e")
                    do_dump("fsl", lambda: fsl[:, :],
                            [R("fsl", f, stt, fo) for stt in range(8) for fo in range(8)])
                    pr.barrier()
                    for stt in range(8):
                        fk = [R("fsl", f, stt, fo) for fo in range(8)]
                        rsd, nmr, skeys = ln_stats(
                            lambda stt=stt: fsl[:, stt * 2048:(stt + 1) * 2048], fk,
                            lambda: lntmp[:, :], R("lntmp"))
                        pr.emit("dve", lambda h, r=rsd, b=nmr, stt=stt: h.tensor_scalar(
                            ost[:, :], fsl[:, stt * 2048:(stt + 1) * 2048], r, b,
                            op0=ALU.mult, op1=ALU.add),
                            reads=fk + skeys, writes=[R("ost")], tag="ln4n")
                        pr.emit("dve", lambda h, stt=stt: h.scalar_tensor_tensor(
                            ost[:, :], x2sl[:, stt * 2048:(stt + 1) * 2048], 1.0,
                            ost[:, :], op0=ALU.mult, op1=ALU.add),
                            reads=[R("x2", stt), R("ost")], writes=[R("ost")], tag="resf")
                        row0 = tok0 + stt * 128
                        pr.dma("sp", "outst",
                               lambda h, r=row0: h.dma_start(out_h[r:r + 128, :],
                                                             ost[:, :]),
                               reads=[R("ost")], tag="outd")
                    for e_ in ENGINES:
                        pr.emit(e_, lambda h: h.nop(), writes=[R("ost")], tag="odrain")
                    pr.barrier()

        if stage <= 3:
            mixer_slab(0, stage)
        else:
            for f in range(NFS):
                for sub in range(2):
                    mixer_slab(f * 2 + sub, 9)
                ffn_slab(f)
                do_dump("x2", lambda: x2sl[:, :], [R("x2", j) for j in range(8)])
        dsem = pr.finalize(st)
        if dsem:
            with nc.Block() as blk:
                def _fin(h):
                    for s_, c_ in dsem.values():
                        h.wait_ge(s_, c_)
                blk.sync(_fin)
    return nc, dump_t


def _rsel():
    r = np.zeros((64, 32 * 128), np.float32)
    for j in range(32):
        r[j, j * 128:(j + 1) * 128] = 1.0
    return r.astype(BF)


def _host_prep(inputs):
    ip = np.ascontiguousarray
    W = inputs["L0_attn_in_proj_w"]
    main = W[:8448].reshape(66, 128, 16, 128).transpose(3, 0, 2, 1).reshape(128, -1)
    dtw = W[8448:].reshape(64, 16, 128).transpose(2, 1, 0).reshape(128, -1)
    per = {
        "wlay": ip(np.concatenate([main, dtw], axis=1)),
        "wout": ip(inputs["L0_attn_out_proj_w"].reshape(8, 256, 32, 128)
                   .transpose(3, 0, 2, 1).reshape(128, -1)),
        "w1": ip(inputs["L0_ffn_W1"].reshape(32, 128, 16, 128)
                 .transpose(3, 0, 2, 1).reshape(128, -1)),
        "w2": ip(inputs["L0_ffn_W2"].reshape(32, 128, 32, 128)
                 .transpose(3, 0, 2, 1).reshape(128, -1)),
        "w3": ip(inputs["L0_ffn_W3"].reshape(8, 256, 32, 128)
                 .transpose(3, 0, 2, 1).reshape(128, -1)),
        "cwt": ip(inputs["L0_attn_conv_w"].reshape(34, 128, 4)
                  .transpose(1, 0, 2).reshape(128, -1)),
        "b1t": ip(inputs["L0_ffn_b1"].reshape(32, 128).T),
        "b2t": ip(inputs["L0_ffn_b2"].reshape(32, 128).T),
        "b3r": ip(inputs["L0_ffn_b3"].reshape(1, -1)),
        "nwt": ip(inputs["L0_attn_norm_w"].reshape(32, 128).T),
        "dtb": ip(inputs["L0_attn_dt_bias"].reshape(64, 1)),
        "alog": ip(inputs["L0_attn_A_log"].reshape(64, 1)),
        "idbf": np.eye(128, dtype=BF),
        "idf": np.eye(128, dtype=np.float32),
        "triu": np.triu(np.ones((128, 128), np.float32)).astype(BF),
        "rep64": np.kron(np.eye(64, dtype=np.float32),
                         np.ones((1, 64), np.float32)).astype(BF),
        "rsel": _rsel(),
        "onesf": np.ones((1, 128), np.float32),
        "onescb": np.ones((128, 1), BF),
        "ones64": np.ones((64, 128), BF),
    }
    D = inputs["L0_attn_D"]
    dc = np.empty((128, 32), np.float32)
    dc[:64, :] = D[0::2][None, :]
    dc[64:, :] = D[1::2][None, :]
    per["dcol"] = dc
    return per


def kernel(**inputs):
    inputs = {k: np.asarray(v) for k, v in inputs.items()}
    per = _host_prep(inputs)
    nc, _ = build(bass.Bass("TRN2", target_bir_lowering=False))
    x = inputs["L0_attn_x"]
    in_maps = []
    for b in range(8):
        d = dict(per)
        d["x"] = np.ascontiguousarray(x[b])
        in_maps.append(d)
    res = run_bass_kernel_spmd(nc, in_maps, core_ids=list(range(8)))
    return np.stack([r["out"] for r in res.results]).astype(np.float32)


# version 11
# speedup vs baseline: 1.1727x; 1.0877x over previous
import contextlib
import numpy as np
import ml_dtypes
import concourse.bass as bass
import concourse.mybir as mybir
from concourse.bass_utils import run_bass_kernel_spmd

F32 = mybir.dt.float32
BF16 = mybir.dt.bfloat16
F16 = mybir.dt.float16
AF = mybir.ActivationFunctionType
ALU = mybir.AluOpType
BF = ml_dtypes.bfloat16

S, DM, HID, NST, HEADS, HP = 2048, 2048, 4096, 128, 64, 64
MS = 512
NMS = S // MS
FS = 1024
NFS = S // FS
CH = 128
NCH = MS // CH
NCC = 66
CLAMP = 30.0
EPS = 1e-5

ENGINES = ("pe", "dve", "act", "pool", "sp")
ENG_ATTR = {"pe": "tensor", "dve": "vector", "act": "scalar",
            "pool": "gpsimd", "sp": "sync"}


class Prog:

    PSBANK = {}

    def __init__(self, nc):
        self.nc = nc
        self.ops = []
        self.stream = {e: [] for e in ENGINES}
        self.last_writer = {}
        self.readers = {}
        self.bankw = {}
        self.epoch = 0

    def emit(self, eng, fn, reads=(), writes=(), tag=""):
        idx = len(self.ops)
        deps = set()
        for r in reads:
            w = self.last_writer.get(r)
            if w is not None:
                deps.add(w)
        for w in writes:
            lw = self.last_writer.get(w)
            if lw is not None:
                deps.add(lw)
            deps.update(self.readers.get(w, ()))
        for w in list(writes) + list(reads):
            bank = Prog.PSBANK.get(w[:2] if len(w) >= 2 else w)
            if bank is not None:
                bw = self.bankw.setdefault(bank, {})
                for e2, i2 in bw.items():
                    if e2 != eng:
                        deps.add(i2)
                bw[eng] = idx
        deps.discard(idx)
        self.ops.append(dict(idx=idx, eng=eng, fn=fn, deps=sorted(deps),
                             epoch=self.epoch, need_inc=False, dma=None, tag=tag))
        self.stream[eng].append(idx)
        for r in reads:
            self.readers.setdefault(r, []).append(idx)
        for w in writes:
            self.last_writer[w] = idx
            self.readers[w] = []
        return idx

    def dma(self, eng, slot, fn, reads=(), writes=(), tag=""):
        idx = self.emit(eng, fn, reads=reads, writes=writes, tag=tag)
        self.ops[idx]["dma"] = slot
        return idx

    def barrier(self):
        self.epoch += 1

    def finalize(self, stack):
        nc = self.nc
        eng_sem = {e: stack.enter_context(nc.semaphore(f"es_{e}")) for e in ENGINES}
        slots = sorted({op["dma"] for op in self.ops if op["dma"] is not None})
        dsem = {s: [stack.enter_context(nc.semaphore(f"ds_{s}")), 0] for s in slots}
        for op in self.ops:
            for d in op["deps"]:
                dop = self.ops[d]
                if dop["dma"] is None and not (dop["eng"] == "pe" and op["eng"] == "pe"):
                    dop["need_inc"] = True
        cnt = {e: 0 for e in ENGINES}
        for e in ENGINES:
            for i in self.stream[e]:
                op = self.ops[i]
                if op["dma"] is not None:
                    dsem[op["dma"]][1] += 16
                    op["sem"], op["semval"], op["incby"] = dsem[op["dma"]][0], dsem[op["dma"]][1], 16
                elif op["need_inc"]:
                    cnt[e] += 1
                    op["sem"], op["semval"], op["incby"] = eng_sem[e], cnt[e], 1
                else:
                    op["sem"] = None
        for e in ENGINES:
            cov = {}
            for i in self.stream[e]:
                op = self.ops[i]
                wmax = {}
                for d in op["deps"]:
                    dop = self.ops[d]
                    if dop["dma"] is None:
                        if dop["eng"] == "pe" and e == "pe":
                            continue
                        if dop["epoch"] < op["epoch"]:
                            continue
                    sem, val = dop["sem"], dop["semval"]
                    k = id(sem)
                    if cov.get(k, 0) >= val:
                        continue
                    if k not in wmax or wmax[k][1] < val:
                        wmax[k] = (sem, val)
                for k, (sem, val) in wmax.items():
                    cov[k] = max(cov.get(k, 0), val)
                op["waits"] = list(wmax.values())
        by_ep = {}
        for e in ENGINES:
            for i in self.stream[e]:
                by_ep.setdefault((self.ops[i]["epoch"], e), []).append(i)
        for ep in range(self.epoch + 1):
            engs = [e for e in ENGINES if (ep, e) in by_ep]
            if not engs:
                continue
            with nc.Block() as blk:
                for e in engs:
                    idxs = by_ep[(ep, e)]

                    def body(h, idxs=idxs):
                        for i in idxs:
                            op = self.ops[i]
                            for sem, val in op["waits"]:
                                h.wait_ge(sem, val)
                            inst = op["fn"](h)
                            if op["sem"] is not None:
                                inst.then_inc(op["sem"], op["incby"])
                    getattr(blk, ENG_ATTR[e])(body)
        return dsem


def ap3(t, base, s1, n1, s2, n2, p0=0, pn=128):
    rl = t.shape[1]
    return bass.AP(t, p0 * rl + base, [[rl, pn], [s1, n1], [s2, n2]])


def ap2(t, base, s1, n1, p0, pn):
    rl = t.shape[1]
    return bass.AP(t, p0 * rl + base, [[rl, pn], [s1, n1]])


def build(nc, dumps=(), stage=9):
    pr = Prog(nc)
    dumps = set(dumps)
    dump_t = {}
    R = lambda *a: a

    def dram(name, shape, dtype, kind):
        return nc.dram_tensor(name, shape, dtype, kind=kind).ap()

    x_h = dram("x", [S, DM], F32, "ExternalInput")
    out_h = dram("out", [S, DM], F32, "ExternalOutput")
    wlay_h = dram("wlay", [128, NCC * 2048 + 1024], F32, "ExternalInput")
    wout_h = dram("wout", [128, 8 * 8192], F32, "ExternalInput")
    w1_h = dram("w1", [128, 32 * 2048], F32, "ExternalInput")
    w2_h = dram("w2", [128, 32 * 4096], F32, "ExternalInput")
    w3_h = dram("w3", [128, 8 * 8192], F32, "ExternalInput")
    cwt_h = dram("cwt", [128, 136], F32, "ExternalInput")
    b1_h = dram("b1t", [128, 32], F32, "ExternalInput")
    b2_h = dram("b2t", [128, 32], F32, "ExternalInput")
    b3_h = dram("b3r", [1, DM], F32, "ExternalInput")
    nw_h = dram("nwt", [128, 32], F32, "ExternalInput")
    dcol_h = dram("dcol", [128, 32], F32, "ExternalInput")
    dtb_h = dram("dtb", [64, 1], F32, "ExternalInput")
    alog_h = dram("alog", [64, 1], F32, "ExternalInput")
    idbf_h = dram("idbf", [128, 128], BF16, "ExternalInput")
    idf_h = dram("idf", [128, 128], F32, "ExternalInput")
    triu_h = dram("triu", [128, 128], BF16, "ExternalInput")
    rep_h = dram("rep64", [64, 4096], BF16, "ExternalInput")
    rsel_h = dram("rsel", [64, 4096], BF16, "ExternalInput")
    onesf_h = dram("onesf", [1, 128], F32, "ExternalInput")
    onescb_h = dram("onescb", [128, 1], BF16, "ExternalInput")
    ones64_h = dram("ones64", [64, 128], BF16, "ExternalInput")

    DUMP_SHAPES = {
        "xbcz": ([128, NCC * MS], BF16), "dt": ([64, MS], F32),
        "dAc": ([64, MS], F32), "gsl": ([128, 32 * MS], BF16),
        "vsl": ([128, NMS * 2048], F16), "x2": ([128, 8 * 2048], BF16),
        "h1T": ([128, 32 * FS], BF16), "fsl": ([128, 8 * 2048], BF16),
        "xhT": ([128, 4096], BF16), "carry": ([128, 4096], BF16),
        "yrw1": ([128, 2048], BF16), "rsb1": ([128, 128], F32),
        "cds0": ([128, 512], BF16), "cmy0": ([128, 512], BF16),
        "cdq0": ([64, 512], BF16), "st0": ([128, 512], BF16),
        "bsb0": ([128, 128], BF16), "xhd0": ([128, 512], BF16), "dtd0": ([128, 512], BF16),
        "ddtbf0": ([64, 512], BF16),
        "carry0": ([128, 4096], BF16), "esb1": ([128, 512], BF16),
    }
    for nm in dumps:
        shp, dt_ = DUMP_SHAPES[nm]
        dump_t[nm] = dram("dump_" + nm, shp, dt_, "ExternalOutput")

    def do_dump(name, src_fn, reads):
        t = dump_t.get(name)
        if t is not None:
            pr.dma("sp", "dmp_" + name, lambda h, t=t: h.dma_start(t, src_fn()),
                   reads=list(reads), tag="dump")

    st = contextlib.ExitStack()
    with st:
        def sb(n, shape, dt_, side="left", stk=st):
            return stk.enter_context(nc.sbuf_tensor("s_" + n, shape, dt_, side=side))

        PY = st.enter_context(nc.psum_tensor("PY", [128, 2048], F32))
        PA = st.enter_context(nc.psum_tensor("PA", [128, 512], F32))
        PB = st.enter_context(nc.psum_tensor("PB", [128, 512], F32))
        PC = st.enter_context(nc.psum_tensor("PC", [128, 512], F32))
        PT = st.enter_context(nc.psum_tensor("PT", [128, 1024], BF16))
        for _t in range(16):
            Prog.PSBANK[("PY", _t)] = _t // 4
        Prog.PSBANK[("PAa",)] = 4
        Prog.PSBANK[("PAs",)] = 4
        Prog.PSBANK[("PAr",)] = 4
        Prog.PSBANK[("PAF",)] = 4
        for _j in range(4):
            Prog.PSBANK[("PB", _j)] = 5
            Prog.PSBANK[("PC", _j)] = 6
        for _j in range(8):
            Prog.PSBANK[("PT", _j)] = 7

        idbf = sb("idbf", [128, 128], BF16)
        idf = sb("idf", [128, 128], F32)
        triu = sb("triu", [128, 128], BF16)
        onesf = sb("onesf", [1, 128], F32)
        onescb = sb("onescb", [128, 1], BF16)
        ones64 = sb("ones64", [64, 128], BF16)
        cwt = sb("cwt", [128, 136], F32)
        b1t = sb("b1t", [128, 32], F32)
        b2t = sb("b2t", [128, 32], F32)
        nwt = sb("nwt", [128, 32], F32)
        dcol = sb("dcol", [128, 32], F32)
        dtb = sb("dtb", [64, 1], F32)
        alog = sb("alog", [64, 1], F32)
        acol = sb("acol", [64, 1], F32)
        carry = sb("carry", [128, 4096], BF16)
        tails = sb("tails", [128, 102], F32)
        lntmp = sb("lntmp", [128, 2048], BF16)
        stat = sb("stat", [128, 16], F32)
        srow = sb("srow", [1, 128], F32)
        x2sl = sb("x2sl", [128, 8 * 2048], BF16, side="right")

        def cload(dst, src, slot):
            pr.dma("sp", slot, lambda h, d=dst, s=src: h.dma_start(d[:, :], s),
                   writes=[R(slot)], tag="cload")
        cload(idbf, idbf_h, "c_idbf"); cload(idf, idf_h, "c_idf")
        cload(triu, triu_h, "c_triu")
        cload(onesf, onesf_h, "c_onesf")
        cload(onescb, onescb_h, "c_onescb"); cload(ones64, ones64_h, "c_ones64")
        cload(cwt, cwt_h, "c_cwt"); cload(b1t, b1_h, "c_b1")
        cload(b2t, b2_h, "c_b2"); cload(nwt, nw_h, "c_nw")
        cload(dcol, dcol_h, "c_dcol"); cload(dtb, dtb_h, "c_dtb")
        cload(alog, alog_h, "c_alog")
        pr.emit("act", lambda h: h.activation(acol[:, :], alog[:, :], AF.Exp),
                reads=[R("c_alog")], writes=[R("acol")])
        pr.emit("dve", lambda h: h.tensor_scalar_mul(acol[:, :], acol[:, :], -1.0),
                reads=[R("acol")], writes=[R("acol")])
        pr.emit("dve", lambda h: h.memset(tails[:, :], 0.0),
                writes=[R("tails", c) for c in range(34)])
        pr.emit("dve", lambda h: h.memset(carry[:, :], 0.0),
                writes=[R("carry", t) for t in range(32)])
        pr.barrier()

        ln_ctr = [0]

        def ln_stats(src_fn, src_keys, scrap_fn, scrap_key):
            k = (ln_ctr[0] % 2) * 5
            ln_ctr[0] += 1
            ssum, ssq, va, rsd, mm_ = k, k + 1, k + 2, k + 3, k + 4
            K = lambda j: R("stat", j)
            pr.emit("dve", lambda h: h.tensor_scalar(
                scrap_fn(), src_fn(), 1.0, 0.0, op0=ALU.mult, op1=ALU.add,
                accum_out=stat[:, ssum:ssum + 1]),
                reads=list(src_keys), writes=[scrap_key, K(ssum)], tag="lns")
            pr.emit("act", lambda h: h.activation(
                scrap_fn(), src_fn(), AF.Square, accum_out=stat[:, ssq:ssq + 1]),
                reads=list(src_keys) + [scrap_key], writes=[scrap_key, K(ssq)], tag="lnq")
            N = 2048.0
            pr.emit("dve", lambda h: h.tensor_scalar(
                stat[:, va:va + 1], stat[:, ssq:ssq + 1], 1.0 / N, EPS,
                op0=ALU.mult, op1=ALU.add), reads=[K(ssq)], writes=[K(va)])
            pr.emit("dve", lambda h: h.tensor_scalar_mul(
                stat[:, mm_:mm_ + 1], stat[:, ssum:ssum + 1], 1.0 / N),
                reads=[K(ssum)], writes=[K(mm_)])
            pr.emit("dve", lambda h: h.tensor_scalar_mul(
                stat[:, ssum:ssum + 1], stat[:, ssum:ssum + 1], -1.0 / N),
                reads=[K(ssum)], writes=[K(ssum)])
            pr.emit("dve", lambda h: h.scalar_tensor_tensor(
                stat[:, va:va + 1], stat[:, mm_:mm_ + 1], stat[:, ssum:ssum + 1],
                stat[:, va:va + 1], op0=ALU.mult, op1=ALU.add),
                reads=[K(mm_), K(ssum), K(va)], writes=[K(va)])
            pr.emit("act", lambda h: h.activation(
                stat[:, rsd:rsd + 1], stat[:, va:va + 1], AF.Ln),
                reads=[K(va)], writes=[K(rsd)])
            pr.emit("act", lambda h: h.activation(
                stat[:, rsd:rsd + 1], stat[:, rsd:rsd + 1], AF.Exp, scale=-0.5),
                reads=[K(rsd)], writes=[K(rsd)])
            pr.emit("dve", lambda h: h.tensor_scalar(
                stat[:, ssum:ssum + 1], stat[:, ssum:ssum + 1],
                stat[:, rsd:rsd + 1], None, op0=ALU.mult),
                reads=[K(ssum), K(rsd)], writes=[K(ssum)])
            return stat[:, rsd:rsd + 1], stat[:, ssum:ssum + 1], [K(rsd), K(ssum)]

        PSL = [("PY", 0), ("PY", 512), ("PY", 1024), ("PY", 1536),
               ("PA", 0), ("PB", 0), ("PC", 0)]

        def psum_of(name):
            return {"PY": PY, "PA": PA, "PB": PB, "PC": PC}[name]

        def pskeys(name, off, width=512):
            if name == "PY":
                return [R("PY", off // 128 + j) for j in range(width // 128)]
            if name == "PA":
                return [R("PAF")]
            return [R(name, off // 128 + j) for j in range(width // 128)]

        def mixer_slab(m, mstage):
            tok0 = m * MS
            ms = contextlib.ExitStack()
            with ms:
                msb = lambda n, s_, d, side="left": sb(f"{n}_{m}", s_, d, side, ms)
                dtF = msb("dtF", [64, MS], F32)
                dAcF = msb("dAcF", [64, MS], F32)
                colBF = msb("colBF", [64, MS], F32)
                ddtF = msb("ddtF", [64, MS], F32)
                zer64 = msb("zer64", [64, 128], F32)
                ddtbf = msb("ddtbf", [64, MS], BF16)
                dachi = msb("dachi", [128, MS], BF16)
                daclo = msb("daclo", [128, MS], BF16)
                dachi2 = msb("dachi2", [64, MS], BF16)
                daclo2 = msb("daclo2", [64, MS], BF16)
                ephbf = msb("ephbf", [64, MS], BF16)
                ephF = msb("ephF", [64, MS], F32)
                gsl = msb("gsl", [128, 32 * MS], BF16, side="right")
                kz = lambda cc: R("xbcz", m, cc)
                xkey, gkeys = R("xst", m), [R("gsl", m, c, H)
                                           for c in range(NCH) for H in range(2)]

                sx = ms.enter_context(contextlib.ExitStack())
                xbcz = sb(f"xbcz_{m}", [128, NCC * MS], BF16, "left", sx)

                with contextlib.ExitStack() as s1:
                    ln1T = sb(f"ln1T_{m}", [128, 16 * MS], BF16, "left", s1)
                    wst = [sb(f"wst{i}_{m}", [128, 2048], BF16, "left", s1) for i in range(2)]
                    rawx = sb(f"rawx_{m}", [128, 1032], F32, "left", s1)
                    cvt = [sb(f"cvt{i}_{m}", [128, 512], F32, "left", s1) for i in range(2)]
                    xst = sb(f"xst_{m}", [128, 2048], F32, "left", s1)
                    lnkeys = []
                    for stt in range(NMS):
                        row0 = tok0 + stt * 128
                        pr.dma("sp", "xld",
                               lambda h, r=row0: h.dma_start(xst[:, :], x_h[r:r + 128, :]),
                               writes=[xkey], tag="xld")
                        rsd, nmr, skeys = ln_stats(lambda: xst[:, :], [xkey],
                                                   lambda: lntmp[:, :], R("lntmp"))
                        pr.emit("dve", lambda h, r=rsd, b=nmr: h.tensor_scalar(
                            lntmp[:, :], xst[:, :], r, b, op0=ALU.mult, op1=ALU.add),
                            reads=[xkey] + skeys, writes=[R("lntmp")], tag="ln1n")
                        for g in range(4):
                            for j in range(4):
                                kt = g * 4 + j
                                pr.emit("pe", lambda h, kt=kt, j=j: h.transpose(
                                    PT[:, j * 128:(j + 1) * 128],
                                    lntmp[:, kt * 128:(kt + 1) * 128], idbf[:, :]),
                                    reads=[R("lntmp"), R("c_idbf")],
                                    writes=[R("PT", j)], tag="ln1t")
                            pr.emit("dve", lambda h, g=g, stt=stt: h.tensor_copy(
                                ap3(ln1T, g * 4 * MS + stt * 128, MS, 4, 1, 128),
                                PT[:, 0:512]),
                                reads=[R("PT", j) for j in range(4)],
                                writes=[R("ln1T", m, g, stt)], tag="ln1e")
                        lnkeys += [R("ln1T", m, g, stt) for g in range(4)]
                    pr.barrier()

                    for cc in range(NCC + 1):
                        slot = cc % 2
                        if cc < NCC:
                            src = wlay_h[:, cc * 2048:(cc + 1) * 2048]
                            dst = wst[slot][:, 0:2048]
                        else:
                            src = wlay_h[:, NCC * 2048:NCC * 2048 + 1024]
                            dst = wst[slot][:, 0:1024]
                        pr.dma("pool", f"wst{slot}",
                               lambda h, d=dst, s=src: h.dma_start(d, s),
                               writes=[R("wst", slot)], tag=f"wld{cc}")
                        pn, off = PSL[cc % 7]
                        pt = psum_of(pn)
                        pkeys = pskeys(pn, off)
                        wid = 64 if cc == NCC else 128
                        for kt in range(16):
                            pr.emit("pe", lambda h, kt=kt, pt=pt, off=off, w=wid, sl=slot:
                                    h.matmul(pt[0:w, off:off + 512],
                                             wst[sl][:, kt * w:(kt + 1) * w],
                                             ln1T[:, kt * MS:(kt + 1) * MS],
                                             start=(kt == 0), stop=(kt == 15)),
                                    reads=[R("wst", slot)] + lnkeys,
                                    writes=pkeys, tag=f"ip{cc}")
                        if cc < 32:
                            pr.emit("act", lambda h, pt=pt, off=off, cc=cc: h.activation(
                                xbcz[:, cc * MS:(cc + 1) * MS], pt[:, off:off + 512],
                                AF.Silu), reads=pkeys, writes=[kz(cc)], tag="zev")
                        elif cc < NCC:
                            ccv = cc - 32
                            par = ccv % 2
                            rx0 = par * 516
                            cv = cvt[par]
                            ck = R("cvt", par)
                            rk0, rk1 = R("rawx", par, 0), R("rawx", par, 1)
                            pr.emit("dve", lambda h, ccv=ccv, rx0=rx0: h.tensor_copy(
                                rawx[:, rx0:rx0 + 3], tails[:, ccv * 3:ccv * 3 + 3]),
                                reads=[R("tails", ccv)], writes=[rk0], tag="cvt0")
                            pr.emit("act", lambda h, pt=pt, off=off, rx0=rx0: h.copy(
                                rawx[:, rx0 + 3:rx0 + 515], pt[:, off:off + 512]),
                                reads=pkeys, writes=[rk1], tag="cvtr")
                            pr.emit("dve", lambda h, ccv=ccv, rx0=rx0: h.tensor_copy(
                                tails[:, ccv * 3:ccv * 3 + 3], rawx[:, rx0 + 512:rx0 + 515]),
                                reads=[rk1], writes=[R("tails", ccv)], tag="cvnt")
                            w = lambda j, ccv=ccv: cwt[:, ccv * 4 + j:ccv * 4 + j + 1]
                            pr.emit("dve", lambda h, w=w, cv=cv, rx0=rx0: h.tensor_scalar(
                                cv[:, :], rawx[:, rx0 + 1:rx0 + 513], w(1), None,
                                op0=ALU.mult),
                                reads=[rk0, rk1, R("c_cwt")], writes=[ck], tag="cva1")
                            pr.emit("dve", lambda h, w=w, cv=cv, rx0=rx0:
                                    h.scalar_tensor_tensor(
                                        cv[:, :], rawx[:, rx0:rx0 + 512], w(0), cv[:, :],
                                        op0=ALU.mult, op1=ALU.add),
                                    reads=[rk0, rk1, R("c_cwt"), ck], writes=[ck], tag="cva0")
                            pr.emit("dve", lambda h, w=w, cv=cv, rx0=rx0:
                                    h.scalar_tensor_tensor(
                                        cv[:, :], rawx[:, rx0 + 3:rx0 + 515], w(3), cv[:, :],
                                        op0=ALU.mult, op1=ALU.add),
                                    reads=[rk1, R("c_cwt"), ck], writes=[ck], tag="cvb3")
                            pr.emit("dve", lambda h, w=w, cv=cv, rx0=rx0:
                                    h.scalar_tensor_tensor(
                                        cv[:, :], rawx[:, rx0 + 2:rx0 + 514], w(2), cv[:, :],
                                        op0=ALU.mult, op1=ALU.add),
                                    reads=[rk0, rk1, R("c_cwt"), ck], writes=[ck], tag="cvb2")
                            pr.emit("act", lambda h, cc=cc, cv=cv: h.activation(
                                xbcz[:, cc * MS:(cc + 1) * MS], cv[:, :], AF.Silu),
                                reads=[ck], writes=[kz(cc)], tag="cvs")
                        else:
                            t0k, t1k = R("dt0"), R("dt1")
                            pr.emit("act", lambda h, pt=pt, off=off: h.activation(
                                cvt[0][0:64, :], pt[0:64, off:off + 512], AF.Abs,
                                bias=dtb[:, :]),
                                reads=pkeys + [R("c_dtb")], writes=[t0k], tag="dtab")
                            pr.emit("act", lambda h: h.activation(
                                cvt[0][0:64, :], cvt[0][0:64, :], AF.Exp, scale=-1.0),
                                reads=[t0k], writes=[t0k], tag="dtex")
                            pr.emit("act", lambda h: h.activation(
                                cvt[0][0:64, :], cvt[0][0:64, :], AF.Ln, bias=1.0),
                                reads=[t0k], writes=[t0k], tag="dtln")
                            pr.emit("act", lambda h, pt=pt, off=off: h.activation(
                                cvt[1][0:64, :], pt[0:64, off:off + 512], AF.Relu,
                                bias=dtb[:, :]),
                                reads=pkeys + [R("c_dtb")], writes=[t1k], tag="dtre")
                            pr.emit("dve", lambda h: h.tensor_add(
                                dtF[:, :], cvt[0][0:64, :], cvt[1][0:64, :]),
                                reads=[t0k, t1k], writes=[R("dt", m)], tag="dtad")
                            pr.emit("act", lambda h: h.activation(
                                colBF[:, :], dtF[:, :], AF.Ln),
                                reads=[R("dt", m)], writes=[R("lndt", m)], tag="dtld")
                    do_dump("xbcz", lambda: xbcz[:, :], [kz(c) for c in range(NCC)])
                    do_dump("dt", lambda: dtF[:, :], [R("dt", m)])
                    pr.barrier()
                if mstage < 2:
                    return

                pr.emit("dve", lambda h: h.tensor_scalar(
                    ddtF[:, :], dtF[:, :], acol[:, :], None, op0=ALU.mult),
                    reads=[R("dt", m), R("acol")], writes=[R("dA", m)], tag="dA")
                pr.emit("dve", lambda h: h.memset(zer64[:, :], 0.0),
                        writes=[R("z64", m)], tag="z64")
                pr.emit("dve", lambda h: h.memset(dachi[64:128, :], 0.0),
                        writes=[R("dahz", m)], tag="hz")
                pr.emit("dve", lambda h: h.memset(daclo[64:128, :], 0.0),
                        writes=[R("dalz", m)], tag="lz")
                dack = [R("dAc", m, c) for c in range(NCH)]
                for c in range(NCH):
                    pr.emit("dve", lambda h, c=c: h.tensor_tensor_scan(
                        dAcF[:, c * CH:(c + 1) * CH],
                        ddtF[:, c * CH:(c + 1) * CH],
                        zer64[:, :], 0.0, op0=ALU.add, op1=ALU.add),
                        reads=[R("dA", m), R("z64", m)], writes=[R("dAc", m, c)], tag="cums")
                pr.emit("dve", lambda h: h.tensor_sub(
                    colBF[:, :], colBF[:, :], dAcF[:, :]),
                    reads=[R("lndt", m)] + dack, writes=[R("colB", m)], tag="colB")
                pr.emit("act", lambda h: h.activation(
                    ephF[:, :], dAcF[:, :], AF.Exp),
                    reads=dack, writes=[R("eph", m)], tag="eph")
                pr.emit("dve", lambda h: h.tensor_copy(ephbf[:, :], ephF[:, :]),
                        reads=[R("eph", m)], writes=[R("ephbf", m)], tag="ephb")
                pr.emit("dve", lambda h: h.tensor_copy(dachi[0:64, :], dAcF[:, :]),
                        reads=dack, writes=[R("dachi", m)], tag="hib")
                pr.emit("dve", lambda h: h.scalar_tensor_tensor(
                    daclo[0:64, :], dachi[0:64, :], -1.0, dAcF[:, :],
                    op0=ALU.mult, op1=ALU.add),
                    reads=[R("dachi", m)] + dack, writes=[R("daclo", m)], tag="lob")
                pr.dma("pool", "dh2", lambda h: h.dma_start(
                    dachi2[0:64, :], dachi[32:96, :]),
                    reads=[R("dachi", m), R("dahz", m)], writes=[R("dachi2", m)], tag="dh2")
                pr.dma("pool", "dl2", lambda h: h.dma_start(
                    daclo2[0:64, :], daclo[32:96, :]),
                    reads=[R("daclo", m), R("dalz", m)], writes=[R("daclo2", m)], tag="dl2")
                for c in range(NCH):
                    last = c * CH + CH - 1
                    pr.emit("dve", lambda h, c=c, last=last: h.tensor_scalar(
                        ddtF[:, c * CH:(c + 1) * CH],
                        dAcF[:, c * CH:(c + 1) * CH],
                        dAcF[:, last:last + 1], -1.0,
                        op0=ALU.subtract, op1=ALU.mult),
                        reads=[R("dAc", m, c)],
                        writes=[R("ddt", m, c), R("dA", m)], tag="dd1")
                    pr.emit("act", lambda h, c=c: h.activation(
                        ddtF[:, c * CH:(c + 1) * CH],
                        ddtF[:, c * CH:(c + 1) * CH], AF.Exp),
                        reads=[R("ddt", m, c)], writes=[R("ddt", m, c)], tag="dd2")
                    pr.emit("dve", lambda h, c=c: h.tensor_mul(
                        ddtF[:, c * CH:(c + 1) * CH],
                        ddtF[:, c * CH:(c + 1) * CH],
                        dtF[:, c * CH:(c + 1) * CH]),
                        reads=[R("ddt", m, c), R("dt", m)],
                        writes=[R("ddt", m, c)], tag="dd3")
                pr.emit("dve", lambda h: h.tensor_copy(ddtbf[:, :], ddtF[:, :]),
                        reads=[R("ddt", m, c) for c in range(NCH)],
                        writes=[R("ddtbf", m)], tag="ddb")
                do_dump("dAc", lambda: dAcF[:, :], dack)
                pr.barrier()

                with contextlib.ExitStack() as s2:
                    rsel = sb(f"rsel_{m}", [64, 4096], BF16, "left", s2)
                    rep64 = sb(f"rep64_{m}", [64, 4096], BF16, "left", s2)
                    xhT = sb(f"xhT_{m}", [128, 4096], BF16, "left", s2)
                    xhd = sb(f"xhd_{m}", [128, 1024], BF16, "left", s2)
                    mtr = sb(f"mtr_{m}", [128, 1024], BF16, "left", s2)
                    gtm = sb(f"gtm_{m}", [128, 128], BF16, "left", s2)
                    bsb = sb(f"bsb_{m}", [128, 128], BF16, "left", s2)
                    cbt = sb(f"cbt_{m}", [128, 64], F32, "left", s2)
                    esb = sb(f"esb_{m}", [128, 512], BF16, "left", s2)
                    yrw = sb(f"yrw_{m}", [128, 2048], BF16, "left", s2)
                    qeb = sb(f"qeb_{m}", [128, 1024], BF16, "left", s2)
                    dxh = sb(f"dxh_{m}", [128, 2048], BF16, "left", s2)
                    cdq = sb(f"cdq_{m}", [64, 512], BF16, "left", s2)
                    cds = sb(f"cds_{m}", [128, 512], BF16, "left", s2)
                    cmy = sb(f"cmy_{m}", [128, 512], BF16, "left", s2)
                    rsb = sb(f"rsb_{m}", [128, 128], F32, "left", s2)
                    pr.dma("sp", "rsel", lambda h: h.dma_start(rsel[:, :], rsel_h),
                           writes=[R("c_rsel")], tag="rsl")
                    pr.dma("sp", "rep", lambda h: h.dma_start(rep64[:, :], rep_h),
                           writes=[R("c_rep")], tag="repl")

                    for c in range(NCH):
                        xhTk = [R("xhT", g) for g in range(8)]
                        for g in range(8):
                            for j in range(4):
                                t = g * 4 + j
                                pr.emit("pe", lambda h, t=t, j=j, c=c: h.transpose(
                                    PT[:, j * 128:(j + 1) * 128],
                                    xbcz[:, (32 + t) * MS + c * CH:(32 + t) * MS + c * CH + 128],
                                    idbf[:, :]),
                                    reads=[kz(32 + t), R("c_idbf")],
                                    writes=[R("PT", j)], tag="xhTt")
                            pr.emit("dve", lambda h, g=g: h.tensor_copy(
                                xhT[:, g * 512:(g + 1) * 512], PT[:, 0:512]),
                                reads=[R("PT", j) for j in range(4)],
                                writes=[R("xhT", g)], tag="xhTe")
                        pr.emit("pe", lambda h, c=c: h.transpose(
                            PT[:, 0:128],
                            xbcz[:, 64 * MS + c * CH:64 * MS + c * CH + 128], idbf[:, :]),
                            reads=[kz(64), R("c_idbf")], writes=[R("PT", 0)], tag="Btt")
                        pr.emit("dve", lambda h: h.tensor_copy(bsb[:, :], PT[:, 0:128]),
                                reads=[R("PT", 0)], writes=[R("bsb")], tag="Bte")
                        pr.emit("pe", lambda h, c=c: h.matmul(
                            PA[:, 0:64], colBF[:, c * CH:(c + 1) * CH],
                            idf[0:64, 0:64], start=True, stop=True),
                            reads=[R("colB", m), R("c_idf")], writes=[R("PAa")], tag="cbtt")
                        pr.emit("dve", lambda h: h.tensor_copy(cbt[:, :], PA[:, 0:64]),
                                reads=[R("PAa")], writes=[R("cbt")], tag="cbte")
                        pr.emit("pe", lambda h, c=c: h.matmul(
                            PB[:, 0:128],
                            xbcz[:, 64 * MS + c * CH:64 * MS + c * CH + 128],
                            xbcz[:, 65 * MS + c * CH:65 * MS + c * CH + 128],
                            start=True, stop=True),
                            reads=[kz(64), kz(65)], writes=[R("PB", 0)], tag="Gt")
                        pr.emit("dve", lambda h: h.tensor_mul(
                            gtm[:, :], PB[:, 0:128], triu[:, :]),
                            reads=[R("PB", 0), R("c_triu")], writes=[R("gtm")], tag="Gtm")

                        for H in range(2):
                            hp0, t0 = H * 2048, H * 16
                            def em_bc(hh):
                                h_ = H * 32 + hh
                                pbs = hh % 4
                                hi_t = dachi if h_ < 32 else dachi2
                                lo_t = daclo if h_ < 32 else daclo2
                                hi_k = R("dachi", m) if h_ < 32 else R("dachi2", m)
                                lo_k = R("daclo", m) if h_ < 32 else R("daclo2", m)
                                pr.emit("pe", lambda h, h_=h_, pbs=pbs, t=hi_t, c=c:
                                        h.matmul(PB[:, pbs * 128:(pbs + 1) * 128],
                                                 rsel[:, (h_ % 32) * 128:(h_ % 32 + 1) * 128],
                                                 ap2(t, c * CH, 1, CH, 0, 64),
                                                 start=True, stop=False),
                                        reads=[hi_k, R("dahz", m), R("c_rsel")],
                                        writes=[R("PB", pbs)], tag="bch")
                                pr.emit("pe", lambda h, h_=h_, pbs=pbs, t=lo_t, c=c:
                                        h.matmul(PB[:, pbs * 128:(pbs + 1) * 128],
                                                 rsel[:, (h_ % 32) * 128:(h_ % 32 + 1) * 128],
                                                 ap2(t, c * CH, 1, CH, 0, 64),
                                                 start=False, stop=True),
                                        reads=[lo_k, R("dalz", m), R("c_rsel")],
                                        writes=[R("PB", pbs)], tag="bcl")

                            def em_seg(hh):
                                h_ = H * 32 + hh
                                pbs = pcs = hh % 4
                                pr.emit("dve", lambda h, pbs=pbs, pcs=pcs, h_=h_:
                                        h.tensor_scalar(
                                            PC[:, pcs * 128:(pcs + 1) * 128],
                                            PB[:, pbs * 128:(pbs + 1) * 128],
                                            cbt[:, h_ % 64:h_ % 64 + 1], CLAMP,
                                            op0=ALU.add, op1=ALU.min),
                                        reads=[R("PB", pbs), R("cbt")],
                                        writes=[R("PC", pcs)], tag="seg")

                            def em_exp4(g):
                                mts = g % 2
                                for j in range(4):
                                    pr.emit("act", lambda h, mts=mts, j=j: h.activation(
                                        mtr[:, mts * 512 + j * 128:
                                            mts * 512 + (j + 1) * 128],
                                        PC[:, j * 128:(j + 1) * 128], AF.Exp),
                                        reads=[R("PC", j)],
                                        writes=[R("mtr", mts)], tag="exp")

                            def em_mtm4(g):
                                mts = g % 2
                                for j in range(4):
                                    pr.emit("dve", lambda h, mts=mts, j=j: h.tensor_mul(
                                        mtr[:, mts * 512 + j * 128:mts * 512 + (j + 1) * 128],
                                        mtr[:, mts * 512 + j * 128:mts * 512 + (j + 1) * 128],
                                        gtm[:, :]),
                                        reads=[R("mtr", mts), R("gtm")],
                                        writes=[R("mtr", mts)], tag="mtm")

                            def em_ydg(hh):
                                h_ = H * 32 + hh
                                mts = (hh // 4) % 2
                                off = (hh % 4) * 128
                                pr.emit("pe", lambda h, h_=h_, hh=hh, mts=mts, off=off:
                                        h.matmul(
                                    PY[(h_ % 2) * 64:(h_ % 2) * 64 + 64,
                                       (hh // 2) * 128:(hh // 2) * 128 + 128],
                                    xhT[:, h_ * 64:(h_ + 1) * 64],
                                    mtr[:, mts * 512 + off:mts * 512 + off + 128],
                                    start=True, stop=True,
                                    tile_position=(0, (h_ % 2) * 64)),
                                    reads=[R("mtr", mts)] + xhTk,
                                    writes=[R("PY", hh // 2)], tag="ydg")

                            for g in range(10):
                                if 1 <= g < 9:
                                    for j in range(4):
                                        em_seg((g - 1) * 4 + j)
                                    em_exp4(g - 1)
                                if g < 8:
                                    for j in range(4):
                                        em_bc(g * 4 + j)
                                if 2 <= g:
                                    em_mtm4(g - 2)
                                    for j in range(4):
                                        em_ydg((g - 2) * 4 + j)
                            pr.emit("act", lambda h: h.copy(yrw[:, :], PY[:, 0:2048]),
                                    reads=[R("PY", t) for t in range(16)],
                                    writes=[R("yrw")], tag="yev")
                            if c == 1 and H == 0 and m == 0:
                                do_dump("yrw1", lambda: yrw[:, :], [R("yrw")])
                            for t in range(16):
                                pr.emit("pe", lambda h, t=t, t0=t0, c=c: h.matmul(
                                    PY[:, t * 128:(t + 1) * 128],
                                    carry[:, (t0 + t) * 128:(t0 + t + 1) * 128],
                                    xbcz[:, 65 * MS + c * CH:65 * MS + c * CH + 128],
                                    start=True, stop=True),
                                    reads=[R("carry", t0 + t), kz(65)],
                                    writes=[R("PY", t)], tag="Q")
                            for q in range(4):
                                for j in range(4):
                                    t = q * 4 + j
                                    pr.emit("pe", lambda h, t=t, j=j, t0=t0, c=c: h.matmul(
                                        PC[:, j * 128:(j + 1) * 128],
                                        rep64[:, (t0 + t) * 128:(t0 + t + 1) * 128],
                                        ephbf[:, c * CH:(c + 1) * CH],
                                        start=True, stop=True),
                                        reads=[R("c_rep"), R("ephbf", m)],
                                        writes=[R("PC", j)], tag="E")
                                pr.emit("act", lambda h: h.copy(esb[:, :], PC[:, 0:512]),
                                        reads=[R("PC", j) for j in range(4)],
                                        writes=[R("esb")], tag="Eev")
                                qs = q % 2
                                pr.emit("dve", lambda h, q=q, qs=qs: h.tensor_mul(
                                    qeb[:, qs * 512:(qs + 1) * 512],
                                    PY[:, q * 512:(q + 1) * 512], esb[:, :]),
                                    reads=[R("PY", q * 4 + j) for j in range(4)] + [R("esb")],
                                    writes=[R("qeb", qs)], tag="qe")
                                pr.emit("dve", lambda h, q=q, qs=qs: h.scalar_tensor_tensor(
                                    yrw[:, q * 512:(q + 1) * 512],
                                    qeb[:, qs * 512:(qs + 1) * 512], 1.0,
                                    yrw[:, q * 512:(q + 1) * 512],
                                    op0=ALU.mult, op1=ALU.add),
                                    reads=[R("qeb", qs), R("yrw")], writes=[R("yrw")],
                                    tag="add1")
                            for t in range(16):
                                pr.emit("dve", lambda h, t=t, t0=t0, c=c: h.tensor_scalar(
                                    dxh[:, t * 128:(t + 1) * 128],
                                    xbcz[:, (32 + t0 + t) * MS + c * CH:
                                         (32 + t0 + t) * MS + c * CH + 128],
                                    dcol[:, t0 + t:t0 + t + 1], None, op0=ALU.mult),
                                    reads=[kz(32 + t0 + t), R("c_dcol")],
                                    writes=[R("dxh", t)], tag="dxh")
                            pr.emit("dve", lambda h: h.scalar_tensor_tensor(
                                yrw[:, :], dxh[:, :], 1.0, yrw[:, :],
                                op0=ALU.mult, op1=ALU.add),
                                reads=[R("dxh", t) for t in range(16)] + [R("yrw")],
                                writes=[R("yrw")], tag="add2")
                            zk = [kz(t0 + t) for t in range(16)]
                            pr.emit("dve", lambda h, t0=t0, c=c: h.tensor_mul(
                                ap3(gsl, t0 * MS + c * CH, MS, 16, 1, 128), yrw[:, :],
                                ap3(xbcz, t0 * MS + c * CH, MS, 16, 1, 128)),
                                reads=[R("yrw")] + zk, writes=[R("gsl", m, c, H)],
                                tag="gate")
                            pr.emit("act", lambda h, t0=t0, c=c: h.activation(
                                dxh[:, :],
                                ap3(gsl, t0 * MS + c * CH, MS, 16, 1, 128), AF.Square),
                                reads=[R("gsl", m, c, H)],
                                writes=[R("gsq")] + [R("dxh", t) for t in range(16)],
                                tag="gsq")
                            for t in range(16):
                                pr.emit("pe", lambda h, t=t, H=H: h.matmul(
                                    PA[0:1, 384:512], onescb[:, :],
                                    dxh[:, t * 128:(t + 1) * 128],
                                    start=(H == 0 and t == 0),
                                    stop=(H == 1 and t == 15)),
                                    reads=[R("gsq"), R("dxh", t), R("c_onescb")],
                                    writes=[R("PAs")], tag="ssq")
                            for q in range(4):
                                pr.emit("pe", lambda h, q=q, hp0=hp0, c=c: h.matmul(
                                    PC[:, 0:512],
                                    ap2(ddtbf, c * CH, 1, CH, 0, 64),
                                    rep64[:, hp0 + q * 512:hp0 + (q + 1) * 512],
                                    start=True, stop=True),
                                    reads=[R("ddtbf", m), R("c_rep")],
                                    writes=[R("PC", j) for j in range(4)], tag="dtd")
                                pr.emit("dve", lambda h, q=q, hp0=hp0: h.tensor_mul(
                                    xhd[:, (q % 2) * 512:(q % 2 + 1) * 512],
                                    xhT[:, hp0 + q * 512:hp0 + (q + 1) * 512],
                                    PC[:, 0:512]),
                                    reads=[R("PC", j) for j in range(4)] + xhTk,
                                    writes=[R("xhd", q % 2)], tag="xhd")
                                if c == 0 and H == 0 and q == 0 and m == 0:
                                    pr.emit("dve", lambda h: h.tensor_copy(
                                        cds[:, :], PC[:, 0:512]),
                                        reads=[R("PC", j) for j in range(4)],
                                        writes=[R("cds")], tag="dtddmp")
                                    do_dump("dtd0", lambda: cds[:, :], [R("cds")])
                                    do_dump("bsb0", lambda: bsb[:, :], [R("bsb")])
                                    do_dump("xhd0", lambda: xhd[:, 0:512], [R("xhd", 0)])
                                    do_dump("ddtbf0", lambda: ddtbf[:, :], [R("ddtbf", m)])
                                for j in range(8):
                                    hh = q * 8 + j
                                    pr.emit("pe", lambda h, hh=hh, q=q: h.matmul(
                                        PY[:, hh * 64:(hh + 1) * 64],
                                        bsb[:, :],
                                        xhd[:, (q % 2) * 512 + (hh % 8) * 64:
                                            (q % 2) * 512 + (hh % 8 + 1) * 64],
                                        start=True, stop=True),
                                        reads=[R("bsb"), R("xhd", q % 2)],
                                        writes=[R("PY", hh // 2)], tag="st")
                            for q in range(4):
                                pr.emit("dve", lambda h, q=q, hp0=hp0, c=c: h.tensor_scalar(
                                    cdq[:, :],
                                    rep64[0:64, hp0 + q * 512:hp0 + (q + 1) * 512],
                                    ephF[:, c * CH + 127:c * CH + 128], None,
                                    op0=ALU.mult),
                                    reads=[R("c_rep"), R("eph", m)],
                                    writes=[R("cdq")], tag="cdq")
                                pr.emit("pe", lambda h: h.matmul(
                                    PC[:, 0:512], ones64[:, :], cdq[:, :],
                                    start=True, stop=True),
                                    reads=[R("cdq"), R("c_ones64")],
                                    writes=[R("PC", j) for j in range(4)], tag="cdm")
                                pr.emit("act", lambda h: h.copy(cds[:, :], PC[:, 0:512]),
                                        reads=[R("PC", j) for j in range(4)],
                                        writes=[R("cds")], tag="cde")
                                pr.emit("dve", lambda h, q=q, hp0=hp0: h.tensor_mul(
                                    cmy[:, :],
                                    carry[:, hp0 + q * 512:hp0 + (q + 1) * 512],
                                    cds[:, :]),
                                    reads=[R("cds")] + [R("carry", t0 + q * 4 + j)
                                                        for j in range(4)],
                                    writes=[R("cmy")], tag="cmy")
                                if c == 0 and H == 0 and q == 0 and m == 0:
                                    do_dump("cdq0", lambda: cdq[:, :], [R("cdq")])
                                    do_dump("cds0", lambda: cds[:, :], [R("cds")])
                                    do_dump("cmy0", lambda: cmy[:, :], [R("cmy")])
                                    pr.emit("dve", lambda h: h.tensor_copy(
                                        esb[:, :], PY[:, 0:512]),
                                        reads=[R("PY", j) for j in range(4)],
                                        writes=[R("esb")], tag="stdmp")
                                    do_dump("st0", lambda: esb[:, :], [R("esb")])
                                pr.emit("dve", lambda h, q=q, hp0=hp0: h.tensor_add(
                                    carry[:, hp0 + q * 512:hp0 + (q + 1) * 512],
                                    cmy[:, :], PY[:, q * 512:(q + 1) * 512]),
                                    reads=[R("cmy")] + [R("PY", q * 4 + j)
                                                        for j in range(4)],
                                    writes=[R("carry", t0 + q * 4 + j) for j in range(4)],
                                    tag="cup")
                        pr.emit("dve", lambda h: h.tensor_scalar(
                            srow[0:1, :], PA[0:1, 384:512], 1.0 / HID, EPS,
                            op0=ALU.mult, op1=ALU.add),
                            reads=[R("PAs")], writes=[R("srow")], tag="rs1")
                        pr.emit("act", lambda h: h.activation(
                            srow[0:1, :], srow[0:1, :], AF.Ln),
                            reads=[R("srow")], writes=[R("srow")], tag="rs2")
                        pr.emit("act", lambda h: h.activation(
                            srow[0:1, :], srow[0:1, :], AF.Exp, scale=-0.5),
                            reads=[R("srow")], writes=[R("srow")], tag="rs3")
                        pr.emit("pe", lambda h: h.matmul(
                            PA[:, 256:384], onesf[:, :], srow[0:1, :],
                            start=True, stop=True),
                            reads=[R("srow"), R("c_onesf")], writes=[R("PAr")], tag="rs4")
                        pr.emit("dve", lambda h: h.tensor_copy(rsb[:, :], PA[:, 256:384]),
                                reads=[R("PAr")], writes=[R("rsb")], tag="rs5")
                        for H in range(2):
                            t0 = H * 16
                            pr.emit("dve", lambda h, t0=t0, c=c: h.tensor_mul(
                                ap3(gsl, t0 * MS + c * CH, MS, 16, 1, 128),
                                ap3(gsl, t0 * MS + c * CH, MS, 16, 1, 128),
                                bass.AP(rsb, 0, [[128, 128], [0, 16], [1, 128]])),
                                reads=[R("gsl", m, c, H), R("rsb")],
                                writes=[R("gsl", m, c, H)], tag="gfin")
                        if c == 0 and m == 0:
                            do_dump("xhT", lambda: xhT[:, :], xhTk)
                            do_dump("carry0", lambda: carry[:, :],
                                    [R("carry", t) for t in range(32)])
                        if c == 1 and m == 0:
                            do_dump("rsb1", lambda: rsb[:, :], [R("rsb")])
                        if c == NCH - 1 and m == NMS - 1:
                            do_dump("carry", lambda: carry[:, :],
                                    [R("carry", t) for t in range(32)])
                    do_dump("gsl", lambda: gsl[:, :], gkeys)
                    pr.barrier()
                sx.close()
                if mstage < 3:
                    return

                with contextlib.ExitStack() as s3:
                    wmv = sb(f"wmv_{m}", [128, 2 * 8192], BF16, "left", s3)
                    vsl = sb(f"vsl_{m}", [128, NMS * 2048], F16, "left", s3)
                    xst = sb(f"xs2_{m}", [128, 2048], F32, "left", s3)
                    for fo in range(8):
                        slot = fo % 2
                        wks = [R("wmv", slot, kt) for kt in range(32)]
                        pr.dma("pool", f"wmv{slot}",
                               lambda h, fo=fo, slot=slot: h.dma_start(
                                   wmv[:, slot * 8192:(slot + 1) * 8192],
                                   wout_h[:, fo * 8192:(fo + 1) * 8192]),
                               writes=wks, tag="wol")
                        for kt in range(32):
                            pr.emit("dve", lambda h, slot=slot, kt=kt: h.tensor_scalar(
                                wmv[:, slot * 8192 + kt * 256:slot * 8192 + (kt + 1) * 256],
                                wmv[:, slot * 8192 + kt * 256:slot * 8192 + (kt + 1) * 256],
                                nwt[:, kt:kt + 1], None, op0=ALU.mult),
                                reads=[R("wmv", slot, kt), R("c_nw")],
                                writes=[R("wmv", slot, kt)], tag="wosc")
                        for stt in range(NMS):
                            for kt in range(32):
                                pr.emit("pe", lambda h, kt=kt, stt=stt, slot=slot:
                                        h.matmul(PY[:, stt * 512:stt * 512 + 256],
                                                 gsl[:, kt * MS + stt * 128:
                                                     kt * MS + stt * 128 + 128],
                                                 wmv[:, slot * 8192 + kt * 256:
                                                     slot * 8192 + (kt + 1) * 256],
                                                 start=(kt == 0), stop=(kt == 31)),
                                        reads=[R("wmv", slot, kt)] + gkeys,
                                        writes=[R("PY", stt * 4), R("PY", stt * 4 + 1)],
                                        tag="opm")
                            pr.emit("dve", lambda h, stt=stt, fo=fo: h.tensor_copy(
                                vsl[:, stt * 2048 + fo * 256:stt * 2048 + (fo + 1) * 256],
                                PY[:, stt * 512:stt * 512 + 256]),
                                reads=[R("PY", stt * 4), R("PY", stt * 4 + 1)],
                                writes=[R("vsl", m, stt, fo)], tag="opev")
                    pr.barrier()
                    for stt in range(NMS):
                        row0 = tok0 + stt * 128
                        vk = [R("vsl", m, stt, fo) for fo in range(8)]
                        pr.dma("sp", "xld",
                               lambda h, r=row0: h.dma_start(xst[:, :], x_h[r:r + 128, :]),
                               writes=[xkey], tag="xld2")
                        rsd, nmr, skeys = ln_stats(
                            lambda stt=stt: vsl[:, stt * 2048:(stt + 1) * 2048], vk,
                            lambda: lntmp[:, :], R("lntmp"))
                        pr.emit("dve", lambda h, r=rsd, b=nmr, stt=stt: h.tensor_scalar(
                            lntmp[:, :], vsl[:, stt * 2048:(stt + 1) * 2048], r, b,
                            op0=ALU.mult, op1=ALU.add),
                            reads=vk + skeys, writes=[R("lntmp")], tag="ln2n")
                        gi = (m * NMS + stt) % 8
                        pr.emit("dve", lambda h, gi=gi: h.tensor_add(
                            x2sl[:, gi * 2048:(gi + 1) * 2048], lntmp[:, :], xst[:, :]),
                            reads=[R("lntmp"), xkey], writes=[R("x2", gi)], tag="x2a")
                    do_dump("vsl", lambda: vsl[:, :],
                            [R("vsl", m, stt, fo) for stt in range(NMS) for fo in range(8)])
                    pr.barrier()

        def ffn_slab(f):
            tok0 = f * FS
            with contextlib.ExitStack() as fsk:
                fh = fsk.enter_context(contextlib.ExitStack())
                h1T = sb(f"h1T_{f}", [128, 32 * FS], BF16, "left", fh)
                h1k = [R("h1T", f, cc) for cc in range(32)]
                with contextlib.ExitStack() as f1:
                    ln3T = sb(f"ln3T_{f}", [128, 16 * FS], BF16, "left", f1)
                    wst = [sb(f"fw{i}_{f}", [128, 2048], BF16, "left", f1)
                           for i in range(2)]
                    lk = []
                    for stt in range(8):
                        x2k = [R("x2", stt)]
                        rsd, nmr, skeys = ln_stats(
                            lambda stt=stt: x2sl[:, stt * 2048:(stt + 1) * 2048], x2k,
                            lambda: lntmp[:, :], R("lntmp"))
                        pr.emit("dve", lambda h, r=rsd, b=nmr, stt=stt: h.tensor_scalar(
                            lntmp[:, :], x2sl[:, stt * 2048:(stt + 1) * 2048], r, b,
                            op0=ALU.mult, op1=ALU.add),
                            reads=x2k + skeys, writes=[R("lntmp")], tag="ln3n")
                        for g in range(4):
                            for j in range(4):
                                kt = g * 4 + j
                                pr.emit("pe", lambda h, kt=kt, j=j: h.transpose(
                                    PT[:, j * 128:(j + 1) * 128],
                                    lntmp[:, kt * 128:(kt + 1) * 128], idbf[:, :]),
                                    reads=[R("lntmp"), R("c_idbf")],
                                    writes=[R("PT", j)], tag="ln3t")
                            pr.emit("dve", lambda h, g=g, stt=stt: h.tensor_copy(
                                ap3(ln3T, g * 4 * FS + stt * 128, FS, 4, 1, 128),
                                PT[:, 0:512]),
                                reads=[R("PT", j) for j in range(4)],
                                writes=[R("ln3T", f, g, stt)], tag="ln3e")
                        lk += [R("ln3T", f, g, stt) for g in range(4)]
                    pr.barrier()
                    for cc in range(32):
                        slot = cc % 2
                        pr.dma("pool", f"fw{slot}",
                               lambda h, cc=cc, slot=slot: h.dma_start(
                                   wst[slot][:, :], w1_h[:, cc * 2048:(cc + 1) * 2048]),
                               writes=[R("fw", slot)], tag="w1l")
                        for ff in range(2):
                            pn, off = PSL[(cc * 2 + ff) % 7]
                            pt = psum_of(pn)
                            pk = pskeys(pn, off)
                            for kt in range(16):
                                pr.emit("pe", lambda h, pt=pt, off=off, kt=kt, ff=ff, sl=slot:
                                        h.matmul(pt[:, off:off + 512],
                                                 wst[sl][:, kt * 128:(kt + 1) * 128],
                                                 ln3T[:, kt * FS + ff * 512:
                                                      kt * FS + ff * 512 + 512],
                                                 start=(kt == 0), stop=(kt == 15)),
                                        reads=[R("fw", slot)] + lk, writes=pk, tag="h1m")
                            pr.emit("act", lambda h, pt=pt, off=off, cc=cc, ff=ff:
                                    h.activation(
                                        h1T[:, cc * FS + ff * 512:cc * FS + ff * 512 + 512],
                                        pt[:, off:off + 512], AF.Relu,
                                        bias=b1t[:, cc:cc + 1]),
                                    reads=pk + [R("c_b1")], writes=[R("h1T", f, cc)],
                                    tag="h1e")
                    do_dump("h1T", lambda: h1T[:, :], h1k)
                    pr.barrier()
                h2T = sb(f"h2T_{f}", [128, 32 * FS], BF16, side="right", stk=fsk)
                h2k = [R("h2T", f, cc) for cc in range(32)]
                with contextlib.ExitStack() as f2:
                    wst2 = [sb(f"f2w{i}_{f}", [128, 4096], BF16, "left", f2)
                            for i in range(2)]
                    for cc in range(32):
                        slot = cc % 2
                        pr.dma("pool", f"f2w{slot}",
                               lambda h, cc=cc, slot=slot: h.dma_start(
                                   wst2[slot][:, :], w2_h[:, cc * 4096:(cc + 1) * 4096]),
                               writes=[R("f2w", slot)], tag="w2l")
                        for ff in range(2):
                            pn, off = PSL[(cc * 2 + ff) % 7]
                            pt = psum_of(pn)
                            pk = pskeys(pn, off)
                            for kt in range(32):
                                pr.emit("pe", lambda h, pt=pt, off=off, kt=kt, ff=ff, sl=slot:
                                        h.matmul(pt[:, off:off + 512],
                                                 wst2[sl][:, kt * 128:(kt + 1) * 128],
                                                 h1T[:, kt * FS + ff * 512:
                                                      kt * FS + ff * 512 + 512],
                                                 start=(kt == 0), stop=(kt == 31)),
                                        reads=[R("f2w", slot)] + h1k, writes=pk, tag="h2m")
                            pr.emit("act", lambda h, pt=pt, off=off, cc=cc, ff=ff:
                                    h.activation(
                                        h2T[:, cc * FS + ff * 512:cc * FS + ff * 512 + 512],
                                        pt[:, off:off + 512], AF.Relu,
                                        bias=b2t[:, cc:cc + 1]),
                                    reads=pk + [R("c_b2")], writes=[R("h2T", f, cc)],
                                    tag="h2e")
                    pr.barrier()
                fh.close()
                with contextlib.ExitStack() as f3:
                    wmv = sb(f"w3m_{f}", [128, 2 * 8192], BF16, "left", f3)
                    fsl = sb(f"fsl_{f}", [128, 8 * 2048], BF16, "left", f3)
                    b3m = sb(f"b3m_{f}", [128, 2048], BF16, "left", f3)
                    b3row = sb(f"b3row_{f}", [1, DM], F32, "left", f3)
                    ost = sb(f"ost_{f}", [128, 2048], F32, "left", f3)
                    pr.dma("sp", "c_b3", lambda h: h.dma_start(b3row[:, :], b3_h),
                           writes=[R("c_b3")], tag="b3l")
                    for q in range(4):
                        pr.emit("pe", lambda h, q=q: h.matmul(
                            PA[:, 0:512], onesf[:, :],
                            b3row[0:1, q * 512:(q + 1) * 512], start=True, stop=True),
                            reads=[R("c_b3"), R("c_onesf")], writes=[R("PAF")], tag="b3b")
                        pr.emit("dve", lambda h, q=q: h.tensor_copy(
                            b3m[:, q * 512:(q + 1) * 512], PA[:, 0:512]),
                            reads=[R("PAF")], writes=[R("b3m", q)], tag="b3e")
                    for fo in range(8):
                        slot = fo % 2
                        pr.dma("pool", f"w3m{slot}",
                               lambda h, fo=fo, slot=slot: h.dma_start(
                                   wmv[:, slot * 8192:(slot + 1) * 8192],
                                   w3_h[:, fo * 8192:(fo + 1) * 8192]),
                               writes=[R("w3m", slot)], tag="w3l")
                        for stt in range(8):
                            pn, off = [("PY", 0), ("PY", 512), ("PY", 1024),
                                       ("PY", 1536), ("PB", 0), ("PC", 0)][stt % 6]
                            pt = psum_of(pn)
                            pk = pskeys(pn, off, 256)
                            for kt in range(32):
                                pr.emit("pe", lambda h, pt=pt, off=off, kt=kt, stt=stt,
                                        sl=slot: h.matmul(
                                            pt[:, off:off + 256],
                                            h2T[:, kt * FS + stt * 128:
                                                 kt * FS + stt * 128 + 128],
                                            wmv[:, sl * 8192 + kt * 256:
                                                sl * 8192 + (kt + 1) * 256],
                                            start=(kt == 0), stop=(kt == 31)),
                                        reads=[R("w3m", slot)] + h2k, writes=pk, tag="w3m")
                            pr.emit("dve", lambda h, pt=pt, off=off, stt=stt, fo=fo:
                                    h.tensor_add(
                                        fsl[:, stt * 2048 + fo * 256:
                                            stt * 2048 + (fo + 1) * 256],
                                        pt[:, off:off + 256],
                                        b3m[:, fo * 256:(fo + 1) * 256]),
                                    reads=pk + [R("b3m", fo // 2)],
                                    writes=[R("fsl", f, stt, fo)], tag="w3e")
                    do_dump("fsl", lambda: fsl[:, :],
                            [R("fsl", f, stt, fo) for stt in range(8) for fo in range(8)])
                    pr.barrier()
                    for stt in range(8):
                        fk = [R("fsl", f, stt, fo) for fo in range(8)]
                        rsd, nmr, skeys = ln_stats(
                            lambda stt=stt: fsl[:, stt * 2048:(stt + 1) * 2048], fk,
                            lambda: lntmp[:, :], R("lntmp"))
                        pr.emit("dve", lambda h, r=rsd, b=nmr, stt=stt: h.tensor_scalar(
                            ost[:, :], fsl[:, stt * 2048:(stt + 1) * 2048], r, b,
                            op0=ALU.mult, op1=ALU.add),
                            reads=fk + skeys, writes=[R("ost")], tag="ln4n")
                        pr.emit("dve", lambda h, stt=stt: h.scalar_tensor_tensor(
                            ost[:, :], x2sl[:, stt * 2048:(stt + 1) * 2048], 1.0,
                            ost[:, :], op0=ALU.mult, op1=ALU.add),
                            reads=[R("x2", stt), R("ost")], writes=[R("ost")], tag="resf")
                        row0 = tok0 + stt * 128
                        pr.dma("sp", "outst",
                               lambda h, r=row0: h.dma_start(out_h[r:r + 128, :],
                                                             ost[:, :]),
                               reads=[R("ost")], tag="outd")
                    for e_ in ENGINES:
                        pr.emit(e_, lambda h: h.nop(), writes=[R("ost")], tag="odrain")
                    pr.barrier()

        if stage <= 3:
            mixer_slab(0, stage)
        else:
            for f in range(NFS):
                for sub in range(2):
                    mixer_slab(f * 2 + sub, 9)
                ffn_slab(f)
                do_dump("x2", lambda: x2sl[:, :], [R("x2", j) for j in range(8)])
        dsem = pr.finalize(st)
        if dsem:
            with nc.Block() as blk:
                def _fin(h):
                    for s_, c_ in dsem.values():
                        h.wait_ge(s_, c_)
                blk.sync(_fin)
    return nc, dump_t


def _rsel():
    r = np.zeros((64, 32 * 128), np.float32)
    for j in range(32):
        r[j, j * 128:(j + 1) * 128] = 1.0
    return r.astype(BF)


def _host_prep(inputs):
    ip = np.ascontiguousarray
    W = inputs["L0_attn_in_proj_w"]
    main = W[:8448].reshape(66, 128, 16, 128).transpose(3, 0, 2, 1).reshape(128, -1)
    dtw = W[8448:].reshape(64, 16, 128).transpose(2, 1, 0).reshape(128, -1)
    per = {
        "wlay": ip(np.concatenate([main, dtw], axis=1)),
        "wout": ip(inputs["L0_attn_out_proj_w"].reshape(8, 256, 32, 128)
                   .transpose(3, 0, 2, 1).reshape(128, -1)),
        "w1": ip(inputs["L0_ffn_W1"].reshape(32, 128, 16, 128)
                 .transpose(3, 0, 2, 1).reshape(128, -1)),
        "w2": ip(inputs["L0_ffn_W2"].reshape(32, 128, 32, 128)
                 .transpose(3, 0, 2, 1).reshape(128, -1)),
        "w3": ip(inputs["L0_ffn_W3"].reshape(8, 256, 32, 128)
                 .transpose(3, 0, 2, 1).reshape(128, -1)),
        "cwt": ip(inputs["L0_attn_conv_w"].reshape(34, 128, 4)
                  .transpose(1, 0, 2).reshape(128, -1)),
        "b1t": ip(inputs["L0_ffn_b1"].reshape(32, 128).T),
        "b2t": ip(inputs["L0_ffn_b2"].reshape(32, 128).T),
        "b3r": ip(inputs["L0_ffn_b3"].reshape(1, -1)),
        "nwt": ip(inputs["L0_attn_norm_w"].reshape(32, 128).T),
        "dtb": ip(inputs["L0_attn_dt_bias"].reshape(64, 1)),
        "alog": ip(inputs["L0_attn_A_log"].reshape(64, 1)),
        "idbf": np.eye(128, dtype=BF),
        "idf": np.eye(128, dtype=np.float32),
        "triu": np.triu(np.ones((128, 128), np.float32)).astype(BF),
        "rep64": np.kron(np.eye(64, dtype=np.float32),
                         np.ones((1, 64), np.float32)).astype(BF),
        "rsel": _rsel(),
        "onesf": np.ones((1, 128), np.float32),
        "onescb": np.ones((128, 1), BF),
        "ones64": np.ones((64, 128), BF),
    }
    D = inputs["L0_attn_D"]
    dc = np.empty((128, 32), np.float32)
    dc[:64, :] = D[0::2][None, :]
    dc[64:, :] = D[1::2][None, :]
    per["dcol"] = dc
    return per


def kernel(**inputs):
    inputs = {k: np.asarray(v) for k, v in inputs.items()}
    per = _host_prep(inputs)
    nc, _ = build(bass.Bass("TRN2", target_bir_lowering=False))
    x = inputs["L0_attn_x"]
    in_maps = []
    for b in range(8):
        d = dict(per)
        d["x"] = np.ascontiguousarray(x[b])
        in_maps.append(d)
    res = run_bass_kernel_spmd(nc, in_maps, core_ids=list(range(8)))
    return np.stack([r["out"] for r in res.results]).astype(np.float32)


# version 12
# speedup vs baseline: 1.2145x; 1.0356x over previous
import contextlib
import numpy as np
import ml_dtypes
import concourse.bass as bass
import concourse.mybir as mybir
from concourse.bass_utils import run_bass_kernel_spmd

F32 = mybir.dt.float32
BF16 = mybir.dt.bfloat16
F16 = mybir.dt.float16
AF = mybir.ActivationFunctionType
ALU = mybir.AluOpType
BF = ml_dtypes.bfloat16

S, DM, HID, NST, HEADS, HP = 2048, 2048, 4096, 128, 64, 64
MS = 512
NMS = S // MS
FS = 1024
NFS = S // FS
CH = 128
NCH = MS // CH
NCC = 66
CLAMP = 30.0
EPS = 1e-5

ENGINES = ("pe", "dve", "act", "pool", "sp")
ENG_ATTR = {"pe": "tensor", "dve": "vector", "act": "scalar",
            "pool": "gpsimd", "sp": "sync"}


class Prog:

    PSBANK = {}

    def __init__(self, nc):
        self.nc = nc
        self.ops = []
        self.stream = {e: [] for e in ENGINES}
        self.last_writer = {}
        self.readers = {}
        self.bankw = {}
        self.epoch = 0

    def emit(self, eng, fn, reads=(), writes=(), tag=""):
        idx = len(self.ops)
        deps = set()
        for r in reads:
            w = self.last_writer.get(r)
            if w is not None:
                deps.add(w)
        for w in writes:
            lw = self.last_writer.get(w)
            if lw is not None:
                deps.add(lw)
            deps.update(self.readers.get(w, ()))
        for w in list(writes) + list(reads):
            bank = Prog.PSBANK.get(w[:2] if len(w) >= 2 else w)
            if bank is not None:
                bw = self.bankw.setdefault(bank, {})
                for e2, i2 in bw.items():
                    if e2 != eng:
                        deps.add(i2)
                bw[eng] = idx
        deps.discard(idx)
        self.ops.append(dict(idx=idx, eng=eng, fn=fn, deps=sorted(deps),
                             epoch=self.epoch, need_inc=False, dma=None, tag=tag))
        self.stream[eng].append(idx)
        for r in reads:
            self.readers.setdefault(r, []).append(idx)
        for w in writes:
            self.last_writer[w] = idx
            self.readers[w] = []
        return idx

    def dma(self, eng, slot, fn, reads=(), writes=(), tag=""):
        idx = self.emit(eng, fn, reads=reads, writes=writes, tag=tag)
        self.ops[idx]["dma"] = slot
        return idx

    def barrier(self):
        self.epoch += 1

    def finalize(self, stack):
        nc = self.nc
        eng_sem = {e: stack.enter_context(nc.semaphore(f"es_{e}")) for e in ENGINES}
        slots = sorted({op["dma"] for op in self.ops if op["dma"] is not None})
        dsem = {s: [stack.enter_context(nc.semaphore(f"ds_{s}")), 0] for s in slots}
        for op in self.ops:
            for d in op["deps"]:
                dop = self.ops[d]
                if dop["dma"] is None and not (dop["eng"] == "pe" and op["eng"] == "pe"):
                    dop["need_inc"] = True
        cnt = {e: 0 for e in ENGINES}
        for e in ENGINES:
            for i in self.stream[e]:
                op = self.ops[i]
                if op["dma"] is not None:
                    dsem[op["dma"]][1] += 16
                    op["sem"], op["semval"], op["incby"] = dsem[op["dma"]][0], dsem[op["dma"]][1], 16
                elif op["need_inc"]:
                    cnt[e] += 1
                    op["sem"], op["semval"], op["incby"] = eng_sem[e], cnt[e], 1
                else:
                    op["sem"] = None
        for e in ENGINES:
            cov = {}
            for i in self.stream[e]:
                op = self.ops[i]
                wmax = {}
                for d in op["deps"]:
                    dop = self.ops[d]
                    if dop["dma"] is None:
                        if dop["eng"] == "pe" and e == "pe":
                            continue
                        if dop["epoch"] < op["epoch"]:
                            continue
                    sem, val = dop["sem"], dop["semval"]
                    k = id(sem)
                    if cov.get(k, 0) >= val:
                        continue
                    if k not in wmax or wmax[k][1] < val:
                        wmax[k] = (sem, val)
                for k, (sem, val) in wmax.items():
                    cov[k] = max(cov.get(k, 0), val)
                op["waits"] = list(wmax.values())
        by_ep = {}
        for e in ENGINES:
            for i in self.stream[e]:
                by_ep.setdefault((self.ops[i]["epoch"], e), []).append(i)
        for ep in range(self.epoch + 1):
            engs = [e for e in ENGINES if (ep, e) in by_ep]
            if not engs:
                continue
            with nc.Block() as blk:
                for e in engs:
                    idxs = by_ep[(ep, e)]

                    def body(h, idxs=idxs):
                        for i in idxs:
                            op = self.ops[i]
                            for sem, val in op["waits"]:
                                h.wait_ge(sem, val)
                            inst = op["fn"](h)
                            if op["sem"] is not None:
                                inst.then_inc(op["sem"], op["incby"])
                    getattr(blk, ENG_ATTR[e])(body)
        return dsem


def ap3(t, base, s1, n1, s2, n2, p0=0, pn=128):
    rl = t.shape[1]
    return bass.AP(t, p0 * rl + base, [[rl, pn], [s1, n1], [s2, n2]])


def ap2(t, base, s1, n1, p0, pn):
    rl = t.shape[1]
    return bass.AP(t, p0 * rl + base, [[rl, pn], [s1, n1]])


def build(nc, dumps=(), stage=9):
    pr = Prog(nc)
    dumps = set(dumps)
    dump_t = {}
    R = lambda *a: a

    def dram(name, shape, dtype, kind):
        return nc.dram_tensor(name, shape, dtype, kind=kind).ap()

    x_h = dram("x", [S, DM], F32, "ExternalInput")
    out_h = dram("out", [S, DM], F32, "ExternalOutput")
    wlay_h = dram("wlay", [128, NCC * 2048 + 1024], F32, "ExternalInput")
    wout_h = dram("wout", [128, 8 * 8192], F32, "ExternalInput")
    w1_h = dram("w1", [128, 32 * 2048], F32, "ExternalInput")
    w2_h = dram("w2", [128, 32 * 4096], F32, "ExternalInput")
    w3_h = dram("w3", [128, 8 * 8192], F32, "ExternalInput")
    cwt_h = dram("cwt", [128, 136], F32, "ExternalInput")
    b1_h = dram("b1t", [128, 32], F32, "ExternalInput")
    b2_h = dram("b2t", [128, 32], F32, "ExternalInput")
    b3_h = dram("b3r", [1, DM], F32, "ExternalInput")
    nw_h = dram("nwt", [128, 32], F32, "ExternalInput")
    dcol_h = dram("dcol", [128, 32], F32, "ExternalInput")
    dtb_h = dram("dtb", [64, 1], F32, "ExternalInput")
    alog_h = dram("alog", [64, 1], F32, "ExternalInput")
    idbf_h = dram("idbf", [128, 128], BF16, "ExternalInput")
    idf_h = dram("idf", [128, 128], F32, "ExternalInput")
    triu_h = dram("triu", [128, 128], BF16, "ExternalInput")
    rep_h = dram("rep64", [64, 4096], BF16, "ExternalInput")
    rsel_h = dram("rsel", [64, 4096], BF16, "ExternalInput")
    onesf_h = dram("onesf", [1, 128], F32, "ExternalInput")
    onescb_h = dram("onescb", [128, 1], BF16, "ExternalInput")
    ones64_h = dram("ones64", [64, 128], BF16, "ExternalInput")

    DUMP_SHAPES = {
        "xbcz": ([128, NCC * MS], BF16), "dt": ([64, MS], F32),
        "dAc": ([64, MS], F32), "gsl": ([128, 32 * MS], BF16),
        "vsl": ([128, NMS * 2048], F16), "x2": ([128, 8 * 2048], BF16),
        "h1T": ([128, 32 * FS], BF16), "fsl": ([128, 8 * 2048], BF16),
        "xhT": ([128, 4096], BF16), "carry": ([128, 4096], BF16),
        "yrw1": ([128, 2048], BF16), "rsb1": ([128, 128], F32),
        "cds0": ([128, 512], BF16), "cmy0": ([128, 512], BF16),
        "cdq0": ([64, 512], BF16), "st0": ([128, 512], BF16),
        "bsb0": ([128, 128], BF16), "xhd0": ([128, 512], BF16), "dtd0": ([128, 512], BF16),
        "ddtbf0": ([64, 512], BF16),
        "carry0": ([128, 4096], BF16), "esb1": ([128, 512], BF16),
    }
    for nm in dumps:
        shp, dt_ = DUMP_SHAPES[nm]
        dump_t[nm] = dram("dump_" + nm, shp, dt_, "ExternalOutput")

    def do_dump(name, src_fn, reads):
        t = dump_t.get(name)
        if t is not None:
            pr.dma("sp", "dmp_" + name, lambda h, t=t: h.dma_start(t, src_fn()),
                   reads=list(reads), tag="dump")

    st = contextlib.ExitStack()
    with st:
        def sb(n, shape, dt_, side="left", stk=st):
            return stk.enter_context(nc.sbuf_tensor("s_" + n, shape, dt_, side=side))

        PY = st.enter_context(nc.psum_tensor("PY", [128, 2048], F32))
        PA = st.enter_context(nc.psum_tensor("PA", [128, 512], F32))
        PB = st.enter_context(nc.psum_tensor("PB", [128, 512], F32))
        PC = st.enter_context(nc.psum_tensor("PC", [128, 512], F32))
        PT = st.enter_context(nc.psum_tensor("PT", [128, 1024], BF16))
        for _t in range(16):
            Prog.PSBANK[("PY", _t)] = _t // 4
        Prog.PSBANK[("PAa",)] = 4
        Prog.PSBANK[("PAs",)] = 4
        Prog.PSBANK[("PAr",)] = 4
        Prog.PSBANK[("PAF",)] = 4
        for _j in range(4):
            Prog.PSBANK[("PB", _j)] = 5
            Prog.PSBANK[("PC", _j)] = 6
        for _j in range(8):
            Prog.PSBANK[("PT", _j)] = 7

        idbf = sb("idbf", [128, 128], BF16)
        idf = sb("idf", [128, 128], F32)
        triu = sb("triu", [128, 128], BF16)
        onesf = sb("onesf", [1, 128], F32)
        onescb = sb("onescb", [128, 1], BF16)
        ones64 = sb("ones64", [64, 128], BF16)
        cwt = sb("cwt", [128, 136], F32)
        b1t = sb("b1t", [128, 32], F32)
        b2t = sb("b2t", [128, 32], F32)
        nwt = sb("nwt", [128, 32], F32)
        dcol = sb("dcol", [128, 32], F32)
        dtb = sb("dtb", [64, 1], F32)
        alog = sb("alog", [64, 1], F32)
        acol = sb("acol", [64, 1], F32)
        carry = sb("carry", [128, 4096], BF16)
        tails = sb("tails", [128, 102], F32)
        lntmp = sb("lntmp", [128, 2048], BF16)
        stat = sb("stat", [128, 16], F32)
        srow = sb("srow", [1, 128], F32)
        x2sl = sb("x2sl", [128, 8 * 2048], BF16, side="right")

        def cload(dst, src, slot):
            pr.dma("sp", slot, lambda h, d=dst, s=src: h.dma_start(d[:, :], s),
                   writes=[R(slot)], tag="cload")
        cload(idbf, idbf_h, "c_idbf"); cload(idf, idf_h, "c_idf")
        cload(triu, triu_h, "c_triu")
        cload(onesf, onesf_h, "c_onesf")
        cload(onescb, onescb_h, "c_onescb"); cload(ones64, ones64_h, "c_ones64")
        cload(cwt, cwt_h, "c_cwt"); cload(b1t, b1_h, "c_b1")
        cload(b2t, b2_h, "c_b2"); cload(nwt, nw_h, "c_nw")
        cload(dcol, dcol_h, "c_dcol"); cload(dtb, dtb_h, "c_dtb")
        cload(alog, alog_h, "c_alog")
        pr.emit("act", lambda h: h.activation(acol[:, :], alog[:, :], AF.Exp),
                reads=[R("c_alog")], writes=[R("acol")])
        pr.emit("dve", lambda h: h.tensor_scalar_mul(acol[:, :], acol[:, :], -1.0),
                reads=[R("acol")], writes=[R("acol")])
        pr.emit("dve", lambda h: h.memset(tails[:, :], 0.0),
                writes=[R("tails", c) for c in range(34)])
        pr.emit("dve", lambda h: h.memset(carry[:, :], 0.0),
                writes=[R("carry", t) for t in range(32)])
        pr.barrier()

        ln_ctr = [0]

        def ln_stats(src_fn, src_keys, scrap_fn, scrap_key):
            k = (ln_ctr[0] % 2) * 5
            ln_ctr[0] += 1
            ssum, ssq, va, rsd, mm_ = k, k + 1, k + 2, k + 3, k + 4
            K = lambda j: R("stat", j)
            pr.emit("dve", lambda h: h.tensor_scalar(
                scrap_fn(), src_fn(), 1.0, 0.0, op0=ALU.mult, op1=ALU.add,
                accum_out=stat[:, ssum:ssum + 1]),
                reads=list(src_keys), writes=[scrap_key, K(ssum)], tag="lns")
            pr.emit("act", lambda h: h.activation(
                scrap_fn(), src_fn(), AF.Square, accum_out=stat[:, ssq:ssq + 1]),
                reads=list(src_keys) + [scrap_key], writes=[scrap_key, K(ssq)], tag="lnq")
            N = 2048.0
            pr.emit("dve", lambda h: h.tensor_scalar(
                stat[:, va:va + 1], stat[:, ssq:ssq + 1], 1.0 / N, EPS,
                op0=ALU.mult, op1=ALU.add), reads=[K(ssq)], writes=[K(va)])
            pr.emit("dve", lambda h: h.tensor_scalar_mul(
                stat[:, mm_:mm_ + 1], stat[:, ssum:ssum + 1], 1.0 / N),
                reads=[K(ssum)], writes=[K(mm_)])
            pr.emit("dve", lambda h: h.tensor_scalar_mul(
                stat[:, ssum:ssum + 1], stat[:, ssum:ssum + 1], -1.0 / N),
                reads=[K(ssum)], writes=[K(ssum)])
            pr.emit("dve", lambda h: h.scalar_tensor_tensor(
                stat[:, va:va + 1], stat[:, mm_:mm_ + 1], stat[:, ssum:ssum + 1],
                stat[:, va:va + 1], op0=ALU.mult, op1=ALU.add),
                reads=[K(mm_), K(ssum), K(va)], writes=[K(va)])
            pr.emit("act", lambda h: h.activation(
                stat[:, rsd:rsd + 1], stat[:, va:va + 1], AF.Ln),
                reads=[K(va)], writes=[K(rsd)])
            pr.emit("act", lambda h: h.activation(
                stat[:, rsd:rsd + 1], stat[:, rsd:rsd + 1], AF.Exp, scale=-0.5),
                reads=[K(rsd)], writes=[K(rsd)])
            pr.emit("dve", lambda h: h.tensor_scalar(
                stat[:, ssum:ssum + 1], stat[:, ssum:ssum + 1],
                stat[:, rsd:rsd + 1], None, op0=ALU.mult),
                reads=[K(ssum), K(rsd)], writes=[K(ssum)])
            return stat[:, rsd:rsd + 1], stat[:, ssum:ssum + 1], [K(rsd), K(ssum)]

        PSL = [("PY", 0), ("PY", 512), ("PY", 1024), ("PY", 1536),
               ("PA", 0), ("PB", 0), ("PC", 0)]

        def psum_of(name):
            return {"PY": PY, "PA": PA, "PB": PB, "PC": PC}[name]

        def pskeys(name, off, width=512):
            if name == "PY":
                return [R("PY", off // 128 + j) for j in range(width // 128)]
            if name == "PA":
                return [R("PAF")]
            return [R(name, off // 128 + j) for j in range(width // 128)]

        def mixer_slab(m, mstage):
            tok0 = m * MS
            ms = contextlib.ExitStack()
            with ms:
                msb = lambda n, s_, d, side="left": sb(f"{n}_{m}", s_, d, side, ms)
                dtF = msb("dtF", [64, MS], F32)
                dAcF = msb("dAcF", [64, MS], F32)
                colBF = msb("colBF", [64, MS], F32)
                ddtF = msb("ddtF", [64, MS], F32)
                zer64 = msb("zer64", [64, 128], F32)
                ddtbf = msb("ddtbf", [64, MS], BF16)
                dachi = msb("dachi", [128, MS], BF16)
                daclo = msb("daclo", [128, MS], BF16)
                dachi2 = msb("dachi2", [64, MS], BF16)
                daclo2 = msb("daclo2", [64, MS], BF16)
                ephbf = msb("ephbf", [64, MS], BF16)
                ephF = msb("ephF", [64, MS], F32)
                gsl = msb("gsl", [128, 32 * MS], BF16, side="right")
                kz = lambda cc: R("xbcz", m, cc)
                xkey, gkeys = R("xst", m), [R("gsl", m, c, H)
                                           for c in range(NCH) for H in range(2)]

                sx = ms.enter_context(contextlib.ExitStack())
                xbcz = sb(f"xbcz_{m}", [128, NCC * MS], BF16, "left", sx)

                with contextlib.ExitStack() as s1:
                    ln1T = sb(f"ln1T_{m}", [128, 16 * MS], BF16, "left", s1)
                    wst = [sb(f"wst{i}_{m}", [128, 2048], BF16, "left", s1) for i in range(2)]
                    rawx = sb(f"rawx_{m}", [128, 1032], F32, "left", s1)
                    cvt = [sb(f"cvt{i}_{m}", [128, 512], F32, "left", s1) for i in range(2)]
                    xst = sb(f"xst_{m}", [128, 2048], F32, "left", s1)
                    lnkeys = []
                    for stt in range(NMS):
                        row0 = tok0 + stt * 128
                        pr.dma("sp", "xld",
                               lambda h, r=row0: h.dma_start(xst[:, :], x_h[r:r + 128, :]),
                               writes=[xkey], tag="xld")
                        rsd, nmr, skeys = ln_stats(lambda: xst[:, :], [xkey],
                                                   lambda: lntmp[:, :], R("lntmp"))
                        pr.emit("dve", lambda h, r=rsd, b=nmr: h.tensor_scalar(
                            lntmp[:, :], xst[:, :], r, b, op0=ALU.mult, op1=ALU.add),
                            reads=[xkey] + skeys, writes=[R("lntmp")], tag="ln1n")
                        for g in range(4):
                            for j in range(4):
                                kt = g * 4 + j
                                pr.emit("pe", lambda h, kt=kt, j=j: h.transpose(
                                    PT[:, j * 128:(j + 1) * 128],
                                    lntmp[:, kt * 128:(kt + 1) * 128], idbf[:, :]),
                                    reads=[R("lntmp"), R("c_idbf")],
                                    writes=[R("PT", j)], tag="ln1t")
                            pr.emit("dve", lambda h, g=g, stt=stt: h.tensor_copy(
                                ap3(ln1T, g * 4 * MS + stt * 128, MS, 4, 1, 128),
                                PT[:, 0:512]),
                                reads=[R("PT", j) for j in range(4)],
                                writes=[R("ln1T", m, g, stt)], tag="ln1e")
                        lnkeys += [R("ln1T", m, g, stt) for g in range(4)]
                    pr.barrier()

                    for cc in range(NCC + 1):
                        slot = cc % 2
                        if cc < NCC:
                            src = wlay_h[:, cc * 2048:(cc + 1) * 2048]
                            dst = wst[slot][:, 0:2048]
                        else:
                            src = wlay_h[:, NCC * 2048:NCC * 2048 + 1024]
                            dst = wst[slot][:, 0:1024]
                        pr.dma("pool", f"wst{slot}",
                               lambda h, d=dst, s=src: h.dma_start(d, s),
                               writes=[R("wst", slot)], tag=f"wld{cc}")
                        pn, off = PSL[cc % 7]
                        pt = psum_of(pn)
                        pkeys = pskeys(pn, off)
                        wid = 64 if cc == NCC else 128
                        for kt in range(16):
                            pr.emit("pe", lambda h, kt=kt, pt=pt, off=off, w=wid, sl=slot:
                                    h.matmul(pt[0:w, off:off + 512],
                                             wst[sl][:, kt * w:(kt + 1) * w],
                                             ln1T[:, kt * MS:(kt + 1) * MS],
                                             start=(kt == 0), stop=(kt == 15)),
                                    reads=[R("wst", slot)] + lnkeys,
                                    writes=pkeys, tag=f"ip{cc}")
                        if cc < 32:
                            pr.emit("act", lambda h, pt=pt, off=off, cc=cc: h.activation(
                                xbcz[:, cc * MS:(cc + 1) * MS], pt[:, off:off + 512],
                                AF.Silu), reads=pkeys, writes=[kz(cc)], tag="zev")
                        elif cc < NCC:
                            ccv = cc - 32
                            par = ccv % 2
                            rx0 = par * 516
                            cv = cvt[par]
                            ck = R("cvt", par)
                            rk0, rk1 = R("rawx", par, 0), R("rawx", par, 1)
                            pr.emit("dve", lambda h, ccv=ccv, rx0=rx0: h.tensor_copy(
                                rawx[:, rx0:rx0 + 3], tails[:, ccv * 3:ccv * 3 + 3]),
                                reads=[R("tails", ccv)], writes=[rk0], tag="cvt0")
                            pr.emit("act", lambda h, pt=pt, off=off, rx0=rx0: h.copy(
                                rawx[:, rx0 + 3:rx0 + 515], pt[:, off:off + 512]),
                                reads=pkeys, writes=[rk1], tag="cvtr")
                            pr.emit("dve", lambda h, ccv=ccv, rx0=rx0: h.tensor_copy(
                                tails[:, ccv * 3:ccv * 3 + 3], rawx[:, rx0 + 512:rx0 + 515]),
                                reads=[rk1], writes=[R("tails", ccv)], tag="cvnt")
                            w = lambda j, ccv=ccv: cwt[:, ccv * 4 + j:ccv * 4 + j + 1]
                            pr.emit("dve", lambda h, w=w, cv=cv, rx0=rx0: h.tensor_scalar(
                                cv[:, :], rawx[:, rx0 + 1:rx0 + 513], w(1), None,
                                op0=ALU.mult),
                                reads=[rk0, rk1, R("c_cwt")], writes=[ck], tag="cva1")
                            pr.emit("dve", lambda h, w=w, cv=cv, rx0=rx0:
                                    h.scalar_tensor_tensor(
                                        cv[:, :], rawx[:, rx0:rx0 + 512], w(0), cv[:, :],
                                        op0=ALU.mult, op1=ALU.add),
                                    reads=[rk0, rk1, R("c_cwt"), ck], writes=[ck], tag="cva0")
                            pr.emit("dve", lambda h, w=w, cv=cv, rx0=rx0:
                                    h.scalar_tensor_tensor(
                                        cv[:, :], rawx[:, rx0 + 3:rx0 + 515], w(3), cv[:, :],
                                        op0=ALU.mult, op1=ALU.add),
                                    reads=[rk1, R("c_cwt"), ck], writes=[ck], tag="cvb3")
                            pr.emit("dve", lambda h, w=w, cv=cv, rx0=rx0:
                                    h.scalar_tensor_tensor(
                                        cv[:, :], rawx[:, rx0 + 2:rx0 + 514], w(2), cv[:, :],
                                        op0=ALU.mult, op1=ALU.add),
                                    reads=[rk0, rk1, R("c_cwt"), ck], writes=[ck], tag="cvb2")
                            pr.emit("act", lambda h, cc=cc, cv=cv: h.activation(
                                xbcz[:, cc * MS:(cc + 1) * MS], cv[:, :], AF.Silu),
                                reads=[ck], writes=[kz(cc)], tag="cvs")
                        else:
                            t0k, t1k = R("dt0"), R("dt1")
                            pr.emit("act", lambda h, pt=pt, off=off: h.activation(
                                cvt[0][0:64, :], pt[0:64, off:off + 512], AF.Abs,
                                bias=dtb[:, :]),
                                reads=pkeys + [R("c_dtb")], writes=[t0k], tag="dtab")
                            pr.emit("act", lambda h: h.activation(
                                cvt[0][0:64, :], cvt[0][0:64, :], AF.Exp, scale=-1.0),
                                reads=[t0k], writes=[t0k], tag="dtex")
                            pr.emit("act", lambda h: h.activation(
                                cvt[0][0:64, :], cvt[0][0:64, :], AF.Ln, bias=1.0),
                                reads=[t0k], writes=[t0k], tag="dtln")
                            pr.emit("act", lambda h, pt=pt, off=off: h.activation(
                                cvt[1][0:64, :], pt[0:64, off:off + 512], AF.Relu,
                                bias=dtb[:, :]),
                                reads=pkeys + [R("c_dtb")], writes=[t1k], tag="dtre")
                            pr.emit("dve", lambda h: h.tensor_add(
                                dtF[:, :], cvt[0][0:64, :], cvt[1][0:64, :]),
                                reads=[t0k, t1k], writes=[R("dt", m)], tag="dtad")
                            pr.emit("act", lambda h: h.activation(
                                colBF[:, :], dtF[:, :], AF.Ln),
                                reads=[R("dt", m)], writes=[R("lndt", m)], tag="dtld")
                    do_dump("xbcz", lambda: xbcz[:, :], [kz(c) for c in range(NCC)])
                    do_dump("dt", lambda: dtF[:, :], [R("dt", m)])
                    pr.barrier()
                if mstage < 2:
                    return

                pr.emit("dve", lambda h: h.tensor_scalar(
                    ddtF[:, :], dtF[:, :], acol[:, :], None, op0=ALU.mult),
                    reads=[R("dt", m), R("acol")], writes=[R("dA", m)], tag="dA")
                pr.emit("dve", lambda h: h.memset(zer64[:, :], 0.0),
                        writes=[R("z64", m)], tag="z64")
                pr.emit("dve", lambda h: h.memset(dachi[64:128, :], 0.0),
                        writes=[R("dahz", m)], tag="hz")
                pr.emit("dve", lambda h: h.memset(daclo[64:128, :], 0.0),
                        writes=[R("dalz", m)], tag="lz")
                dack = [R("dAc", m, c) for c in range(NCH)]
                for c in range(NCH):
                    pr.emit("dve", lambda h, c=c: h.tensor_tensor_scan(
                        dAcF[:, c * CH:(c + 1) * CH],
                        ddtF[:, c * CH:(c + 1) * CH],
                        zer64[:, :], 0.0, op0=ALU.add, op1=ALU.add),
                        reads=[R("dA", m), R("z64", m)], writes=[R("dAc", m, c)], tag="cums")
                pr.emit("dve", lambda h: h.tensor_sub(
                    colBF[:, :], colBF[:, :], dAcF[:, :]),
                    reads=[R("lndt", m)] + dack, writes=[R("colB", m)], tag="colB")
                pr.emit("act", lambda h: h.activation(
                    ephF[:, :], dAcF[:, :], AF.Exp),
                    reads=dack, writes=[R("eph", m)], tag="eph")
                pr.emit("dve", lambda h: h.tensor_copy(ephbf[:, :], ephF[:, :]),
                        reads=[R("eph", m)], writes=[R("ephbf", m)], tag="ephb")
                pr.emit("dve", lambda h: h.tensor_copy(dachi[0:64, :], dAcF[:, :]),
                        reads=dack, writes=[R("dachi", m)], tag="hib")
                pr.emit("dve", lambda h: h.scalar_tensor_tensor(
                    daclo[0:64, :], dachi[0:64, :], -1.0, dAcF[:, :],
                    op0=ALU.mult, op1=ALU.add),
                    reads=[R("dachi", m)] + dack, writes=[R("daclo", m)], tag="lob")
                pr.dma("pool", "dh2", lambda h: h.dma_start(
                    dachi2[0:64, :], dachi[32:96, :]),
                    reads=[R("dachi", m), R("dahz", m)], writes=[R("dachi2", m)], tag="dh2")
                pr.dma("pool", "dl2", lambda h: h.dma_start(
                    daclo2[0:64, :], daclo[32:96, :]),
                    reads=[R("daclo", m), R("dalz", m)], writes=[R("daclo2", m)], tag="dl2")
                for c in range(NCH):
                    last = c * CH + CH - 1
                    pr.emit("dve", lambda h, c=c, last=last: h.tensor_scalar(
                        ddtF[:, c * CH:(c + 1) * CH],
                        dAcF[:, c * CH:(c + 1) * CH],
                        dAcF[:, last:last + 1], -1.0,
                        op0=ALU.subtract, op1=ALU.mult),
                        reads=[R("dAc", m, c)],
                        writes=[R("ddt", m, c), R("dA", m)], tag="dd1")
                    pr.emit("act", lambda h, c=c: h.activation(
                        ddtF[:, c * CH:(c + 1) * CH],
                        ddtF[:, c * CH:(c + 1) * CH], AF.Exp),
                        reads=[R("ddt", m, c)], writes=[R("ddt", m, c)], tag="dd2")
                    pr.emit("dve", lambda h, c=c: h.tensor_mul(
                        ddtF[:, c * CH:(c + 1) * CH],
                        ddtF[:, c * CH:(c + 1) * CH],
                        dtF[:, c * CH:(c + 1) * CH]),
                        reads=[R("ddt", m, c), R("dt", m)],
                        writes=[R("ddt", m, c)], tag="dd3")
                pr.emit("dve", lambda h: h.tensor_copy(ddtbf[:, :], ddtF[:, :]),
                        reads=[R("ddt", m, c) for c in range(NCH)],
                        writes=[R("ddtbf", m)], tag="ddb")
                do_dump("dAc", lambda: dAcF[:, :], dack)
                pr.barrier()

                with contextlib.ExitStack() as s2:
                    rsel = sb(f"rsel_{m}", [64, 4096], BF16, "left", s2)
                    rep64 = sb(f"rep64_{m}", [64, 4096], BF16, "left", s2)
                    xhT = sb(f"xhT_{m}", [128, 4096], BF16, "left", s2)
                    xhd = sb(f"xhd_{m}", [128, 1024], BF16, "left", s2)
                    mtr = sb(f"mtr_{m}", [128, 1024], BF16, "left", s2)
                    gtm = sb(f"gtm_{m}", [128, 128], BF16, "left", s2)
                    bsb = sb(f"bsb_{m}", [128, 128], BF16, "left", s2)
                    cbt = sb(f"cbt_{m}", [128, 64], F32, "left", s2)
                    esb = sb(f"esb_{m}", [128, 512], BF16, "left", s2)
                    yrw = sb(f"yrw_{m}", [128, 4096], BF16, "left", s2)
                    qeb = sb(f"qeb_{m}", [128, 512], BF16, "left", s2)
                    dxh = sb(f"dxh_{m}", [128, 2048], BF16, "left", s2)
                    cdq = sb(f"cdq_{m}", [64, 512], BF16, "left", s2)
                    cmy = sb(f"cmy_{m}", [128, 512], BF16, "left", s2)
                    rsb = sb(f"rsb_{m}", [128, 128], F32, "left", s2)
                    pr.dma("sp", "rsel", lambda h: h.dma_start(rsel[:, :], rsel_h),
                           writes=[R("c_rsel")], tag="rsl")
                    pr.dma("sp", "rep", lambda h: h.dma_start(rep64[:, :], rep_h),
                           writes=[R("c_rep")], tag="repl")

                    xhTk = [R("xhT", g) for g in range(8)]

                    def em_prep(c):
                        for g in range(8):
                            for j in range(4):
                                t = g * 4 + j
                                pr.emit("pe", lambda h, t=t, j=j, c=c: h.transpose(
                                    PT[:, j * 128:(j + 1) * 128],
                                    xbcz[:, (32 + t) * MS + c * CH:
                                         (32 + t) * MS + c * CH + 128],
                                    idbf[:, :]),
                                    reads=[kz(32 + t), R("c_idbf")],
                                    writes=[R("PT", j)], tag="xhTt")
                            pr.emit("dve", lambda h, g=g: h.tensor_copy(
                                xhT[:, g * 512:(g + 1) * 512], PT[:, 0:512]),
                                reads=[R("PT", j) for j in range(4)],
                                writes=[R("xhT", g)], tag="xhTe")
                        pr.emit("pe", lambda h, c=c: h.transpose(
                            PT[:, 0:128],
                            xbcz[:, 64 * MS + c * CH:64 * MS + c * CH + 128],
                            idbf[:, :]),
                            reads=[kz(64), R("c_idbf")], writes=[R("PT", 0)], tag="Btt")
                        pr.emit("dve", lambda h: h.tensor_copy(bsb[:, :], PT[:, 0:128]),
                                reads=[R("PT", 0)], writes=[R("bsb")], tag="Bte")
                        pr.emit("pe", lambda h, c=c: h.matmul(
                            PA[:, 0:64], colBF[:, c * CH:(c + 1) * CH],
                            idf[0:64, 0:64], start=True, stop=True),
                            reads=[R("colB", m), R("c_idf")], writes=[R("PAa")], tag="cbtt")
                        pr.emit("dve", lambda h: h.tensor_copy(cbt[:, :], PA[:, 0:64]),
                                reads=[R("PAa")], writes=[R("cbt")], tag="cbte")
                        pr.emit("pe", lambda h, c=c: h.matmul(
                            PB[:, 0:128],
                            xbcz[:, 64 * MS + c * CH:64 * MS + c * CH + 128],
                            xbcz[:, 65 * MS + c * CH:65 * MS + c * CH + 128],
                            start=True, stop=True),
                            reads=[kz(64), kz(65)], writes=[R("PB", 0)], tag="Gt")
                        pr.emit("dve", lambda h: h.tensor_mul(
                            gtm[:, :], PB[:, 0:128], triu[:, :]),
                            reads=[R("PB", 0), R("c_triu")], writes=[R("gtm")], tag="Gtm")

                    def mt_units(s):
                        c, H = s // 2, s % 2
                        ys = s % 2

                        def em_bc(hh):
                            h_ = H * 32 + hh
                            pbs = hh % 4
                            hi_t = dachi if h_ < 32 else dachi2
                            lo_t = daclo if h_ < 32 else daclo2
                            hi_k = R("dachi", m) if h_ < 32 else R("dachi2", m)
                            lo_k = R("daclo", m) if h_ < 32 else R("daclo2", m)
                            pr.emit("pe", lambda h, h_=h_, pbs=pbs, t=hi_t, c=c:
                                    h.matmul(PB[:, pbs * 128:(pbs + 1) * 128],
                                             rsel[:, (h_ % 32) * 128:(h_ % 32 + 1) * 128],
                                             ap2(t, c * CH, 1, CH, 0, 64),
                                             start=True, stop=False),
                                    reads=[hi_k, R("dahz", m), R("c_rsel")],
                                    writes=[R("PB", pbs)], tag="bch")
                            pr.emit("pe", lambda h, h_=h_, pbs=pbs, t=lo_t, c=c:
                                    h.matmul(PB[:, pbs * 128:(pbs + 1) * 128],
                                             rsel[:, (h_ % 32) * 128:(h_ % 32 + 1) * 128],
                                             ap2(t, c * CH, 1, CH, 0, 64),
                                             start=False, stop=True),
                                    reads=[lo_k, R("dalz", m), R("c_rsel")],
                                    writes=[R("PB", pbs)], tag="bcl")

                        def em_seg(hh):
                            h_ = H * 32 + hh
                            pbs = pcs = hh % 4
                            pr.emit("dve", lambda h, pbs=pbs, pcs=pcs, h_=h_:
                                    h.tensor_scalar(
                                        PC[:, pcs * 128:(pcs + 1) * 128],
                                        PB[:, pbs * 128:(pbs + 1) * 128],
                                        cbt[:, h_ % 64:h_ % 64 + 1], CLAMP,
                                        op0=ALU.add, op1=ALU.min),
                                    reads=[R("PB", pbs), R("cbt")],
                                    writes=[R("PC", pcs)], tag="seg")

                        def em_exp4(g):
                            mts = g % 2
                            for j in range(4):
                                pr.emit("act", lambda h, mts=mts, j=j: h.activation(
                                    mtr[:, mts * 512 + j * 128:mts * 512 + (j + 1) * 128],
                                    PC[:, j * 128:(j + 1) * 128], AF.Exp),
                                    reads=[R("PC", j)],
                                    writes=[R("mtr", mts)], tag="exp")

                        def em_mtm4(g):
                            mts = g % 2
                            pr.emit("dve", lambda h, mts=mts: h.tensor_mul(
                                mtr[:, mts * 512:(mts + 1) * 512],
                                mtr[:, mts * 512:(mts + 1) * 512],
                                bass.AP(gtm, 0, [[128, 128], [0, 4], [1, 128]])),
                                reads=[R("mtr", mts), R("gtm")],
                                writes=[R("mtr", mts)], tag="mtm")

                        def em_ydg(hh):
                            h_ = H * 32 + hh
                            mts = (hh // 4) % 2
                            off = (hh % 4) * 128
                            pr.emit("pe", lambda h, h_=h_, hh=hh, mts=mts, off=off:
                                    h.matmul(
                                PY[(h_ % 2) * 64:(h_ % 2) * 64 + 64,
                                   (hh // 2) * 128:(hh // 2) * 128 + 128],
                                xhT[:, h_ * 64:(h_ + 1) * 64],
                                mtr[:, mts * 512 + off:mts * 512 + off + 128],
                                start=True, stop=True,
                                tile_position=(0, (h_ % 2) * 64)),
                                reads=[R("mtr", mts)] + xhTk,
                                writes=[R("PY", hh // 2)], tag="ydg")

                        units = []
                        for g in range(10):
                            def unit(g=g):
                                if 1 <= g < 9:
                                    for j in range(4):
                                        em_seg((g - 1) * 4 + j)
                                    em_exp4(g - 1)
                                if g < 8:
                                    for j in range(4):
                                        em_bc(g * 4 + j)
                                if 2 <= g:
                                    em_mtm4(g - 2)
                                    for j in range(4):
                                        em_ydg((g - 2) * 4 + j)
                            units.append(unit)

                        def yev():
                            pr.emit("act", lambda h, ys=ys: h.copy(
                                yrw[:, ys * 2048:(ys + 1) * 2048], PY[:, 0:2048]),
                                reads=[R("PY", t) for t in range(16)],
                                writes=[R("yrw", ys)], tag="yev")
                        return units, yev

                    def b_units(s):
                        c, H = s // 2, s % 2
                        hp0, t0 = H * 2048, H * 16
                        ys = s % 2
                        units = []

                        def uQ():
                            for t in range(16):
                                pr.emit("pe", lambda h, t=t, t0=t0, c=c: h.matmul(
                                    PY[:, t * 128:(t + 1) * 128],
                                    carry[:, (t0 + t) * 128:(t0 + t + 1) * 128],
                                    xbcz[:, 65 * MS + c * CH:65 * MS + c * CH + 128],
                                    start=True, stop=True),
                                    reads=[R("carry", t0 + t), kz(65)],
                                    writes=[R("PY", t)], tag="Q")
                        units.append(uQ)

                        def uDxh(qq):
                            for t in range(qq * 4, qq * 4 + 4):
                                pr.emit("dve", lambda h, t=t, t0=t0, c=c: h.tensor_scalar(
                                    dxh[:, t * 128:(t + 1) * 128],
                                    xbcz[:, (32 + t0 + t) * MS + c * CH:
                                         (32 + t0 + t) * MS + c * CH + 128],
                                    dcol[:, t0 + t:t0 + t + 1], None, op0=ALU.mult),
                                    reads=[kz(32 + t0 + t), R("c_dcol")],
                                    writes=[R("dxh", t)], tag="dxh")
                        for qq in range(4):
                            units.append(lambda qq=qq: uDxh(qq))

                        def uEqe(q):
                            for j in range(4):
                                t = q * 4 + j
                                pr.emit("pe", lambda h, t=t, j=j, t0=t0, c=c: h.matmul(
                                    PC[:, j * 128:(j + 1) * 128],
                                    rep64[:, (t0 + t) * 128:(t0 + t + 1) * 128],
                                    ephbf[:, c * CH:(c + 1) * CH],
                                    start=True, stop=True),
                                    reads=[R("c_rep"), R("ephbf", m)],
                                    writes=[R("PC", j)], tag="E")
                            pr.emit("act", lambda h: h.copy(esb[:, :], PC[:, 0:512]),
                                    reads=[R("PC", j) for j in range(4)],
                                    writes=[R("esb")], tag="Eev")
                            pr.emit("dve", lambda h, q=q: h.tensor_mul(
                                qeb[:, :],
                                PY[:, q * 512:(q + 1) * 512], esb[:, :]),
                                reads=[R("PY", q * 4 + j) for j in range(4)] + [R("esb")],
                                writes=[R("qeb", 0)], tag="qe")
                            pr.emit("dve", lambda h, q=q, ys=ys:
                                    h.scalar_tensor_tensor(
                                        yrw[:, ys * 2048 + q * 512:
                                            ys * 2048 + (q + 1) * 512],
                                        qeb[:, :], 1.0,
                                        yrw[:, ys * 2048 + q * 512:
                                            ys * 2048 + (q + 1) * 512],
                                        op0=ALU.mult, op1=ALU.add),
                                    reads=[R("qeb", 0), R("yrw", ys)],
                                    writes=[R("yrw", ys)], tag="add1")
                        for q in range(4):
                            units.append(lambda q=q: uEqe(q))

                        def uSt(q):
                            pr.emit("pe", lambda h, q=q, hp0=hp0, c=c: h.matmul(
                                PC[:, 0:512],
                                ap2(ddtbf, c * CH, 1, CH, 0, 64),
                                rep64[:, hp0 + q * 512:hp0 + (q + 1) * 512],
                                start=True, stop=True),
                                reads=[R("ddtbf", m), R("c_rep")],
                                writes=[R("PC", j) for j in range(4)], tag="dtd")
                            pr.emit("dve", lambda h, q=q, hp0=hp0: h.tensor_mul(
                                xhd[:, (q % 2) * 512:(q % 2 + 1) * 512],
                                xhT[:, hp0 + q * 512:hp0 + (q + 1) * 512],
                                PC[:, 0:512]),
                                reads=[R("PC", j) for j in range(4)] + xhTk,
                                writes=[R("xhd", q % 2)], tag="xhd")
                            for j in range(8):
                                hh = q * 8 + j
                                pr.emit("pe", lambda h, hh=hh, q=q: h.matmul(
                                    PY[:, hh * 64:(hh + 1) * 64],
                                    bsb[:, :],
                                    xhd[:, (q % 2) * 512 + (hh % 8) * 64:
                                        (q % 2) * 512 + (hh % 8 + 1) * 64],
                                    start=True, stop=True),
                                    reads=[R("bsb"), R("xhd", q % 2)],
                                    writes=[R("PY", hh // 2)], tag="st")
                        for q in range(4):
                            units.append(lambda q=q: uSt(q))

                        def uCarry(q):
                            pr.emit("dve", lambda h, q=q, hp0=hp0, c=c: h.tensor_scalar(
                                cdq[:, :],
                                rep64[0:64, hp0 + q * 512:hp0 + (q + 1) * 512],
                                ephF[:, c * CH + 127:c * CH + 128], None,
                                op0=ALU.mult),
                                reads=[R("c_rep"), R("eph", m)],
                                writes=[R("cdq")], tag="cdq")
                            pr.emit("pe", lambda h: h.matmul(
                                PC[:, 0:512], ones64[:, :], cdq[:, :],
                                start=True, stop=True),
                                reads=[R("cdq"), R("c_ones64")],
                                writes=[R("PC", j) for j in range(4)], tag="cdm")
                            pr.emit("dve", lambda h, q=q, hp0=hp0: h.tensor_mul(
                                cmy[:, :],
                                carry[:, hp0 + q * 512:hp0 + (q + 1) * 512],
                                PC[:, 0:512]),
                                reads=[R("PC", j) for j in range(4)] +
                                      [R("carry", t0 + q * 4 + j) for j in range(4)],
                                writes=[R("cmy")], tag="cmy")
                            pr.emit("dve", lambda h, q=q, hp0=hp0: h.tensor_add(
                                carry[:, hp0 + q * 512:hp0 + (q + 1) * 512],
                                cmy[:, :], PY[:, q * 512:(q + 1) * 512]),
                                reads=[R("cmy")] + [R("PY", q * 4 + j)
                                                    for j in range(4)],
                                writes=[R("carry", t0 + q * 4 + j) for j in range(4)],
                                tag="cup")
                        for q in range(4):
                            units.append(lambda q=q: uCarry(q))

                        def uAdd2():
                            pr.emit("dve", lambda h, ys=ys: h.scalar_tensor_tensor(
                                yrw[:, ys * 2048:(ys + 1) * 2048], dxh[:, :], 1.0,
                                yrw[:, ys * 2048:(ys + 1) * 2048],
                                op0=ALU.mult, op1=ALU.add),
                                reads=[R("dxh", t) for t in range(16)] + [R("yrw", ys)],
                                writes=[R("yrw", ys)], tag="add2")
                            zk = [kz(t0 + t) for t in range(16)]
                            pr.emit("dve", lambda h, t0=t0, c=c, ys=ys: h.tensor_mul(
                                ap3(gsl, t0 * MS + c * CH, MS, 16, 1, 128),
                                yrw[:, ys * 2048:(ys + 1) * 2048],
                                ap3(xbcz, t0 * MS + c * CH, MS, 16, 1, 128)),
                                reads=[R("yrw", ys)] + zk, writes=[R("gsl", m, c, H)],
                                tag="gate")
                            pr.emit("act", lambda h, t0=t0, c=c: h.activation(
                                dxh[:, :],
                                ap3(gsl, t0 * MS + c * CH, MS, 16, 1, 128), AF.Square),
                                reads=[R("gsl", m, c, H)],
                                writes=[R("gsq")] + [R("dxh", t) for t in range(16)],
                                tag="gsq")
                        units.append(uAdd2)

                        def uSsq():
                            for t in range(16):
                                pr.emit("pe", lambda h, t=t: h.matmul(
                                    PA[0:1, 384:512], onescb[:, :],
                                    dxh[:, t * 128:(t + 1) * 128],
                                    start=(t == 0), stop=(t == 15)),
                                    reads=[R("gsq"), R("dxh", t), R("c_onescb")],
                                    writes=[R("PAs")], tag="ssq")
                            if H == 0:
                                pr.emit("dve", lambda h: h.tensor_copy(
                                    srow[0:1, :], PA[0:1, 384:512]),
                                    reads=[R("PAs")], writes=[R("srow")], tag="ssa0")
                            else:
                                pr.emit("dve", lambda h: h.scalar_tensor_tensor(
                                    srow[0:1, :], PA[0:1, 384:512], 1.0, srow[0:1, :],
                                    op0=ALU.mult, op1=ALU.add),
                                    reads=[R("PAs"), R("srow")], writes=[R("srow")],
                                    tag="ssa1")
                        units.append(uSsq)

                        if H == 1:
                            def uRstd():
                                pr.emit("dve", lambda h: h.tensor_scalar(
                                    srow[0:1, :], srow[0:1, :], 1.0 / HID, EPS,
                                    op0=ALU.mult, op1=ALU.add),
                                    reads=[R("srow")], writes=[R("srow")], tag="rs1")
                                pr.emit("act", lambda h: h.activation(
                                    srow[0:1, :], srow[0:1, :], AF.Ln),
                                    reads=[R("srow")], writes=[R("srow")], tag="rs2")
                                pr.emit("act", lambda h: h.activation(
                                    srow[0:1, :], srow[0:1, :], AF.Exp, scale=-0.5),
                                    reads=[R("srow")], writes=[R("srow")], tag="rs3")
                                pr.emit("pe", lambda h: h.matmul(
                                    PA[:, 256:384], onesf[:, :], srow[0:1, :],
                                    start=True, stop=True),
                                    reads=[R("srow"), R("c_onesf")],
                                    writes=[R("PAr")], tag="rs4")
                                pr.emit("dve", lambda h: h.tensor_copy(
                                    rsb[:, :], PA[:, 256:384]),
                                    reads=[R("PAr")], writes=[R("rsb")], tag="rs5")
                            units.append(uRstd)

                            def uGfin():
                                for Hx in range(2):
                                    tx = Hx * 16
                                    pr.emit("dve", lambda h, tx=tx, c=c: h.tensor_mul(
                                        ap3(gsl, tx * MS + c * CH, MS, 16, 1, 128),
                                        ap3(gsl, tx * MS + c * CH, MS, 16, 1, 128),
                                        bass.AP(rsb, 0, [[128, 128], [0, 16], [1, 128]])),
                                        reads=[R("gsl", m, c, Hx), R("rsb")],
                                        writes=[R("gsl", m, c, Hx)], tag="gfin")
                            units.append(uGfin)
                        return units

                    prevB = []
                    for s in range(9):
                        nb = len(prevB)
                        head = [u for u in prevB[:1 + 4 + 4 + 4 + 4]]
                        tail = prevB[1 + 4 + 4 + 4 + 4:]
                        for u in head:
                            u()
                        if s < 8:
                            if s % 2 == 0:
                                em_prep(s // 2)
                            units, yev = mt_units(s)
                            ti = 0
                            for i, u in enumerate(units):
                                u()
                                if ti < len(tail):
                                    tail[ti]()
                                    ti += 1
                            while ti < len(tail):
                                tail[ti]()
                                ti += 1
                            yev()
                            prevB = b_units(s)
                        else:
                            for u in tail:
                                u()
                    if m == NMS - 1:
                        do_dump("carry", lambda: carry[:, :],
                                [R("carry", t) for t in range(32)])
                    do_dump("gsl", lambda: gsl[:, :], gkeys)
                    pr.barrier()
                sx.close()
                if mstage < 3:
                    return

                with contextlib.ExitStack() as s3:
                    wmv = sb(f"wmv_{m}", [128, 2 * 8192], BF16, "left", s3)
                    vsl = sb(f"vsl_{m}", [128, NMS * 2048], F16, "left", s3)
                    xst = sb(f"xs2_{m}", [128, 2048], F32, "left", s3)
                    for fo in range(8):
                        slot = fo % 2
                        wks = [R("wmv", slot, kt) for kt in range(32)]
                        pr.dma("pool", f"wmv{slot}",
                               lambda h, fo=fo, slot=slot: h.dma_start(
                                   wmv[:, slot * 8192:(slot + 1) * 8192],
                                   wout_h[:, fo * 8192:(fo + 1) * 8192]),
                               writes=wks, tag="wol")
                        for kt in range(32):
                            pr.emit("dve", lambda h, slot=slot, kt=kt: h.tensor_scalar(
                                wmv[:, slot * 8192 + kt * 256:slot * 8192 + (kt + 1) * 256],
                                wmv[:, slot * 8192 + kt * 256:slot * 8192 + (kt + 1) * 256],
                                nwt[:, kt:kt + 1], None, op0=ALU.mult),
                                reads=[R("wmv", slot, kt), R("c_nw")],
                                writes=[R("wmv", slot, kt)], tag="wosc")
                        for stt in range(NMS):
                            for kt in range(32):
                                pr.emit("pe", lambda h, kt=kt, stt=stt, slot=slot:
                                        h.matmul(PY[:, stt * 512:stt * 512 + 256],
                                                 gsl[:, kt * MS + stt * 128:
                                                     kt * MS + stt * 128 + 128],
                                                 wmv[:, slot * 8192 + kt * 256:
                                                     slot * 8192 + (kt + 1) * 256],
                                                 start=(kt == 0), stop=(kt == 31)),
                                        reads=[R("wmv", slot, kt)] + gkeys,
                                        writes=[R("PY", stt * 4), R("PY", stt * 4 + 1)],
                                        tag="opm")
                            pr.emit("dve", lambda h, stt=stt, fo=fo: h.tensor_copy(
                                vsl[:, stt * 2048 + fo * 256:stt * 2048 + (fo + 1) * 256],
                                PY[:, stt * 512:stt * 512 + 256]),
                                reads=[R("PY", stt * 4), R("PY", stt * 4 + 1)],
                                writes=[R("vsl", m, stt, fo)], tag="opev")
                    pr.barrier()
                    for stt in range(NMS):
                        row0 = tok0 + stt * 128
                        vk = [R("vsl", m, stt, fo) for fo in range(8)]
                        pr.dma("sp", "xld",
                               lambda h, r=row0: h.dma_start(xst[:, :], x_h[r:r + 128, :]),
                               writes=[xkey], tag="xld2")
                        rsd, nmr, skeys = ln_stats(
                            lambda stt=stt: vsl[:, stt * 2048:(stt + 1) * 2048], vk,
                            lambda: lntmp[:, :], R("lntmp"))
                        pr.emit("dve", lambda h, r=rsd, b=nmr, stt=stt: h.tensor_scalar(
                            lntmp[:, :], vsl[:, stt * 2048:(stt + 1) * 2048], r, b,
                            op0=ALU.mult, op1=ALU.add),
                            reads=vk + skeys, writes=[R("lntmp")], tag="ln2n")
                        gi = (m * NMS + stt) % 8
                        pr.emit("dve", lambda h, gi=gi: h.tensor_add(
                            x2sl[:, gi * 2048:(gi + 1) * 2048], lntmp[:, :], xst[:, :]),
                            reads=[R("lntmp"), xkey], writes=[R("x2", gi)], tag="x2a")
                    do_dump("vsl", lambda: vsl[:, :],
                            [R("vsl", m, stt, fo) for stt in range(NMS) for fo in range(8)])
                    pr.barrier()

        def ffn_slab(f):
            tok0 = f * FS
            with contextlib.ExitStack() as fsk:
                fh = fsk.enter_context(contextlib.ExitStack())
                h1T = sb(f"h1T_{f}", [128, 32 * FS], BF16, "left", fh)
                h1k = [R("h1T", f, cc) for cc in range(32)]
                with contextlib.ExitStack() as f1:
                    ln3T = sb(f"ln3T_{f}", [128, 16 * FS], BF16, "left", f1)
                    wst = [sb(f"fw{i}_{f}", [128, 2048], BF16, "left", f1)
                           for i in range(2)]
                    lk = []
                    for stt in range(8):
                        x2k = [R("x2", stt)]
                        rsd, nmr, skeys = ln_stats(
                            lambda stt=stt: x2sl[:, stt * 2048:(stt + 1) * 2048], x2k,
                            lambda: lntmp[:, :], R("lntmp"))
                        pr.emit("dve", lambda h, r=rsd, b=nmr, stt=stt: h.tensor_scalar(
                            lntmp[:, :], x2sl[:, stt * 2048:(stt + 1) * 2048], r, b,
                            op0=ALU.mult, op1=ALU.add),
                            reads=x2k + skeys, writes=[R("lntmp")], tag="ln3n")
                        for g in range(4):
                            for j in range(4):
                                kt = g * 4 + j
                                pr.emit("pe", lambda h, kt=kt, j=j: h.transpose(
                                    PT[:, j * 128:(j + 1) * 128],
                                    lntmp[:, kt * 128:(kt + 1) * 128], idbf[:, :]),
                                    reads=[R("lntmp"), R("c_idbf")],
                                    writes=[R("PT", j)], tag="ln3t")
                            pr.emit("dve", lambda h, g=g, stt=stt: h.tensor_copy(
                                ap3(ln3T, g * 4 * FS + stt * 128, FS, 4, 1, 128),
                                PT[:, 0:512]),
                                reads=[R("PT", j) for j in range(4)],
                                writes=[R("ln3T", f, g, stt)], tag="ln3e")
                        lk += [R("ln3T", f, g, stt) for g in range(4)]
                    pr.barrier()
                    for cc in range(32):
                        slot = cc % 2
                        pr.dma("pool", f"fw{slot}",
                               lambda h, cc=cc, slot=slot: h.dma_start(
                                   wst[slot][:, :], w1_h[:, cc * 2048:(cc + 1) * 2048]),
                               writes=[R("fw", slot)], tag="w1l")
                        for ff in range(2):
                            pn, off = PSL[(cc * 2 + ff) % 7]
                            pt = psum_of(pn)
                            pk = pskeys(pn, off)
                            for kt in range(16):
                                pr.emit("pe", lambda h, pt=pt, off=off, kt=kt, ff=ff, sl=slot:
                                        h.matmul(pt[:, off:off + 512],
                                                 wst[sl][:, kt * 128:(kt + 1) * 128],
                                                 ln3T[:, kt * FS + ff * 512:
                                                      kt * FS + ff * 512 + 512],
                                                 start=(kt == 0), stop=(kt == 15)),
                                        reads=[R("fw", slot)] + lk, writes=pk, tag="h1m")
                            pr.emit("act", lambda h, pt=pt, off=off, cc=cc, ff=ff:
                                    h.activation(
                                        h1T[:, cc * FS + ff * 512:cc * FS + ff * 512 + 512],
                                        pt[:, off:off + 512], AF.Relu,
                                        bias=b1t[:, cc:cc + 1]),
                                    reads=pk + [R("c_b1")], writes=[R("h1T", f, cc)],
                                    tag="h1e")
                    do_dump("h1T", lambda: h1T[:, :], h1k)
                    pr.barrier()
                h2T = sb(f"h2T_{f}", [128, 32 * FS], BF16, side="right", stk=fsk)
                h2k = [R("h2T", f, cc) for cc in range(32)]
                with contextlib.ExitStack() as f2:
                    wst2 = [sb(f"f2w{i}_{f}", [128, 4096], BF16, "left", f2)
                            for i in range(2)]
                    for cc in range(32):
                        slot = cc % 2
                        pr.dma("pool", f"f2w{slot}",
                               lambda h, cc=cc, slot=slot: h.dma_start(
                                   wst2[slot][:, :], w2_h[:, cc * 4096:(cc + 1) * 4096]),
                               writes=[R("f2w", slot)], tag="w2l")
                        for ff in range(2):
                            pn, off = PSL[(cc * 2 + ff) % 7]
                            pt = psum_of(pn)
                            pk = pskeys(pn, off)
                            for kt in range(32):
                                pr.emit("pe", lambda h, pt=pt, off=off, kt=kt, ff=ff, sl=slot:
                                        h.matmul(pt[:, off:off + 512],
                                                 wst2[sl][:, kt * 128:(kt + 1) * 128],
                                                 h1T[:, kt * FS + ff * 512:
                                                      kt * FS + ff * 512 + 512],
                                                 start=(kt == 0), stop=(kt == 31)),
                                        reads=[R("f2w", slot)] + h1k, writes=pk, tag="h2m")
                            pr.emit("act", lambda h, pt=pt, off=off, cc=cc, ff=ff:
                                    h.activation(
                                        h2T[:, cc * FS + ff * 512:cc * FS + ff * 512 + 512],
                                        pt[:, off:off + 512], AF.Relu,
                                        bias=b2t[:, cc:cc + 1]),
                                    reads=pk + [R("c_b2")], writes=[R("h2T", f, cc)],
                                    tag="h2e")
                    pr.barrier()
                fh.close()
                with contextlib.ExitStack() as f3:
                    wmv = sb(f"w3m_{f}", [128, 2 * 8192], BF16, "left", f3)
                    fsl = sb(f"fsl_{f}", [128, 8 * 2048], BF16, "left", f3)
                    b3m = sb(f"b3m_{f}", [128, 2048], BF16, "left", f3)
                    b3row = sb(f"b3row_{f}", [1, DM], F32, "left", f3)
                    ost = sb(f"ost_{f}", [128, 2048], F32, "left", f3)
                    pr.dma("sp", "c_b3", lambda h: h.dma_start(b3row[:, :], b3_h),
                           writes=[R("c_b3")], tag="b3l")
                    for q in range(4):
                        pr.emit("pe", lambda h, q=q: h.matmul(
                            PA[:, 0:512], onesf[:, :],
                            b3row[0:1, q * 512:(q + 1) * 512], start=True, stop=True),
                            reads=[R("c_b3"), R("c_onesf")], writes=[R("PAF")], tag="b3b")
                        pr.emit("dve", lambda h, q=q: h.tensor_copy(
                            b3m[:, q * 512:(q + 1) * 512], PA[:, 0:512]),
                            reads=[R("PAF")], writes=[R("b3m", q)], tag="b3e")
                    for fo in range(8):
                        slot = fo % 2
                        pr.dma("pool", f"w3m{slot}",
                               lambda h, fo=fo, slot=slot: h.dma_start(
                                   wmv[:, slot * 8192:(slot + 1) * 8192],
                                   w3_h[:, fo * 8192:(fo + 1) * 8192]),
                               writes=[R("w3m", slot)], tag="w3l")
                        for stt in range(8):
                            pn, off = [("PY", 0), ("PY", 512), ("PY", 1024),
                                       ("PY", 1536), ("PB", 0), ("PC", 0)][stt % 6]
                            pt = psum_of(pn)
                            pk = pskeys(pn, off, 256)
                            for kt in range(32):
                                pr.emit("pe", lambda h, pt=pt, off=off, kt=kt, stt=stt,
                                        sl=slot: h.matmul(
                                            pt[:, off:off + 256],
                                            h2T[:, kt * FS + stt * 128:
                                                 kt * FS + stt * 128 + 128],
                                            wmv[:, sl * 8192 + kt * 256:
                                                sl * 8192 + (kt + 1) * 256],
                                            start=(kt == 0), stop=(kt == 31)),
                                        reads=[R("w3m", slot)] + h2k, writes=pk, tag="w3m")
                            pr.emit("dve", lambda h, pt=pt, off=off, stt=stt, fo=fo:
                                    h.tensor_add(
                                        fsl[:, stt * 2048 + fo * 256:
                                            stt * 2048 + (fo + 1) * 256],
                                        pt[:, off:off + 256],
                                        b3m[:, fo * 256:(fo + 1) * 256]),
                                    reads=pk + [R("b3m", fo // 2)],
                                    writes=[R("fsl", f, stt, fo)], tag="w3e")
                    do_dump("fsl", lambda: fsl[:, :],
                            [R("fsl", f, stt, fo) for stt in range(8) for fo in range(8)])
                    pr.barrier()
                    for stt in range(8):
                        fk = [R("fsl", f, stt, fo) for fo in range(8)]
                        rsd, nmr, skeys = ln_stats(
                            lambda stt=stt: fsl[:, stt * 2048:(stt + 1) * 2048], fk,
                            lambda: lntmp[:, :], R("lntmp"))
                        pr.emit("dve", lambda h, r=rsd, b=nmr, stt=stt: h.tensor_scalar(
                            ost[:, :], fsl[:, stt * 2048:(stt + 1) * 2048], r, b,
                            op0=ALU.mult, op1=ALU.add),
                            reads=fk + skeys, writes=[R("ost")], tag="ln4n")
                        pr.emit("dve", lambda h, stt=stt: h.scalar_tensor_tensor(
                            ost[:, :], x2sl[:, stt * 2048:(stt + 1) * 2048], 1.0,
                            ost[:, :], op0=ALU.mult, op1=ALU.add),
                            reads=[R("x2", stt), R("ost")], writes=[R("ost")], tag="resf")
                        row0 = tok0 + stt * 128
                        pr.dma("sp", "outst",
                               lambda h, r=row0: h.dma_start(out_h[r:r + 128, :],
                                                             ost[:, :]),
                               reads=[R("ost")], tag="outd")
                    for e_ in ENGINES:
                        pr.emit(e_, lambda h: h.nop(), writes=[R("ost")], tag="odrain")
                    pr.barrier()

        if stage <= 3:
            mixer_slab(0, stage)
        else:
            for f in range(NFS):
                for sub in range(2):
                    mixer_slab(f * 2 + sub, 9)
                ffn_slab(f)
                do_dump("x2", lambda: x2sl[:, :], [R("x2", j) for j in range(8)])
        dsem = pr.finalize(st)
        if dsem:
            with nc.Block() as blk:
                def _fin(h):
                    for s_, c_ in dsem.values():
                        h.wait_ge(s_, c_)
                blk.sync(_fin)
    return nc, dump_t


def _rsel():
    r = np.zeros((64, 32 * 128), np.float32)
    for j in range(32):
        r[j, j * 128:(j + 1) * 128] = 1.0
    return r.astype(BF)


def _host_prep(inputs):
    ip = np.ascontiguousarray
    W = inputs["L0_attn_in_proj_w"]
    main = W[:8448].reshape(66, 128, 16, 128).transpose(3, 0, 2, 1).reshape(128, -1)
    dtw = W[8448:].reshape(64, 16, 128).transpose(2, 1, 0).reshape(128, -1)
    per = {
        "wlay": ip(np.concatenate([main, dtw], axis=1)),
        "wout": ip(inputs["L0_attn_out_proj_w"].reshape(8, 256, 32, 128)
                   .transpose(3, 0, 2, 1).reshape(128, -1)),
        "w1": ip(inputs["L0_ffn_W1"].reshape(32, 128, 16, 128)
                 .transpose(3, 0, 2, 1).reshape(128, -1)),
        "w2": ip(inputs["L0_ffn_W2"].reshape(32, 128, 32, 128)
                 .transpose(3, 0, 2, 1).reshape(128, -1)),
        "w3": ip(inputs["L0_ffn_W3"].reshape(8, 256, 32, 128)
                 .transpose(3, 0, 2, 1).reshape(128, -1)),
        "cwt": ip(inputs["L0_attn_conv_w"].reshape(34, 128, 4)
                  .transpose(1, 0, 2).reshape(128, -1)),
        "b1t": ip(inputs["L0_ffn_b1"].reshape(32, 128).T),
        "b2t": ip(inputs["L0_ffn_b2"].reshape(32, 128).T),
        "b3r": ip(inputs["L0_ffn_b3"].reshape(1, -1)),
        "nwt": ip(inputs["L0_attn_norm_w"].reshape(32, 128).T),
        "dtb": ip(inputs["L0_attn_dt_bias"].reshape(64, 1)),
        "alog": ip(inputs["L0_attn_A_log"].reshape(64, 1)),
        "idbf": np.eye(128, dtype=BF),
        "idf": np.eye(128, dtype=np.float32),
        "triu": np.triu(np.ones((128, 128), np.float32)).astype(BF),
        "rep64": np.kron(np.eye(64, dtype=np.float32),
                         np.ones((1, 64), np.float32)).astype(BF),
        "rsel": _rsel(),
        "onesf": np.ones((1, 128), np.float32),
        "onescb": np.ones((128, 1), BF),
        "ones64": np.ones((64, 128), BF),
    }
    D = inputs["L0_attn_D"]
    dc = np.empty((128, 32), np.float32)
    dc[:64, :] = D[0::2][None, :]
    dc[64:, :] = D[1::2][None, :]
    per["dcol"] = dc
    return per


def kernel(**inputs):
    inputs = {k: np.asarray(v) for k, v in inputs.items()}
    per = _host_prep(inputs)
    nc, _ = build(bass.Bass("TRN2", target_bir_lowering=False))
    x = inputs["L0_attn_x"]
    in_maps = []
    for b in range(8):
        d = dict(per)
        d["x"] = np.ascontiguousarray(x[b])
        in_maps.append(d)
    res = run_bass_kernel_spmd(nc, in_maps, core_ids=list(range(8)))
    return np.stack([r["out"] for r in res.results]).astype(np.float32)
